# version 15
# baseline (speedup 1.0000x reference)
import contextlib
import numpy as np
import concourse.bass as bass
import concourse.mybir as mybir
from concourse.bass_utils import run_bass_kernel_spmd

F32 = mybir.dt.float32
BF16 = mybir.dt.bfloat16
AF = mybir.ActivationFunctionType
ALU = mybir.AluOpType
AX = mybir.AxisListType


class _FakeInstr:
    __slots__ = ()

    def then_inc(self, sem, k=1):
        return self


class FakeEng:
    def __getattr__(self, name):
        def _f(*a, **k):
            return _FakeInstr()
        return _f


class Plan:
    def __init__(self):
        self.counts = {}
        self.marks = {}
        self.real = False
        self.sems = {}

    def inc(self, instr, semname, key=None, k=1):
        c = self.counts.get(semname, 0) + k
        self.counts[semname] = c
        if key is not None:
            if not self.real:
                assert key not in self.marks, f"dup milestone {key}"
                self.marks[key] = (semname, c)
            else:
                assert self.marks[key] == (semname, c), \
                    f"pass mismatch {key}: {self.marks[key]} vs {(semname, c)}"
        if self.real:
            instr.then_inc(self.sems[semname], k)
        return c

    def wait(self, eng, key):
        if self.real:
            semname, c = self.marks[key]
            eng.wait_ge(self.sems[semname], c)

    def wait_cnt(self, eng, semname, c):
        if self.real:
            eng.wait_ge(self.sems[semname], c)

    def total(self, semname):
        return self.counts.get(semname, 0)


SEMS = ["dx", "st", "dw0", "dw1", "dwa", "dwb", "dwc", "cg",
        "ct", "cs", "cv", "by", "dws", "dwy", "sz0", "sz1", "ds0", "ds1",
        "dg0", "dg1", "dy0", "dy1", "dy2", "dy3", "dze0", "dze1"]
ENGS = ["sync", "gpsimd", "tensor", "scalar", "vector"]
OWN = {"sync": "by", "gpsimd": "cg", "tensor": "ct", "scalar": "cs",
       "vector": "cv"}


def make_cfg(B=2, T=2048, C=1024, NL=12, H=16, E=8, FE=4096, TT=512, FV=512,
             n_cores=8):
    DH = 64
    assert C == H * DH and C % 128 == 0 and H % 2 == 0
    assert T % TT == 0 and TT % 128 == 0 and FE % FV == 0 and FV % 128 == 0
    assert (2 * C) % 512 == 0
    cfg = dict(B=B, T=T, C=C, NL=NL, H=H, DH=DH, E=E, FE=FE, TT=TT, FV=FV,
               n_cores=n_cores,
               NCC=C // 128, NTT=T // TT, NSUB=TT // 128, NCFE=FE // 128,
               NVQ=FE // FV, NFV=FV // 128, NKVG=(2 * C) // 512)
    cfg["NVE"] = E * cfg["NVQ"]
    cfg["sparse"] = True
    cfg["CAP"] = 1024
    assert cfg["CAP"] % TT == 0
    cfg["NST"] = cfg["CAP"] // TT
    ppe, nvq = (cfg["CAP"] // TT) * cfg["NSUB"], cfg["NVQ"]
    assert ppe % nvq == 0 and ppe // nvq <= 2
    cfg["NI"] = E * cfg["CAP"]
    cfg["NCHK"] = T // 128
    assert cfg["NCHK"] * E <= 128 and (cfg["NI"] * 2) % 128 == 0
    return cfg


def build_nc(cfg, es):
    B, T, C, NL, H, DH, E = (cfg[k] for k in
                             ("B", "T", "C", "NL", "H", "DH", "E"))
    FE, TT, FV = cfg["FE"], cfg["TT"], cfg["FV"]
    NCC, NTT, NSUB, NCFE = cfg["NCC"], cfg["NTT"], cfg["NSUB"], cfg["NCFE"]
    NVQ, NVE, NFV, NKVG = cfg["NVQ"], cfg["NVE"], cfg["NFV"], cfg["NKVG"]
    SP = min(8, H)
    SPARSE = cfg.get("sparse", False)
    CAP, NST, NI, NCHK = cfg["CAP"], cfg["NST"], cfg["NI"], cfg["NCHK"]
    ZROW = T
    BIGP = 1.0e5
    EPS_LN, EPS_GATE = 1e-5, 1e-9

    nc = bass.Bass("TRN2", target_bir_lowering=False, debug=False)
    dt = nc.dram_tensor
    x_in = dt("x_in", [B, NTT, 128, NCC * TT], F32, kind="ExternalInput").ap()
    wqh_d = dt("wqh", [NL, 128, NCC * C], F32, kind="ExternalInput").ap()
    wkvh_d = dt("wkvh", [NL, 128, NCC * 2 * C], F32,
                kind="ExternalInput").ap()
    bqqh = dt("bqqh", [NL, 128, NCC], F32, kind="ExternalInput").ap()
    bkvh = dt("bkvh", [NL, 1, 2 * C], F32, kind="ExternalInput").ap()
    wprojh = dt("wprojh", [NL, 128, NCC * C], F32, kind="ExternalInput").ap()
    bpjh = dt("bpjh", [NL, 128, NCC], F32, kind="ExternalInput").ap()
    wgh = dt("wgh", [128, NCC * E], F32, kind="ExternalInput").ap()
    w1h = dt("w1h", [E, NVQ, 128, NCC * FV], F32, kind="ExternalInput").ap()
    b1h = dt("b1h", [E, 128, NCFE], F32, kind="ExternalInput").ap()
    w2h = dt("w2h", [E, NVQ, 128, NFV * C], F32, kind="ExternalInput").ap()
    b2h = dt("b2h", [E, C], F32, kind="ExternalInput").ap()
    eyeh = dt("eyeh", [128, 128], F32, kind="ExternalInput").ap()
    ohbh = dt("ohbh", [E, E * 128], F32, kind="ExternalInput").ap()
    out_d = dt("out", [B, 128, NCC * T], F32, kind="ExternalOutput").ap()
    x1s = dt("x1s", [B, 128, NCC * T], F32, kind="Internal").ap()
    if SPARSE:
        ltc_h = dt("ltc_h", [128, 128], F32, kind="ExternalInput").ap()
        selm_h = dt("selm_h", [128, 128], F32, kind="ExternalInput").ap()
        segc_h = dt("segc_h", [128, 1], F32, kind="ExternalInput").ap()
        tidh_h = dt("tidh_h", [128, NCHK], mybir.dt.int32,
                    kind="ExternalInput").ap()
        pfh_h = dt("pfh_h", [128, NI * 2 // 128], mybir.dt.int32,
                   kind="ExternalInput").ap()
        zt_s = dt("zt_s", [T + 1, C], BF16, kind="Internal").ap()
        yinst_s = dt("yinst_s", [NI, C], F32, kind="Internal").ap()
        idxpw_s = dt("idxpw_s", [NI + 128, 2], mybir.dt.int32,
                     kind="Internal").ap()
        zc_s = dt("zc_s", [E, NST, 128, NSUB * NCC * 128], BF16,
                  kind="Internal").ap()
    dbg = (dt("dbg", [B, 128, NCC * T], F32, kind="ExternalOutput").ap()
           if cfg.get("debug") else None)

    yx = es.enter_context(nc.sbuf_tensor("yx", [128, NCC * T], F32))
    eyes = es.enter_context(nc.sbuf_tensor("eyes", [128, 128], F32))
    eyeb = es.enter_context(nc.sbuf_tensor("eyeb", [128, 128], BF16))
    ohbs = es.enter_context(nc.sbuf_tensor("ohbs", [E, E * 128], BF16))
    b2s = es.enter_context(nc.sbuf_tensor("b2s", [E, C], BF16))
    wgs = es.enter_context(nc.sbuf_tensor("wgs", [128, NCC * E], BF16))
    onec = es.enter_context(nc.sbuf_tensor("onec", [128, 1], BF16))
    oner = es.enter_context(nc.sbuf_tensor("oner", [1, 128], BF16))
    ssb = es.enter_context(nc.sbuf_tensor("ssb", [64, H * DH], F32))
    sbf2 = es.enter_context(nc.sbuf_tensor("sbf2", [128, NCC * 128], BF16))
    pwtb = es.enter_context(nc.sbuf_tensor("pwtb", [E, T], BF16))
    bkvs = es.enter_context(nc.sbuf_tensor("bkvs", [1, 2 * 2 * C], BF16))
    bqs = es.enter_context(nc.sbuf_tensor("bqs", [128, 2 * NCC], F32))
    bps = es.enter_context(nc.sbuf_tensor("bps", [128, 2 * NCC], F32))
    b1s = es.enter_context(nc.sbuf_tensor("b1s", [128, 2 * NCFE], F32))
    pfull = es.enter_context(nc.sbuf_tensor("pfull", [128, E], F32))
    ppm = es.enter_context(nc.sbuf_tensor("ppm", [128, 1], F32))
    ppex = es.enter_context(nc.sbuf_tensor("ppex", [128, E], F32))
    ppa = es.enter_context(nc.sbuf_tensor("ppa", [128, E], F32))
    ppb = es.enter_context(nc.sbuf_tensor("ppb", [128, E], F32))
    ppr = es.enter_context(nc.sbuf_tensor("ppr", [128, 1], F32))
    I32 = mybir.dt.int32
    if SPARSE:
        ltc = es.enter_context(nc.sbuf_tensor("ltc", [128, 128], BF16))
        selm = es.enter_context(nc.sbuf_tensor("selm", [128, 128], BF16))
        segc = es.enter_context(nc.sbuf_tensor("segc", [128, 1], F32))
        tidh = es.enter_context(nc.sbuf_tensor("tidh", [128, NCHK], I32))
        pfh = es.enter_context(nc.sbuf_tensor("pfh", [128, NI * 2 // 128],
                                              I32))
        c64r = es.enter_context(nc.sbuf_tensor("c64r", [1, 128], BF16))
        zrow0 = es.enter_context(nc.sbuf_tensor("zrow0", [1, C], BF16))
        maskb = es.enter_context(nc.sbuf_tensor("maskb", [128, NCHK * E],
                                                BF16))
        maskf = es.enter_context(nc.sbuf_tensor("maskf", [128, NCHK * E],
                                                F32))
        eqf = es.enter_context(nc.sbuf_tensor("eqf", [128, NCHK * E], F32))
        pwm = es.enter_context(nc.sbuf_tensor("pwm", [128, NCHK * E], F32))
        crow = es.enter_context(nc.sbuf_tensor("crow", [1, 128], F32))
        ccb = es.enter_context(nc.sbuf_tensor("ccb", [128, 1], BF16))
        cbt = es.enter_context(nc.sbuf_tensor("cbt", [128, 3], F32))
        qsr = es.enter_context(nc.sbuf_tensor("qsr", [1, 2 * 128], BF16))
        posb = es.enter_context(nc.sbuf_tensor("posb", [128, 2 * NCHK], I32))
        sctg = es.enter_context(nc.sbuf_tensor("sctg", [128, 2 * 2], I32))
        pp1 = es.enter_context(nc.sbuf_tensor("pp1", [128, E], F32))
        pp2 = es.enter_context(nc.sbuf_tensor("pp2", [128, E], F32))
        pp3 = es.enter_context(nc.sbuf_tensor("pp3", [128, 1], F32))
        idxt = es.enter_context(nc.sbuf_tensor("idxt", [128, 2 * 2], I32))
        pwsb = es.enter_context(nc.sbuf_tensor("pwsb", [128, E * NST * NSUB],
                                               F32))
        dmpc = es.enter_context(nc.sbuf_tensor("dmpc", [128, 1], F32))
        ppA = es.enter_context(nc.sbuf_tensor("ppA", [128, NCHK * E], F32))
        ppB = es.enter_context(nc.sbuf_tensor("ppB", [128, NCHK * E], F32))
        pw12 = es.enter_context(nc.sbuf_tensor("pw12", [128, 2 * NCHK], F32))
        sctgA = es.enter_context(nc.sbuf_tensor("sctgA", [128, 4 * NCHK],
                                                I32))

    W1C, W2C = NCC * FV, NFV * C
    ZHC, HSC = NCC * T, NFV * TT
    XSC = 2 * NCC * TT
    KVC, QTC, YTC = 2 * 2 * C, NCC * TT, NCC * TT
    ROWC, RBFC = 2 * 4 * TT, 2 * TT
    WKVC, WQC, WPC, XBC = NCC * 2 * C, NCC * C, NCC * C, NCC * TT
    YAC = 2 * NCC * CAP if SPARSE else 0
    ffn_cols = ZHC + 2 * (W1C + W2C) + YAC + HSC + 2 * TT + 2 * 2 * TT
    attn_cols = WKVC + WQC + WPC + 2 * XBC + KVC + QTC + YTC + 2 * 2 * TT
    ffn_ovl = XSC + ROWC + RBFC + XBC + XBC
    if SPARSE:
        ffn_ovl += 2 * C + 2 * (2 * C)
    attn_ovl = ROWC + RBFC
    ACOLS = max(ffn_cols, attn_cols)
    ffn_ovl_fits = ffn_ovl <= 2 * (W1C + W2C) + YAC
    attn_ovl_fits = attn_ovl <= WKVC
    extra = (0 if ffn_ovl_fits else ffn_ovl) + (0 if attn_ovl_fits else
                                                attn_ovl)
    ACOLS += extra
    arena = es.enter_context(nc.sbuf_tensor("arena", [128, ACOLS], BF16))

    def carve(off, cols):
        assert off + cols <= ACOLS
        return arena[:, off:off + cols], off + cols

    from concourse.ap import AP as _AP

    def sub_ap(slice_ap, extra_off, dims):
        return _AP(tensor=slice_ap.tensor,
                   offset=slice_ap.offset + extra_off,
                   ap=[[int(slice_ap.ap[0][0]), int(slice_ap.ap[0][1])]] +
                      [[int(s), int(n)] for s, n in dims])

    o = 0
    zh, o = carve(o, ZHC)
    wdb_off = o
    w1b, w2b = [], []
    for i in range(2):
        a, o = carve(o, W1C); w1b.append(a)
        a, o = carve(o, W2C); w2b.append(a)
    if SPARSE:
        a, o = carve(o, YAC); yacc_f = a.bitcast(F32)
    hsb, o = carve(o, HSC)
    a, o = carve(o, 2 * TT); tmpv_f = a.bitcast(F32)
    a, o = carve(o, 4 * TT); pbs_f = a.bitcast(F32)
    oo = wdb_off if ffn_ovl_fits else max(ffn_cols, attn_cols)
    a, oo = carve(oo, XSC); xstr_f = a.bitcast(F32)
    a, oo = carve(oo, ROWC); rows_ff = a[0:1, :].bitcast(F32)
    rbf_ff, oo = carve(oo, RBFC); rbf_ff = rbf_ff[0:1, :]
    xbb_ff, oo = carve(oo, XBC)
    xqq_ff, oo = carve(oo, XBC)

    if SPARSE:
        NSTl = cfg["NST"]
        zneed = NSTl * NCC * TT + 2 * (2 * C) + 2 * C
        if zneed <= ZHC:
            zo = 0
        else:
            zo = ACOLS
            arena2 = es.enter_context(
                nc.sbuf_tensor("arena2pad", [128, zneed + 2 * (2 * C)], BF16))
            def carve2(off, cols, _base=zo):
                return arena2[:, off - _base:off - _base + cols], off + cols
        _carve = carve if zneed <= ZHC else carve2
        zeS = []
        for i in range(NSTl):
            a, zo = _carve(zo, NCC * TT); zeS.append(a)
        ytr = []
        for i in range(2):
            a, zo = _carve(zo, 2 * C); ytr.append(a.bitcast(F32))
        zgk = []
        for i in range(2):
            a, zo = _carve(zo, C); zgk.append(a)
        if zneed <= ZHC and ZHC - zneed >= 2 * C and C <= 2 * TT:
            a, zo = _carve(zo, 2 * C); ytr.append(a.bitcast(F32))
            ytr.append(pbs_f[:, 0:C])
        else:
            a, zo = _carve(zo, 2 * C); ytr.append(a.bitcast(F32))
            a, zo = _carve(zo, 2 * C); ytr.append(a.bitcast(F32))
        ztma = []
        for ii in range(2):
            a2, oo = carve(oo, C); ztma.append(a2)
        ybk = []
        for i in range(2):
            a, oo = carve(oo, 2 * C); ybk.append(a.bitcast(F32))
        assert oo <= wdb_off + 2 * (W1C + W2C) + YAC or not ffn_ovl_fits

    o = 0
    wkv, o = carve(o, WKVC)
    wqv, o = carve(o, WQC)
    wpv, o = carve(o, WPC)
    xbA = []
    for i in range(2):
        a, o = carve(o, XBC); xbA.append(a)
    kvt, o = carve(o, KVC)
    qTt, o = carve(o, QTC)
    yTt, o = carve(o, YTC)
    a, o = carve(o, 2 * 2 * TT); tmpa_f = a.bitcast(F32)
    oo = 0 if attn_ovl_fits else \
        (max(ffn_cols, attn_cols) + (0 if ffn_ovl_fits else ffn_ovl))
    a, oo = carve(oo, ROWC); rows_at = a[0:1, :].bitcast(F32)
    rbf_at, oo = carve(oo, RBFC); rbf_at = rbf_at[0:1, :]
    xa_f = xbA[0].bitcast(F32)
    assert xa_f.shape[1] >= T // 2 or True

    ps = [es.enter_context(nc.psum_tensor(f"ps{i}", [128, 512], F32))
          for i in range(8)]
    HB = [6, 7, 2]
    HBS = [6, 7, 3]
    YB = [0, 1, 3]
    YBS = [0, 1]
    KB = [0, 1, 4, 5]
    QB = [4, 5, 0]
    TB = [6, 7, 1]
    PB = [4, 5, 0]

    R_NEGMU, R_SCR, R_RSTD, R_BROW = 0, 1, 2, 3

    def row(view, i, n=1):
        return view[0:1, i * TT:(i + n) * TT]

    NST_ = cfg.get("NST", 1)

    def gidx(k):
        e2, r = divmod(k, NST_ * NSUB)
        t2, s2 = divmod(r, NSUB)
        return (e2, t2, s2)

    def psl(base, qi):
        PPE_ = NST_ * NSUB
        return range(base + qi * PPE_ // NVQ, base + (qi + 1) * PPE_ // NVQ)

    phases = [("setup",)]
    for b in range(B):
        phases += [("ffn", b, 0), ("attn", b), ("ffn", b, 1)]
    if cfg.get("only_ffn1"):
        phases = [("setup",)] + [("ffn", b, 0) for b in range(B)]

    def ln_tile(ename, e, M, p, tag, t, xsrc, zdst, rowsv, rbfv, castb, sqb,
                streamed, tmpb, z_gate=None, s1_gate=None, bc_gate=None):
        K = lambda name: (p, tag + name, t)
        if ename == "tensor":
            M.wait(e, K("xcast"))
            if s1_gate is not None:
                M.wait(e, s1_gate)
            if t >= 1:
                M.wait(e, (p, tag + "statsread", t - 1))
            for cc in range(NCC):
                mm = e.matmul(ps[0][0:1, 0:TT], onec[:],
                              castb[:, cc * TT:(cc + 1) * TT],
                              start=(cc == 0), stop=(cc == NCC - 1))
            M.inc(mm, "ct", K("s1"))
            M.wait(e, K("xsq"))
            for cc in range(NCC):
                mm = e.matmul(ps[1][0:1, 0:TT], onec[:],
                              sqb[:, cc * TT:(cc + 1) * TT],
                              start=(cc == 0), stop=(cc == NCC - 1))
            M.inc(mm, "ct", K("s2"))
            M.wait(e, K("rbf"))
            if t >= 1:
                M.wait(e, (p, tag + "ztile", t - 1))
            if bc_gate is not None:
                M.wait(e, bc_gate)
            e.matmul(ps[2][:, 0:TT], oner[:], rbfv[0:1, 0:TT],
                     start=True, stop=True)
            M.inc(e.matmul(ps[3][:, 0:TT], oner[:], rbfv[0:1, TT:2 * TT],
                           start=True, stop=True), "ct", K("bc"))
        elif ename == "scalar":
            if streamed:
                M.wait(e, K("xld"))
            if t >= 1:
                M.wait(e, (p, tag + "s1", t - 1))
            for cc in range(NCC):
                M.inc(e.copy(out=castb[:, cc * TT:(cc + 1) * TT],
                             in_=xsrc(cc)), "cs",
                      K("xcast") if cc == NCC - 1 else None)
            M.wait(e, K("var"))
            c = M.inc(e.activation(out=row(rowsv, R_RSTD),
                                   in_=row(rowsv, R_SCR), func=AF.Ln), "cs")
            M.wait_cnt(e, "cs", c)
            M.inc(e.activation(out=row(rowsv, R_RSTD), in_=row(rowsv, R_RSTD),
                               func=AF.Exp, scale=-0.5), "cs", K("rstd"))
            M.wait(e, K("brow"))
            M.inc(e.copy(out=rbfv[0:1, 0:2 * TT], in_=row(rowsv, R_RSTD, 2)),
                  "cs", K("rbf"))
        elif ename == "gpsimd":
            if streamed:
                M.wait(e, K("xld"))
            if t >= 1:
                M.wait(e, (p, tag + "s2", t - 1))
            for cc in range(NCC):
                M.inc(e.tensor_tensor(out=sqb[:, cc * TT:(cc + 1) * TT],
                                      in0=xsrc(cc), in1=xsrc(cc),
                                      op=ALU.mult), "cg",
                      K("xsq") if cc == NCC - 1 else None)
        elif ename == "vector":
            def ch(op, key=None):
                c = M.inc(op, "cv", key)
                M.wait_cnt(e, "cv", c)
            M.wait(e, K("s1"))
            M.wait(e, K("s2"))
            ch(e.tensor_scalar(out=row(rowsv, R_NEGMU), in0=ps[0][0:1, 0:TT],
                               scalar1=-1.0 / C, scalar2=None, op0=ALU.mult))
            ch(e.tensor_scalar(out=row(rowsv, R_SCR), in0=ps[1][0:1, 0:TT],
                               scalar1=1.0 / C, scalar2=EPS_LN,
                               op0=ALU.mult, op1=ALU.add),
               (p, tag + "statsread", t))
            ch(e.tensor_tensor(out=row(rowsv, R_RSTD),
                               in0=row(rowsv, R_NEGMU),
                               in1=row(rowsv, R_NEGMU), op=ALU.mult))
            ch(e.tensor_tensor(out=row(rowsv, R_SCR), in0=row(rowsv, R_SCR),
                               in1=row(rowsv, R_RSTD), op=ALU.subtract),
               K("var"))
            M.wait(e, K("rstd"))
            ch(e.tensor_tensor(out=row(rowsv, R_BROW),
                               in0=row(rowsv, R_NEGMU),
                               in1=row(rowsv, R_RSTD), op=ALU.mult),
               K("brow"))
            M.wait(e, K("bc"))
            if z_gate is not None:
                M.wait(e, z_gate)
            for cc in range(NCC):
                ch(e.tensor_tensor(out=tmpb, in0=xsrc(cc),
                                   in1=ps[2][:, 0:TT], op=ALU.mult),
                   K("xfree") if (cc == NCC - 1 and streamed) else None)
                ch(e.tensor_tensor(out=zdst(cc), in0=tmpb,
                                   in1=ps[3][:, 0:TT], op=ALU.add),
                   K("ztile") if cc == NCC - 1 else None)

    def ffn_phase(ename, e, M, p, b, call):
        from_yx = (call == 1)
        NG = NTT * NSUB

        def xsrc(t):
            def f(cc):
                if from_yx:
                    return yx[:, cc * T + t * TT:cc * T + (t + 1) * TT]
                return xstr_f[:, cc * TT:(cc + 1) * TT]
            return f

        def zdst(t):
            def f(cc):
                return zh[:, cc * T + t * TT:cc * T + (t + 1) * TT]
            return f

        def lng(t):
            kw = {}
            if SPARSE:
                if t >= 1:
                    kw["bc_gate"] = (p, "ztmev", t * NSUB - 1)
            return kw

        NTI = NST if SPARSE else NTT

        if ename == "sync":
            if from_yx and dbg is not None:
                M.inc(e.dma_start(out=dbg[b], in_=yx[:]), "st",
                      (p, "dbgst"), 16)
            for t in range(NTT):
                if not from_yx:
                    if t >= 1:
                        M.wait(e, (p, "Lxfree", t - 1))
                    M.inc(e.dma_start(out=xstr_f[:, :], in_=x_in[b, t]),
                          "dx", (p, "Lxld", t), 16)
                if SPARSE:
                    for s_ in range(NSUB):
                        zi = t * NSUB + s_
                        M.wait(e, (p, "ztmev", zi))
                        M.inc(e.dma_start(
                            out=zt_s[zi * 128:(zi + 1) * 128, :],
                            in_=ztma[zi % 2][:, :]),
                            f"sz{zi % 2}", (p, "ztst", zi), 16)
            if not from_yx:
                for t in range(NTT):
                    if t == 0:
                        M.wait(e, (p, "Lxfree", NTT - 1))
                    else:
                        M.wait(e, (p, "yi", (t - 1) * NCC + NCC - 1))
                    M.inc(e.dma_start(out=xstr_f[:, :], in_=x_in[b, t]),
                          "dx", (p, "xld2", t), 16)
            if SPARSE:
                PPE = NST * NSUB
                def emit_zbst(kblock):
                    for k in kblock:
                        ei2, t2, sub2 = gidx(k)
                        M.wait(e, (p, "zbev_v", k))
                        M.inc(e.dma_start(
                            out=zc_s[ei2, t2][:, sub2 * NCC * 128:
                                              (sub2 + 1) * NCC * 128],
                            in_=zgk[k % 2][:, 0:NCC * 128]),
                            f"sz{k % 2}", (p, "zbst", ei2, t2, sub2), 16)
                emit_zbst(range(0, 2 * PPE))
                for i in range(NVE):
                    ei, qi = i // NVQ, i % NVQ
                    if ei + 2 < E:
                        emit_zbst(psl((ei + 2) * PPE, qi))
            if call == 1:
                M.wait(e, (p, "ffn_done"))
                M.inc(e.dma_start(out=out_d[b], in_=yx[:]), "st",
                      (p, "outst"), 16)
            return

        if ename == "gpsimd":
            for t in range(NTT):
                ln_tile("gpsimd", e, M, p, "L", t, xsrc(t), zdst(t),
                        rows_ff, rbf_ff, xbb_ff, xqq_ff, not from_yx,
                        tmpv_f[:, 0:TT], **lng(t))
            if SPARSE:
                M.inc(e.dma_start(out=idxpw_s[0:NI, :].rearrange(
                    "a b -> (a b)").rearrange("(x y) -> x y", x=128),
                    in_=pfh[:]), "dws", (p, "pfil"), 16)
                M.wait(e, (p, "pfil"))
                nds = [0, 0]
                for half2 in range(2):
                    M.wait(e, (p, "sct2", half2))
                    for c in range(NCHK):
                        s2 = half2 * NCHK + c
                        if s2 >= 1:
                            M.wait_cnt(e, "ds0", 16 * nds[0])
                            M.wait_cnt(e, "ds1", 16 * nds[1])
                        cds = M.inc(e.indirect_dma_start(
                            out=idxpw_s[:, :],
                            out_offset=bass.IndirectOffsetOnAxis(
                                posb[:, s2:s2 + 1], 0),
                            in_=sctgA[:, s2 * 2:s2 * 2 + 2],
                            in_offset=None),
                            f"ds{s2 % 2}", (p, "sctd", c, half2), 16)
                        nds[s2 % 2] = cds // 16
            def emit_zgather(kblock):
                for k in kblock:
                    ei2, t2, sub2 = gidx(k)
                    gsl = k % 2
                    if k < 2:
                        M.wait(e, (p, "ztst", NG - 1))
                        M.wait(e, (p, "ztst", NG - 2))
                        M.wait(e, (p, "sctd", NCHK - 1, 1))
                        M.wait(e, (p, "sctd", NCHK - 2, 1))
                    else:
                        M.wait(e, (p, "zbst", *gidx(k - 2)))
                        M.wait(e, (p, "pwx", *gidx(k - 2)))
                    r0 = ei2 * CAP + t2 * TT + sub2 * 128
                    sm = f"dg{k % 2}"
                    cidx = M.inc(e.dma_start(
                        out=idxt[:, gsl * 2:gsl * 2 + 2],
                        in_=idxpw_s[r0:r0 + 128, :]), sm, None, 16)
                    M.wait_cnt(e, sm, cidx)
                    M.inc(e.indirect_dma_start(
                        out=zgk[gsl][:, :], out_offset=None,
                        in_=zt_s[:, :],
                        in_offset=bass.IndirectOffsetOnAxis(
                            idxt[:, gsl * 2:gsl * 2 + 1], 0)),
                        sm, (p, "zg", ei2, t2, sub2), 16)
                    if k >= 1:
                        kp = k - 1
                        ep, tp, sp = gidx(kp)
                        gslp = kp % 2
                        M.wait(e, (p, "zg", ep, tp, sp))
                        M.inc(e.tensor_copy(
                            out=pwsb[:, kp:kp + 1],
                            in_=idxt[:, gslp * 2 + 1:gslp * 2 + 2]
                            .bitcast(F32)), "cg", (p, "pwx", ep, tp, sp))
                    if k == E * NST * NSUB - 1:
                        M.wait(e, (p, "zg", ei2, t2, sub2))
                        M.inc(e.tensor_copy(
                            out=pwsb[:, k:k + 1],
                            in_=idxt[:, gsl * 2 + 1:gsl * 2 + 2]
                            .bitcast(F32)), "cg", (p, "pwx", ei2, t2, sub2))
            PPE = NST * NSUB
            if SPARSE:
                emit_zgather(range(0, 2 * PPE))
            def emit_yst1(ei2, j):
                M.wait(e, (p, "ysclh", ei2, j, 1))
                r0 = ei2 * CAP + j * 128
                M.inc(e.dma_start(
                    out=yinst_s[r0:r0 + 128, :],
                    in_=ytr[j % 4][:, 0:C]),
                    f"dy{j % 4}", (p, "yst", ei2, j), 16)
            for i in range(NVE):
                ei, qi = i // NVQ, i % NVQ
                par = i % 2
                if SPARSE and qi == 0 and ei >= 1:
                    for j in range(PPE):
                        emit_yst1(ei - 1, j)
                if SPARSE and ei + 2 < E:
                    emit_zgather(psl((ei + 2) * PPE, qi))
                if i < 2:
                    M.wait(e, (p, "yi", NTT * NCC - 1))
                else:
                    M.wait(e, (p, "ymm_last", i - 2))
                M.inc(e.dma_start(out=w1b[par][:, :], in_=w1h[ei, qi]),
                      f"dw{par}", None, 16)
                if qi == 0:
                    M.inc(e.dma_start(
                        out=b1s[:, (ei % 2) * NCFE:(ei % 2 + 1) * NCFE],
                        in_=b1h[ei]), f"dw{par}", None, 16)
                M.inc(e.dma_start(out=w2b[par][:, :], in_=w2h[ei, qi]),
                      f"dw{par}", (p, "wv", i), 16)
                if SPARSE and i == NVE - 1:
                    for j in range(PPE):
                        emit_yst1(E - 1, j)
            if SPARSE:
                for s_ in ("dy0", "dy1", "dy2", "dy3"):
                    M.wait_cnt(e, s_, M.total(s_))
                for c in range(NCHK):
                    if c >= 1:
                        M.wait(e, (p, "ybtp", c - 1, NCC - 1))
                    for half2 in range(2):
                        M.inc(e.indirect_dma_start(
                            out=ybk[half2][:, 0:C], out_offset=None,
                            in_=yinst_s[:, :],
                            in_offset=bass.IndirectOffsetOnAxis(
                                posb[:, half2 * NCHK + c:
                                     half2 * NCHK + c + 1], 0)),
                            f"dg{c % 2}",
                            (p, "ybg", c) if half2 == 1 else None, 16)
            return

        if ename == "tensor":
            for t in range(NTT):
                ln_tile("tensor", e, M, p, "L", t, xsrc(t), zdst(t),
                        rows_ff, rbf_ff, xbb_ff, xqq_ff, not from_yx,
                        tmpv_f[:, 0:TT], **lng(t))
                M.wait(e, (p, "Lztile", t))
                for s in range(NSUB):
                    g = t * NSUB + s
                    if g >= 4:
                        M.wait(e, (p, "pexp", g - 4))
                    for cc in range(NCC):
                        mm = e.matmul(
                            ps[4 + g % 4][:, 0:E],
                            zh[:, cc * T + t * TT + s * 128:
                               cc * T + t * TT + (s + 1) * 128],
                            wgs[:, cc * E:(cc + 1) * E],
                            start=(cc == 0), stop=(cc == NCC - 1))
                    M.inc(mm, "ct", (p, "lg", g))
                    M.wait(e, (p, "pw", g))
                    if g >= 2:
                        M.wait(e, (p, "pwtev", g - 2))
                    M.inc(e.transpose(ps[6 + g % 2][0:E, 128:256],
                                      pfull[:, 0:E], eyes[:]),
                          "ct", (p, "pwt_ps", g))
                    if SPARSE:
                        if g >= 2:
                            M.wait(e, (p, "crow", g - 2))
                        M.wait(e, (p, "maskw", g))
                        M.inc(e.matmul(ps[6 + g % 2][0:1, 384:384 + E],
                                       onec[:],
                                       maskb[:, g * E:(g + 1) * E],
                                       start=True, stop=True),
                              "ct", (p, "cnt", g))
                if SPARSE:
                    for s in range(NSUB):
                        zi = t * NSUB + s
                        if zi >= 2:
                            M.wait(e, (p, "ztmev", zi - 2))
                        for cc in range(NCC):
                            mm = e.transpose(
                                ps[2 + zi % 2].bitcast(BF16)[
                                    :, cc * 128:(cc + 1) * 128],
                                zh[:, cc * T + t * TT + s * 128:
                                   cc * T + t * TT + (s + 1) * 128],
                                eyeb[:])
                        M.inc(mm, "ct", (p, "ztp", zi))
            if SPARSE:
                M.wait(e, (p, "crow", NG - 1))
                M.inc(e.transpose(ps[4][0:128, 0:1], crow[0:1, 0:128],
                                  eyes[0:1, 0:1]), "ct", (p, "ccolp"))
                M.wait(e, (p, "ccbev"))
                M.inc(e.matmul(ps[4][:, 0:1], selm[:, :], ccb[:, 0:1],
                               start=True, stop=True), "ct", (p, "cbp"))
                M.wait(e, (p, "qscols"))
                M.inc(e.transpose(ps[4][0:1, 0:128], cbt[:, 1:2], eyes[:]),
                      "ct", (p, "qtp"))
                M.wait(e, (p, "qev2"))
                M.inc(e.transpose(ps[4][0:1, 0:128], cbt[:, 2:3], eyes[:]),
                      "ct", (p, "stp"))
                M.wait(e, (p, "qsrev"))
                M.wait(e, (p, "pexp", NG - 2))
                e.matmul(ps[6][:, 0:NCHK * E], ltc[:, :],
                         maskb[:, 0:NCHK * E], start=True, stop=False)
                e.matmul(ps[6][:, 0:NCHK * E], c64r[:, :],
                         qsr[0:1, 0:NCHK * E], start=False, stop=False)
                M.inc(e.matmul(ps[6][:, 0:NCHK * E], oner[:, :],
                               qsr[0:1, 128:128 + NCHK * E],
                               start=False, stop=True),
                      "ct", (p, "pospAll"))
            M.wait(e, (p, "pwtev", NG - 1))
            M.wait(e, (p, "Lstatsread", NTT - 1))
            if SPARSE:
                M.wait(e, (p, "crow", NG - 1))
            for t in range(NTT):
                for cc in range(NCC):
                    g = t * NCC + cc
                    if g >= 2:
                        M.wait(e, (p, "yi", g - 2))
                    M.inc(e.matmul(ps[g % 2][:, 0:TT],
                                   b2s[:, cc * 128:(cc + 1) * 128],
                                   pwtb[:, t * TT:(t + 1) * TT],
                                   start=True, stop=True),
                          "ct", (p, "bm", g))
            def emit_ztr2(e, kblock):
                for k in kblock:
                    ei2, t2, sub2 = gidx(k)
                    gsl = k % 2
                    pro = k < 2 * PPE
                    bank = 2 + (k % 2 if pro else 0)
                    M.wait(e, (p, "zg", ei2, t2, sub2))
                    if k < 2:
                        M.wait(e, (p, "ztmev", NG - 2 + k))
                    elif pro:
                        M.wait(e, (p, "zbev_v", k - 2))
                    else:
                        M.wait(e, (p, "zbev_v", k - 1))
                    for cc in range(NCC):
                        mm = e.transpose(
                            ps[bank].bitcast(BF16)[
                                :, cc * 128:(cc + 1) * 128],
                            zgk[gsl][:, cc * 128:(cc + 1) * 128],
                            eyeb[:])
                    M.inc(mm, "ct", (p, "ztr2", ei2, t2, sub2))
            PPE = NST * NSUB
            if SPARSE:
                emit_ztr2(e, range(0, 2 * PPE))
            for i in range(NVE):
                par = i % 2
                ei, qi = i // NVQ, i % NVQ
                M.wait(e, (p, "wv", i))
                for t in range(NTI):
                    g = i * NTI + t
                    if not SPARSE:
                        if g >= 2:
                            M.wait(e, (p, "pbev", g - 2))
                        M.inc(e.matmul(ps[4 + g % 2][:, 0:TT],
                                       ohbs[:, ei * 128:(ei + 1) * 128],
                                       pwtb[:, t * TT:(t + 1) * TT],
                                       start=True, stop=True),
                              "ct", (p, "pb", g))
                    if SPARSE:
                        M.wait(e, (p, "zeld", ei, t))
                    HBm = HBS if SPARSE else HB
                    for fc in range(NFV):
                        hg = g * NFV + fc
                        if SPARSE and hg == 0:
                            M.wait(e, (p, "posv2"))
                        if SPARSE and hg == 1:
                            M.wait(e, (p, "pwtev", NG - 1))
                        if SPARSE and hg == 2:
                            M.wait(e, (p, "zbev_v", 2 * PPE - 1))
                        if hg == 2 and not SPARSE:
                            M.wait(e, (p, "Lztile", NTT - 1))
                        if hg >= len(HBm):
                            M.wait(e, (p, "hev", hg - len(HBm)))
                        for cc in range(NCC):
                            if SPARSE:
                                mslice = sub_ap(zeS[t][:, :], cc * 128,
                                                [(NCC * 128, NSUB),
                                                 (1, 128)])
                            else:
                                mslice = zh[:, cc * T + t * TT:
                                            cc * T + (t + 1) * TT]
                            mm = e.matmul(
                                ps[HBm[hg % len(HBm)]][:, 0:TT],
                                w1b[par][:, cc * FV + fc * 128:
                                         cc * FV + (fc + 1) * 128],
                                mslice,
                                start=(cc == 0), stop=(cc == NCC - 1))
                        M.inc(mm, "ct", (p, "h", hg))
                    M.wait(e, (p, "hev", g * NFV + NFV - 1))
                    if i == 0 and t == 0:
                        M.wait(e, (p, "yi", NTT * NCC - 1))
                    YBm = YBS if SPARSE else YB
                    for dc in range(NCC):
                        yg = g * NCC + dc
                        if yg == 2 and not SPARSE:
                            M.wait(e, (p, "Lztile", NTT - 1))
                        if yg >= len(YBm):
                            M.wait(e, (p, "yev" if not SPARSE else "yadd",
                                       yg - len(YBm)))
                        for fc in range(NFV):
                            mm = e.matmul(
                                ps[YBm[yg % len(YBm)]][:, 0:TT],
                                w2b[par][:, fc * C + dc * 128:
                                         fc * C + (dc + 1) * 128],
                                hsb[:, fc * TT:(fc + 1) * TT],
                                start=(fc == 0), stop=(fc == NFV - 1))
                        key = (p, "ymm", yg)
                        if t == NTI - 1 and dc == NCC - 1:
                            key = (p, "ymm_last", i)
                        M.inc(mm, "ct", key)
                if SPARSE and qi == NVQ - 1 and t == NTI - 1:
                    for j in range(PPE):
                        tl = j // NSUB
                        M.wait(e, (p, "yadd",
                                   (i * NTI + tl) * NCC + NCC - 1))
                        jj = ei * PPE + j
                        for hh in range(2):
                            if jj >= 1:
                                M.wait(e, (p, "ysclh",
                                           *(divmod(jj - 1, PPE) + (hh,))))
                            elif hh == 0:
                                M.wait(e, (p, "qsrev"))
                                M.wait(e, (p, "pexp", NG - 1))
                                M.wait(e, (p, "pexp", NG - 2))
                            for dcl in range(NCC // 2):
                                dc = hh * (NCC // 2) + dcl
                                mm = e.transpose(
                                    ps[4 + hh][:, dcl * 128:
                                               (dcl + 1) * 128],
                                    yacc_f[:, dc * CAP + j * 128:
                                           dc * CAP + (j + 1) * 128],
                                    eyes[:])
                            M.inc(mm, "ct", (p, "ytp2h", ei, j, hh))
                if SPARSE and ei + 2 < E:
                    emit_ztr2(e, psl((ei + 2) * PPE, qi))
            if SPARSE:
                for c in range(NCHK):
                    M.wait(e, (p, "ybsum", c))
                    if c == 0:
                        M.wait(e, (p, "ysclh", E - 1, PPE - 1, 1))
                    for cc in range(NCC):
                        if cc >= 2:
                            M.wait(e, (p, "ybadd", c, cc - 2))
                        elif c >= 1:
                            M.wait(e, (p, "ybadd", c - 1, NCC - 1))
                        mm = e.transpose(
                            ps[4 + cc % 2][:, 0:128],
                            ybk[0][:, cc * 128:(cc + 1) * 128], eyes[:])
                        M.inc(mm, "ct", (p, "ybtp", c, cc))
            return

        if ename == "scalar":
            for t in range(NTT):
                ln_tile("scalar", e, M, p, "L", t, xsrc(t), zdst(t),
                        rows_ff, rbf_ff, xbb_ff, xqq_ff, not from_yx,
                        tmpv_f[:, 0:TT], **lng(t))
                for s in range(NSUB):
                    g = t * NSUB + s
                    M.wait(e, (p, "negmax", g))
                    M.inc(e.activation(out=ppex[:, 0:E],
                                       in_=ps[4 + g % 4][:, 0:E],
                                       func=AF.Exp, bias=ppm[:, 0:1],
                                       scale=1.0), "cs", (p, "pexp", g))
                    M.wait(e, (p, "pwt_ps", g))
                    M.inc(e.copy(out=pwtb[0:E, t * TT + s * 128:
                                          t * TT + (s + 1) * 128],
                                 in_=ps[6 + g % 2][0:E, 128:256]),
                          "cs", (p, "pwtev", g))
                    if SPARSE:
                        M.wait(e, (p, "cnt", g))
                        M.inc(e.copy(out=crow[0:1, g * E:(g + 1) * E],
                                     in_=ps[6 + g % 2][0:1, 384:384 + E]),
                              "cs", (p, "crow", g))
                if SPARSE:
                    for s in range(NSUB):
                        zi = t * NSUB + s
                        M.wait(e, (p, "ztp", zi))
                        if zi >= 2:
                            M.wait(e, (p, "ztst", zi - 2))
                        for cc in range(NCC):
                            op = e.copy(
                                out=ztma[zi % 2][:, cc * 128:(cc + 1) * 128],
                                in_=ps[2 + zi % 2].bitcast(BF16)[
                                    :, cc * 128:(cc + 1) * 128])
                            M.inc(op, "cs",
                                  (p, "ztmev", zi) if cc == NCC - 1 else None)
            if SPARSE:
                M.wait(e, (p, "ccolp"))
                M.inc(e.copy(out=ccb[:, 0:1], in_=ps[4][0:128, 0:1]),
                      "cs", (p, "ccbev"))
                M.wait(e, (p, "qtp"))
                M.inc(e.copy(out=qsr[0:1, 0:128], in_=ps[4][0:1, 0:128]),
                      "cs", (p, "qev2"))
                M.wait(e, (p, "stp"))
                M.inc(e.copy(out=qsr[0:1, 128:256], in_=ps[4][0:1, 0:128]),
                      "cs", (p, "qsrev"))
            for i in range(NVE):
                ei, qi = i // NVQ, i % NVQ
                if SPARSE and qi == 0:
                    for t in range(NST):
                        if ei >= 1:
                            hlast = (((ei - 1) * NVQ + NVQ - 1) * NTI
                                     + t) * NFV + NFV - 1
                            M.wait(e, (p, "h", hlast))
                        M.wait(e, (p, "zbst", ei, t, NSUB - 1))
                        M.wait(e, (p, "zbst", ei, t, NSUB - 2))
                        M.inc(e.dma_start(out=zeS[t][:, :],
                                          in_=zc_s[ei, t]),
                              f"dze{t % 2}", (p, "zeld", ei, t), 16)
                for t in range(NTI):
                    g = i * NTI + t
                    if not SPARSE:
                        M.wait(e, (p, "pb", g))
                        if g >= 2:
                            M.wait(e, (p, "yev", (g - 2) * NCC + NCC - 1))
                        M.inc(e.copy(
                            out=pbs_f[:, (g % 2) * TT:(g % 2 + 1) * TT],
                            in_=ps[4 + g % 2][:, 0:TT]),
                            "cs", (p, "pbev", g))
                    for fc in range(NFV):
                        hg = g * NFV + fc
                        M.wait(e, (p, "h", hg))
                        fcg = (ei % 2) * NCFE + qi * NFV + fc
                        HBm = HBS if SPARSE else HB
                        M.inc(e.activation(
                            out=hsb[:, fc * TT:(fc + 1) * TT],
                            in_=ps[HBm[hg % len(HBm)]][:, 0:TT],
                            func=AF.Gelu,
                            bias=b1s[:, fcg:fcg + 1], scale=1.0),
                            "cs", (p, "hev", hg))
            return

        if ename == "vector":
            def ch(op, key=None):
                c = M.inc(op, "cv", key)
                M.wait_cnt(e, "cv", c)
            for t in range(NTT):
                ln_tile("vector", e, M, p, "L", t, xsrc(t), zdst(t),
                        rows_ff, rbf_ff, xbb_ff, xqq_ff, not from_yx,
                        tmpv_f[:, 0:TT], **lng(t))
                for s in range(NSUB):
                    g = t * NSUB + s
                    M.wait(e, (p, "lg", g))
                    if g >= 1:
                        M.wait(e, (p, "pwt_ps", g - 1))
                    ch(e.tensor_reduce(out=ppm[:, 0:1],
                                       in_=ps[4 + g % 4][:, 0:E],
                                       axis=AX.X, op=ALU.max, negate=True),
                       (p, "negmax", g))
                    M.wait(e, (p, "pexp", g))
                    ch(e.tensor_reduce(out=ppm[:, 0:1], in_=ppex[:, 0:E],
                                       axis=AX.X, op=ALU.max))
                    ch(e.tensor_scalar(out=ppb[:, 0:E], in0=ppex[:, 0:E],
                                       scalar1=ppm[:, 0:1], scalar2=None,
                                       op0=ALU.is_ge))
                    if SPARSE:
                        ch(e.tensor_copy(out=eqf[:, g * E:(g + 1) * E],
                                         in_=ppb[:, 0:E]))
                    ch(e.scalar_tensor_tensor(out=ppb[:, 0:E],
                                              in0=ppb[:, 0:E], scalar=-2.0,
                                              in1=ppex[:, 0:E], op0=ALU.mult,
                                              op1=ALU.add))
                    ch(e.tensor_reduce(out=ppm[:, 0:1], in_=ppb[:, 0:E],
                                       axis=AX.X, op=ALU.max))
                    ch(e.tensor_scalar(out=ppb[:, 0:E], in0=ppex[:, 0:E],
                                       scalar1=ppm[:, 0:1], scalar2=None,
                                       op0=ALU.is_ge))
                    if SPARSE:
                        ch(e.tensor_copy(out=maskf[:, g * E:(g + 1) * E],
                                         in_=ppb[:, 0:E]))
                        ch(e.tensor_copy(out=maskb[:, g * E:(g + 1) * E],
                                         in_=ppb[:, 0:E]), (p, "maskw", g))
                    ch(e.tensor_tensor(out=ppa[:, 0:E], in0=ppex[:, 0:E],
                                       in1=ppb[:, 0:E], op=ALU.mult))
                    ch(e.tensor_reduce(out=ppr[:, 0:1], in_=ppa[:, 0:E],
                                       axis=AX.X, op=ALU.add))
                    ch(e.tensor_scalar(out=ppr[:, 0:1], in0=ppr[:, 0:1],
                                       scalar1=2.0, scalar2=2 * EPS_GATE,
                                       op0=ALU.mult, op1=ALU.add))
                    ch(e.reciprocal(out=ppr[:, 0:1], in_=ppr[:, 0:1]))
                    ch(e.tensor_scalar(out=pfull[:, 0:E], in0=ppa[:, 0:E],
                                       scalar1=ppr[:, 0:1], scalar2=None,
                                       op0=ALU.mult), (p, "pw", g))
                    if SPARSE:
                        ch(e.tensor_copy(out=pwm[:, g * E:(g + 1) * E],
                                         in_=pfull[:, 0:E]), (p, "pwmw", g))
            if SPARSE:
                M.wait(e, (p, "cbp"))
                ch(e.tensor_tensor(out=cbt[:, 0:1], in0=ps[4][:, 0:1],
                                   in1=segc[:, 0:1], op=ALU.add))
                st_i = idxt[:, 0:1]
                ch(e.tensor_copy(out=st_i, in_=cbt[:, 0:1]))
                ch(e.tensor_scalar(out=cbt[:, 2:3].bitcast(mybir.dt.int32),
                                   in0=st_i, scalar1=63, scalar2=None,
                                   op0=ALU.bitwise_and))
                ch(e.tensor_copy(out=cbt[:, 2:3],
                                 in_=cbt[:, 2:3].bitcast(mybir.dt.int32)))
                ch(e.tensor_scalar(out=cbt[:, 1:2].bitcast(mybir.dt.int32),
                                   in0=st_i, scalar1=6, scalar2=None,
                                   op0=ALU.arith_shift_right))
                ch(e.tensor_copy(out=cbt[:, 1:2],
                                 in_=cbt[:, 1:2].bitcast(mybir.dt.int32)),
                   (p, "qscols"))
                NE = NCHK * E
                M.wait(e, (p, "pospAll"))
                M.wait(e, (p, "pwmw", NG - 1))
                M.wait(e, "dmpc")
                ch(e.tensor_scalar(out=ppA[:, 0:NE],
                                   in0=maskf[:, 0:NE],
                                   scalar1=-1.0, scalar2=1.0,
                                   op0=ALU.mult, op1=ALU.add))
                ch(e.tensor_scalar(out=ppA[:, 0:NE], in0=ppA[:, 0:NE],
                                   scalar1=dmpc[:, 0:1], scalar2=None,
                                   op0=ALU.mult))
                ch(e.tensor_tensor(out=ppB[:, 0:NE], in0=ps[6][:, 0:NE],
                                   in1=maskf[:, 0:NE], op=ALU.mult),
                   (p, "posv2"))
                ch(e.tensor_tensor(out=ppA[:, 0:NE], in0=ppB[:, 0:NE],
                                   in1=ppA[:, 0:NE], op=ALU.add))

                def segsum(dst, srcsel):
                    assert E & (E - 1) == 0
                    w = E
                    while w > 1:
                        h2 = w // 2
                        va = sub_ap(ppB[:, 0:NE], 0, [(E, NCHK), (1, h2)])
                        vb = sub_ap(ppB[:, 0:NE], h2, [(E, NCHK), (1, h2)])
                        ch(e.tensor_tensor(out=va, in0=va, in1=vb,
                                           op=ALU.add))
                        w = h2
                    ch(e.tensor_copy(
                        out=dst, in_=sub_ap(ppB[:, 0:NE], 0,
                                            [(E, NCHK), (1, 1)])))
                ch(e.tensor_tensor(out=ppB[:, 0:NE], in0=ppA[:, 0:NE],
                                   in1=eqf[:, 0:NE], op=ALU.mult))
                segsum(posb[:, 0:NCHK], None)
                ch(e.tensor_tensor(out=ppB[:, 0:NE], in0=pwm[:, 0:NE],
                                   in1=eqf[:, 0:NE], op=ALU.mult))
                segsum(pw12[:, 0:NCHK], None)
                ch(e.tensor_tensor(out=ppB[:, 0:NE], in0=maskf[:, 0:NE],
                                   in1=eqf[:, 0:NE], op=ALU.subtract))
                ch(e.tensor_tensor(out=ppA[:, 0:NE], in0=ppA[:, 0:NE],
                                   in1=ppB[:, 0:NE], op=ALU.mult))
                ch(e.tensor_tensor(out=ppB[:, 0:NE], in0=pwm[:, 0:NE],
                                   in1=ppB[:, 0:NE], op=ALU.mult))
                segsum(pw12[:, NCHK:2 * NCHK], None)
                ch(e.tensor_copy(out=ppB[:, 0:NE], in_=ppA[:, 0:NE]))
                segsum(posb[:, NCHK:2 * NCHK], None)
                sctf = sctgA[:, :].bitcast(F32)
                for half in range(2):
                    ch(e.tensor_copy(
                        out=sub_ap(sctgA[:, :], 2 * NCHK * half,
                                   [(2, NCHK)]),
                        in_=tidh[:, 0:NCHK]))
                    ch(e.tensor_copy(
                        out=sub_ap(sctf, 2 * NCHK * half + 1, [(2, NCHK)]),
                        in_=pw12[:, half * NCHK:(half + 1) * NCHK]),
                       (p, "sct2", half))
            if from_yx and dbg is not None:
                M.wait(e, (p, "dbgst"))
            for t in range(NTT):
                if not from_yx:
                    M.wait(e, (p, "xld2", t))
                for cc in range(NCC):
                    g = t * NCC + cc
                    M.wait(e, (p, "bm", g))
                    dst = yx[:, cc * T + t * TT:cc * T + (t + 1) * TT]
                    src0 = dst if from_yx else \
                        xstr_f[:, cc * TT:(cc + 1) * TT]
                    ch(e.tensor_tensor(out=dst, in0=src0,
                                       in1=ps[g % 2][:, 0:TT], op=ALU.add),
                       (p, "yi", g))
            if SPARSE:
                def emit_zbev(kblock):
                    for k in kblock:
                        ei2, t2, sub2 = gidx(k)
                        gsl = k % 2
                        bank = 2 + (k % 2 if k < 2 * PPE else 0)
                        M.wait(e, (p, "ztr2", ei2, t2, sub2))
                        if k >= 2:
                            M.wait(e, (p, "zbst", *gidx(k - 2)))
                        for cc in range(NCC):
                            op = e.tensor_copy(
                                out=zgk[gsl][:, cc * 128:(cc + 1) * 128],
                                in_=ps[bank].bitcast(BF16)[
                                    :, cc * 128:(cc + 1) * 128])
                            M.inc(op, "cv",
                                  (p, "zbev_v", k) if cc == NCC - 1
                                  else None)
                        M.wait_cnt(e, "cv", M.total("cv"))
                PPE = NST * NSUB
                emit_zbev(range(0, 2 * PPE))
                for i in range(NVE):
                    ei, qi = i // NVQ, i % NVQ
                    for t in range(NST):
                        g = i * NST + t
                        for dc in range(NCC):
                            yg = g * NCC + dc
                            if t == NST - 1 and dc == NCC - 1:
                                M.wait(e, (p, "ymm_last", i))
                            else:
                                M.wait(e, (p, "ymm", yg))
                            dst = yacc_f[:, dc * CAP + t * TT:
                                         dc * CAP + (t + 1) * TT]
                            if qi == 0:
                                if ei >= 1 and dc == 0:
                                    M.wait(e, (p, "ytp2h", ei - 1,
                                               t * NSUB + NSUB - 1, 1))
                                op = e.tensor_copy(
                                    out=dst, in_=ps[YBS[yg % 2]][:, 0:TT])
                            else:
                                op = e.tensor_tensor(
                                    out=dst, in0=dst,
                                    in1=ps[YBS[yg % 2]][:, 0:TT],
                                    op=ALU.add)
                            ch(op, (p, "yadd", yg))
                    if qi == NVQ - 1:
                        for j in range(PPE):
                            jj = ei * PPE + j
                            M.wait(e, (p, "ytp2h", ei, j, 0))
                            if jj >= 4:
                                M.wait(e, (p, "yst", *divmod(jj - 4, PPE)))
                            M.wait(e, (p, "pwx", ei, j // NSUB, j % NSUB))
                            ch(e.tensor_scalar(
                                out=ytr[j % 4][:, 0:C // 2],
                                in0=ps[4][:, 0:C // 2],
                                scalar1=pwsb[:, jj:jj + 1],
                                scalar2=None, op0=ALU.mult),
                               (p, "ysclh", ei, j, 0))
                            M.wait(e, (p, "ytp2h", ei, j, 1))
                            ch(e.tensor_scalar(
                                out=ytr[j % 4][:, C // 2:C],
                                in0=ps[5][:, 0:C // 2],
                                scalar1=pwsb[:, jj:jj + 1],
                                scalar2=None, op0=ALU.mult),
                               (p, "ysclh", ei, j, 1))
                    if ei + 2 < E:
                        emit_zbev(psl((ei + 2) * PPE, qi))
                for c in range(NCHK):
                    M.wait(e, (p, "ybg", c))
                    ch(e.tensor_tensor(out=ybk[0][:, 0:C],
                                       in0=ybk[0][:, 0:C],
                                       in1=ybk[1][:, 0:C], op=ALU.add),
                       (p, "ybsum", c))
                    for cc in range(NCC):
                        M.wait(e, (p, "ybtp", c, cc))
                        dst = yx[:, cc * T + c * 128:cc * T + (c + 1) * 128]
                        key = (p, "ybadd", c, cc)
                        if c == NCHK - 1 and cc == NCC - 1:
                            key = (p, "ffn_done")
                        ch(e.tensor_tensor(
                            out=dst, in0=dst,
                            in1=ps[4 + cc % 2][:, 0:128],
                            op=ALU.add), key)
            else:
                for i in range(NVE):
                    for t in range(NTT):
                        g = i * NTT + t
                        M.wait(e, (p, "pbev", g))
                        for dc in range(NCC):
                            yg = g * NCC + dc
                            if t == NTT - 1 and dc == NCC - 1:
                                M.wait(e, (p, "ymm_last", i))
                            else:
                                M.wait(e, (p, "ymm", yg))
                            ch(e.tensor_tensor(
                                out=tmpv_f[:, 0:TT],
                                in0=ps[YB[yg % 3]][:, 0:TT],
                                in1=pbs_f[:, (g % 2) * TT:(g % 2 + 1) * TT],
                                op=ALU.mult))
                            dst = yx[:, dc * T + t * TT:dc * T + (t + 1) * TT]
                            key = (p, "yev", yg)
                            if i == NVE - 1 and t == NTT - 1 and dc == NCC - 1:
                                key = (p, "ffn_done")
                            ch(e.tensor_tensor(out=dst, in0=dst,
                                               in1=tmpv_f[:, 0:TT],
                                               op=ALU.add), key)
            return

    def attn_phase(ename, e, M, p, b):
        NGK = NTT * NSUB * NKVG
        NSG = NTT * NSUB

        def xsrc(t):
            def f(cc):
                return yx[:, cc * T + t * TT:cc * T + (t + 1) * TT]
            return f

        zdst = xsrc

        if ename == "sync":
            M.inc(e.dma_start(out=x1s[b], in_=yx[:]), "st", (p, "x1st"), 16)
            if NL >= 1:
                M.wait(e, (p, "projmm", NL - 1, NTT - 1, NCC - 1))
            else:
                M.wait(e, (p, "Aztile", NTT - 1))
            CH = min(T, (NCC * TT) // 2)
            for cc in range(NCC):
                for j in range(T // CH):
                    if cc + j > 0:
                        M.wait(e, (p, "xaadd", cc, j - 1) if j >= 1 else
                               (p, "xaadd", cc - 1, T // CH - 1))
                    M.inc(e.dma_start(
                        out=xa_f[:, 0:CH],
                        in_=x1s[b][:, cc * T + j * CH:cc * T + (j + 1) * CH]),
                        "dx", (p, "x1ld", cc, j), 16)
            return

        if ename == "gpsimd":
            for t in range(NTT):
                ln_tile("gpsimd", e, M, p, "A", t, xsrc(t), zdst(t),
                        rows_at, rbf_at, xbA[0], xbA[1], False,
                        tmpa_f[:, 0:TT], z_gate=(p, "x1st"))
            for L in range(NL):
                if L == 0:
                    M.wait(e, (p, "Aztile", NTT - 1))
                else:
                    M.wait(e, (p, "kvmm", L - 1, NGK - 1))
                M.inc(e.dma_start(out=wkv[:, :], in_=wkvh_d[L]),
                      "dwa", None, 16)
                M.inc(e.dma_start(
                    out=bkvs[0:1, (L % 2) * 2 * C:(L % 2 + 1) * 2 * C],
                    in_=bkvh[L]), "dwa", (p, "wkv", L), 16)
                if L >= 1:
                    M.wait(e, (p, "projmm", L - 1, NTT - 1, NCC - 1))
                M.inc(e.dma_start(out=wqv[:, :], in_=wqh_d[L]),
                      "dwb", None, 16)
                M.inc(e.dma_start(out=wpv[:, :], in_=wprojh[L]),
                      "dwb", None, 16)
                M.inc(e.dma_start(out=bqs[:, (L % 2) * NCC:(L % 2 + 1) * NCC],
                                  in_=bqqh[L]), "dwb", None, 16)
                M.inc(e.dma_start(out=bps[:, (L % 2) * NCC:(L % 2 + 1) * NCC],
                                  in_=bpjh[L]), "dwb", (p, "wqp", L), 16)
            return

        if ename == "tensor":
            for t in range(NTT):
                ln_tile("tensor", e, M, p, "A", t, xsrc(t), zdst(t),
                        rows_at, rbf_at, xbA[0], xbA[1], False,
                        tmpa_f[:, 0:TT])
            for L in range(NL):
                M.wait(e, (p, "wkv", L))
                def emit_S(sg):
                    slot = (sg % 2) * 2 * C
                    if sg == 0 and L == 0:
                        M.wait(e, (p, "Aztile", NTT - 1))
                    if sg >= 1:
                        M.wait(e, (p, "sacc", L, sg - 1))
                    elif L >= 1:
                        M.wait(e, (p, "sacc", L - 1, NSG - 1))
                    for h in range(H):
                        mm = e.matmul(
                            ps[2 + h // SP][0:64,
                                            (h % SP) * 64:(h % SP) * 64 + 64],
                            kvt[:, slot + h * DH:slot + (h + 1) * DH],
                            kvt[:, slot + C + h * DH:
                                slot + C + (h + 1) * DH],
                            start=True, stop=True)
                    M.inc(mm, "ct", (p, "smm", L, sg))
                for t in range(NTT):
                    M.wait(e, (p, "xbc1", L, t))
                    for sub in range(NSUB):
                        sg = t * NSUB + sub
                        for kg in range(NKVG):
                            gk = sg * NKVG + kg
                            if L == 0 and gk < 2:
                                M.wait(e, (p, "Astatsread", NTT - 1))
                            if L >= 1 and gk in (2, 3):
                                M.wait(e, (p, "projev", L - 1, NTT - 1,
                                           NCC - 1))
                            if gk >= 4:
                                M.wait(e, (p, "kvev", L, gk - 4))
                            for cc in range(NCC):
                                e.matmul(
                                    ps[KB[gk % 4]][:, 0:512],
                                    xbA[t % 2][:, cc * TT + sub * 128:
                                               cc * TT + (sub + 1) * 128],
                                    wkv[:, cc * 2 * C + kg * 512:
                                        cc * 2 * C + (kg + 1) * 512],
                                    start=(cc == 0), stop=False)
                            M.inc(e.matmul(
                                ps[KB[gk % 4]][:, 0:512], oner[:],
                                bkvs[0:1, (L % 2) * 2 * C + kg * 512:
                                     (L % 2) * 2 * C + (kg + 1) * 512],
                                start=False, stop=True),
                                "ct", (p, "kvmm", L, gk))
                        if sg >= 1:
                            M.wait(e, (p, "kvev", L,
                                       (sg - 1) * NKVG + NKVG - 1))
                            emit_S(sg - 1)
                        if sg == NSUB * NTT - 1:
                            M.wait(e, (p, "kvev", L, sg * NKVG + NKVG - 1))
                            emit_S(sg)
                M.wait(e, (p, "wqp", L))
                M.wait(e, (p, "sbf2", L))
                def qwait(t, fc):
                    if fc >= 3:
                        M.wait(e, (p, "qev", L, t, fc - 3))
                    elif t >= 1:
                        M.wait(e, (p, "projev", L, t - 1, NCC - 1))
                    elif L >= 1:
                        M.wait(e, (p, "projev", L - 1, NTT - 1, NCC - 1))
                for t in range(NTT):
                    M.wait(e, (p, "xbc2", L, t))
                    if t == 0:
                        M.wait(e, (p, "kvev", L, NGK - 1))
                    for fc in range(NCC):
                        qwait(t, fc)
                        for cc in range(NCC):
                            mm = e.matmul(
                                ps[QB[fc % 3]][:, 0:TT],
                                wqv[:, cc * C + fc * 128:
                                    cc * C + (fc + 1) * 128],
                                xbA[t % 2][:, cc * TT:(cc + 1) * TT],
                                start=(cc == 0), stop=(cc == NCC - 1))
                        M.inc(mm, "ct", (p, "qmm", L, t, fc))
                    for fc in range(NCC):
                        M.wait(e, (p, "qev", L, t, fc))
                        if fc >= 3:
                            M.wait(e, (p, "yTev", L, t, fc - 3))
                        elif t >= 1:
                            M.wait(e, (p, "yTev", L, t - 1, NCC - 1))
                        elif L >= 1:
                            M.wait(e, (p, "yTev", L - 1, NTT - 1, NCC - 1))
                        M.inc(e.matmul(ps[TB[fc % 3]][:, 0:TT],
                                       sbf2[:, fc * 128:(fc + 1) * 128],
                                       qTt[:, fc * TT:(fc + 1) * TT],
                                       start=True, stop=True),
                              "ct", (p, "yTmm", L, t, fc))
                    M.wait(e, (p, "yTev", L, t, NCC - 1))
                    for cc in range(NCC):
                        if cc >= 3:
                            M.wait(e, (p, "projev", L, t, cc - 3))
                        for fc in range(NCC):
                            mm = e.matmul(
                                ps[PB[cc % 3]][:, 0:TT],
                                wpv[:, fc * C + cc * 128:
                                    fc * C + (cc + 1) * 128],
                                yTt[:, fc * TT:(fc + 1) * TT],
                                start=(fc == 0), stop=(fc == NCC - 1))
                        M.inc(mm, "ct", (p, "projmm", L, t, cc))
            return

        if ename == "scalar":
            for t in range(NTT):
                ln_tile("scalar", e, M, p, "A", t, xsrc(t), zdst(t),
                        rows_at, rbf_at, xbA[0], xbA[1], False,
                        tmpa_f[:, 0:TT])
            for L in range(NL):
                for t in range(NTT):
                    for sub in range(NSUB):
                        for kg in range(NKVG):
                            gk = (t * NSUB + sub) * NKVG + kg
                            sg = t * NSUB + sub
                            M.wait(e, (p, "kvmm", L, gk))
                            slot = (sg % 2) * 2 * C
                            M.inc(e.copy(
                                out=kvt[:, slot + kg * 512:
                                        slot + (kg + 1) * 512],
                                in_=ps[KB[gk % 4]][:, 0:512]),
                                "cs", (p, "kvev", L, gk))
                M.wait(e, (p, "sacc", L, NSG - 1))
                for h in range(H):
                    off = (h % 2) * 64
                    fc = h // 2
                    M.inc(e.activation(
                        out=sbf2[off:off + 64,
                                 fc * 128 + off:fc * 128 + off + 64],
                        in_=ssb[0:64, h * DH:(h + 1) * DH],
                        func=AF.Identity, bias=0.0, scale=0.125),
                        "cs", (p, "sbf2", L) if h == H - 1 else None)
                for t in range(NTT):
                    if t < 2:
                        M.wait(e, (p, "smm", L, NSG - 1))
                    else:
                        M.wait(e, (p, "qmm", L, t - 2, NCC - 1))
                    for cc in range(NCC):
                        M.inc(e.copy(out=xbA[t % 2][:, cc * TT:(cc + 1) * TT],
                                     in_=xsrc(t)(cc)), "cs",
                              (p, "xbc2", L, t) if cc == NCC - 1 else None)
                    M.wait_cnt(e, "cs", M.total("cs"))
                    for fc in range(NCC):
                        M.wait(e, (p, "qmm", L, t, fc))
                        M.inc(e.activation(
                            out=qTt[:, fc * TT:(fc + 1) * TT],
                            in_=ps[QB[fc % 3]][:, 0:TT], func=AF.Identity,
                            bias=bqs[:, (L % 2) * NCC + fc:
                                     (L % 2) * NCC + fc + 1], scale=1.0),
                            "cs", (p, "qev", L, t, fc))
                    for fc in range(NCC):
                        M.wait(e, (p, "yTmm", L, t, fc))
                        M.inc(e.copy(out=yTt[:, fc * TT:(fc + 1) * TT],
                                     in_=ps[TB[fc % 3]][:, 0:TT]),
                              "cs", (p, "yTev", L, t, fc))
                    for cc in range(NCC):
                        M.wait(e, (p, "projmm", L, t, cc))
                        if cc >= 2:
                            M.wait(e, (p, "xadd", L, t, cc - 2))
                        elif t >= 1:
                            M.wait(e, (p, "xadd", L, t - 1, NCC - 2 + cc))
                        elif L >= 1:
                            M.wait(e, (p, "xadd", L - 1, NTT - 1,
                                       NCC - 2 + cc))
                        M.inc(e.activation(
                            out=tmpa_f[:, (cc % 2) * TT:(cc % 2 + 1) * TT],
                            in_=ps[PB[cc % 3]][:, 0:TT], func=AF.Identity,
                            bias=bps[:, (L % 2) * NCC + cc:
                                     (L % 2) * NCC + cc + 1], scale=1.0),
                            "cs", (p, "projev", L, t, cc))
            return

        if ename == "vector":
            def ch(op, key=None):
                c = M.inc(op, "cv", key)
                M.wait_cnt(e, "cv", c)
            for t in range(NTT):
                ln_tile("vector", e, M, p, "A", t, xsrc(t), zdst(t),
                        rows_at, rbf_at, xbA[0], xbA[1], False,
                        tmpa_f[:, 0:TT], z_gate=(p, "x1st"))
            for L in range(NL):
                for t in range(NTT):
                    if t >= 2:
                        M.wait(e, (p, "kvmm", L,
                                   ((t - 2) * NSUB + NSUB - 1) * NKVG +
                                   NKVG - 1))
                    elif L == 0:
                        M.wait(e, (p, "As1", NTT - 1))
                        M.wait(e, (p, "As2", NTT - 1))
                    else:
                        M.wait(e, (p, "projmm", L - 1, NTT - 1, NCC - 1))
                    for cc in range(NCC):
                        M.inc(e.tensor_copy(
                            out=xbA[t % 2][:, cc * TT:(cc + 1) * TT],
                            in_=xsrc(t)(cc)), "cv",
                            (p, "xbc1", L, t) if cc == NCC - 1 else None)
                    M.wait_cnt(e, "cv", M.total("cv"))
                    for sub in range(NSUB):
                        sg = t * NSUB + sub
                        for s_ in ([sg - 1] if sg >= 1 else []) + \
                                ([sg] if sg == NSUB * NTT - 1 else []):
                            M.wait(e, (p, "smm", L, s_))
                            first = (s_ == 0)
                            if first and L >= 1:
                                M.wait(e, (p, "sbf2", L - 1))
                            if first:
                                ch(e.tensor_copy(out=ssb[:, 0:SP * 64],
                                                 in_=ps[2][0:64, 0:SP * 64]))
                            else:
                                ch(e.tensor_tensor(
                                    out=ssb[:, 0:SP * 64],
                                    in0=ssb[:, 0:SP * 64],
                                    in1=ps[2][0:64, 0:SP * 64], op=ALU.add))
                            if H > SP:
                                op2 = (e.tensor_copy(
                                    out=ssb[:, SP * 64:H * 64],
                                    in_=ps[3][0:64, 0:(H - SP) * 64])
                                    if first else
                                    e.tensor_tensor(
                                        out=ssb[:, SP * 64:H * 64],
                                        in0=ssb[:, SP * 64:H * 64],
                                        in1=ps[3][0:64, 0:(H - SP) * 64],
                                        op=ALU.add))
                                ch(op2, (p, "sacc", L, s_))
                            else:
                                M.inc(e.nop(), "cv", (p, "sacc", L, s_))
                for t in range(NTT):
                    for cc in range(NCC):
                        M.wait(e, (p, "projev", L, t, cc))
                        dst = yx[:, cc * T + t * TT:cc * T + (t + 1) * TT]
                        ch(e.tensor_tensor(
                            out=dst, in0=dst,
                            in1=tmpa_f[:, (cc % 2) * TT:(cc % 2 + 1) * TT],
                            op=ALU.add), (p, "xadd", L, t, cc))
            CH = min(T, (NCC * TT) // 2)
            for cc in range(NCC):
                for j in range(T // CH):
                    M.wait(e, (p, "x1ld", cc, j))
                    dst = yx[:, cc * T + j * CH:cc * T + (j + 1) * CH]
                    ch(e.tensor_tensor(out=dst, in0=dst, in1=xa_f[:, 0:CH],
                                       op=ALU.add), (p, "xaadd", cc, j))
            return

    def setup_phase(ename, e, M, p):
        if ename == "sync":
            M.inc(e.dma_start(out=eyes[:], in_=eyeh[:]), "dx", "eyes", 16)
        elif ename == "gpsimd":
            M.inc(e.dma_start(out=eyeb[:], in_=eyeh[:]), "dwc", None, 16)
            M.inc(e.dma_start(out=wgs[:], in_=wgh[:]), "dwc", None, 16)
            M.inc(e.dma_start(out=b2s[:], in_=b2h[:]), "dwc", None, 16)
            M.inc(e.dma_start(out=ohbs[:], in_=ohbh[:]), "dwc", "const_w", 16)
            M.inc(e.memset(onec[:], 1.0), "cg", None)
            M.inc(e.memset(oner[:], 1.0), "cg", None)
            M.inc(e.memset(sbf2[:], 0.0), "cg", "memsets")
            if SPARSE:
                M.inc(e.dma_start(out=ltc[:], in_=ltc_h[:]), "dwc", None, 16)
                M.inc(e.dma_start(out=selm[:], in_=selm_h[:]), "dwc", None,
                      16)
                M.inc(e.dma_start(out=segc[:], in_=segc_h[:]), "dwc", None,
                      16)
                M.inc(e.dma_start(out=tidh[:], in_=tidh_h[:]), "dwc", None,
                      16)
                M.inc(e.dma_start(out=pfh[:], in_=pfh_h[:]), "dwc",
                      "sparse_consts", 16)
                M.inc(e.memset(c64r[:], 64.0), "cg", None)
                M.inc(e.memset(crow[:], 0.0), "cg", None)
                M.inc(e.memset(zrow0[:], 0.0), "cg", "memsets2")
                M.wait(e, "memsets2")
                M.inc(e.dma_start(out=zt_s[T:T + 1, :], in_=zrow0[:]),
                      "dws", "ztzero", 16)
        elif ename == "vector" and SPARSE:
            M.wait(e, "sparse_consts")
            c_ = M.inc(e.tensor_copy(out=dmpc[:, 0:1], in_=tidh[:, 0:1]),
                       "cv")
            M.wait_cnt(e, "cv", c_)
            M.inc(e.tensor_scalar(out=dmpc[:, 0:1], in0=dmpc[:, 0:1],
                                  scalar1=float(NI), scalar2=None,
                                  op0=ALU.add), "cv", "dmpc")


    def make_emitter(ename):
        def f(e, M):
            for pid, ph in enumerate(phases):
                if pid >= 1:
                    for o_ in ENGS:
                        if o_ != ename:
                            M.wait(e, ("pd", pid - 1, o_))
                if ph[0] == "setup":
                    setup_phase(ename, e, M, pid)
                elif ph[0] == "ffn":
                    ffn_phase(ename, e, M, pid, ph[1], ph[2])
                else:
                    attn_phase(ename, e, M, pid, ph[1])
                if ename == "sync":
                    M.wait_cnt(e, "dx", M.total("dx"))
                    M.wait_cnt(e, "st", M.total("st"))
                    M.wait_cnt(e, "sz0", M.total("sz0"))
                    M.wait_cnt(e, "sz1", M.total("sz1"))
                elif ename == "gpsimd":
                    for s_ in ("dw0", "dw1", "dwa", "dwb", "dwc", "dws",
                               "dwy", "ds0", "ds1", "dg0", "dg1",
                               "dy0", "dy1", "dy2", "dy3"):
                        M.wait_cnt(e, s_, M.total(s_))
                else:
                    if ename == "scalar":
                        M.wait_cnt(e, "dze0", M.total("dze0"))
                        M.wait_cnt(e, "dze1", M.total("dze1"))
                    M.wait_cnt(e, OWN[ename], M.total(OWN[ename]))
                M.inc(e.nop(), OWN[ename], ("pd", pid, ename))
            if ename == "sync":
                M.wait_cnt(e, "st", M.total("st"))
        return f

    emitters = [(n, make_emitter(n)) for n in ENGS]
    M = Plan()
    M.real = False
    M.counts = {}
    fake = FakeEng()
    for n, f in emitters:
        f(fake, M)
    M.real = True
    M.counts = {}
    for s in SEMS:
        M.sems[s] = es.enter_context(nc.semaphore(s))
    block = es.enter_context(nc.Block())
    for n, f in emitters:
        getattr(block, n)(lambda eng, f=f: f(eng, M))
    return nc


def _host_prep(cfg, L0_attn_inputs_embeds, L0_attn_Wqkv, L0_attn_bqkv,
               L0_attn_Wproj, L0_attn_bproj, L0_ffn_Wg, L0_ffn_W1, L0_ffn_b1,
               L0_ffn_W2, L0_ffn_b2):
    B, T, C, NL, E = cfg["B"], cfg["T"], cfg["C"], cfg["NL"], cfg["E"]
    FE, TT, FV = cfg["FE"], cfg["TT"], cfg["FV"]
    NCC, NTT, NCFE, NVQ, NFV = (cfg["NCC"], cfg["NTT"], cfg["NCFE"],
                                cfg["NVQ"], cfg["NFV"])
    f32 = np.float32
    ac = np.ascontiguousarray

    x = np.asarray(L0_attn_inputs_embeds, dtype=f32)
    Bfull = x.shape[0]
    xt = x.transpose(0, 2, 1)
    x_in = ac(xt.reshape(Bfull, NCC, 128, NTT, TT)
              .transpose(0, 3, 2, 1, 4)
              .reshape(Bfull, NTT, 128, NCC * TT))

    Wqkv = np.asarray(L0_attn_Wqkv, dtype=f32)
    WqkvT = Wqkv.transpose(0, 2, 1)
    wqh = ac(WqkvT[:, :, :C].reshape(NL, NCC, 128, C)
             .transpose(0, 2, 1, 3).reshape(NL, 128, NCC * C))
    wkvh = ac(WqkvT[:, :, C:].reshape(NL, NCC, 128, 2 * C)
              .transpose(0, 2, 1, 3).reshape(NL, 128, NCC * 2 * C))
    bqkv = np.asarray(L0_attn_bqkv, dtype=f32)
    bqqh = ac(bqkv[:, :C].reshape(NL, NCC, 128).transpose(0, 2, 1))
    bkvh = ac(bqkv[:, C:].reshape(NL, 1, 2 * C))
    Wproj = np.asarray(L0_attn_Wproj, dtype=f32)
    wprojh = ac(Wproj.transpose(0, 2, 1).reshape(NL, NCC, 128, C)
                .transpose(0, 2, 1, 3).reshape(NL, 128, NCC * C))
    bpjh = ac(np.asarray(L0_attn_bproj, dtype=f32)
              .reshape(NL, NCC, 128).transpose(0, 2, 1))

    Wg = np.asarray(L0_ffn_Wg, dtype=f32)
    wgh = ac(Wg.reshape(NCC, 128, E).transpose(1, 0, 2).reshape(128, NCC * E))
    W1 = np.asarray(L0_ffn_W1, dtype=f32)
    w1h = ac(W1.reshape(E, NCC, 128, NVQ, FV)
             .transpose(0, 3, 2, 1, 4).reshape(E, NVQ, 128, NCC * FV))
    b1h = ac(np.asarray(L0_ffn_b1, dtype=f32)
             .reshape(E, NCFE, 128).transpose(0, 2, 1))
    W2 = np.asarray(L0_ffn_W2, dtype=f32)
    w2h = ac(W2.reshape(E, NVQ, NFV, 128, C)
             .transpose(0, 1, 3, 2, 4).reshape(E, NVQ, 128, NFV * C))
    b2h = ac(np.asarray(L0_ffn_b2, dtype=f32))
    eyeh = np.eye(128, dtype=f32)
    ohbh = ac(np.kron(np.eye(E, dtype=f32), np.ones((1, 128), f32))
              .reshape(E, E * 128))
    sparse_inputs = {}
    if cfg.get("sparse"):
        CAP, NI, NCHK = cfg["CAP"], cfg["NI"], cfg["NCHK"]
        ltc = np.tril(np.ones((128, 128), f32), -1)
        ltc = ac(ltc.T * 0 + np.triu(np.ones((128, 128), f32), 1))
        selm = np.zeros((128, 128), f32)
        for pp in range(NCHK * E):
            for p2 in range(NCHK * E):
                if pp % E == p2 % E and pp // E < p2 // E:
                    selm[pp, p2] = 1.0
        segc = np.zeros((128, 1), f32)
        segc[:NCHK * E, 0] = (np.arange(NCHK * E) % E) * CAP
        tidh = ac((np.arange(NCHK)[None, :] * 128 +
                   np.arange(128)[:, None]).astype(np.int32))
        pf = np.zeros((NI, 2), np.int32)
        pf[:, 0] = cfg["T"]
        pf[:, 1] = 0
        pfh = ac(pf.reshape(128, NI * 2 // 128))
        sparse_inputs = dict(ltc_h=ltc, selm_h=selm, segc_h=segc,
                             tidh_h=tidh, pfh_h=pfh)

    shared = dict(wqh=wqh, wkvh=wkvh, bqqh=bqqh, bkvh=bkvh, wprojh=wprojh,
                  bpjh=bpjh, wgh=wgh, w1h=w1h, b1h=b1h, w2h=w2h, b2h=b2h,
                  eyeh=eyeh, ohbh=ohbh, **sparse_inputs)
    in_maps = []
    for c in range(cfg["n_cores"]):
        m = dict(shared)
        m["x_in"] = ac(x_in[c * B:(c + 1) * B])
        in_maps.append(m)
    return in_maps


def _host_post(cfg, results):
    B, T, C, NCC = cfg["B"], cfg["T"], cfg["C"], cfg["NCC"]
    outs = []
    for r in results:
        o = r["out"]
        o = o.reshape(B, 128, NCC, T).transpose(0, 2, 1, 3) \
             .reshape(B, C, T).transpose(0, 2, 1)
        outs.append(o)
    return np.ascontiguousarray(np.concatenate(outs, axis=0)).astype(
        np.float32)


def kernel(L0_attn_inputs_embeds, L0_attn_Wqkv, L0_attn_bqkv, L0_attn_Wproj,
           L0_attn_bproj, L0_ffn_Wg, L0_ffn_W1, L0_ffn_b1, L0_ffn_W2,
           L0_ffn_b2):
    cfg = make_cfg()
    in_maps = _host_prep(cfg, L0_attn_inputs_embeds, L0_attn_Wqkv,
                         L0_attn_bqkv, L0_attn_Wproj, L0_attn_bproj,
                         L0_ffn_Wg, L0_ffn_W1, L0_ffn_b1, L0_ffn_W2,
                         L0_ffn_b2)
    with contextlib.ExitStack() as es:
        nc = build_nc(cfg, es)
    res = run_bass_kernel_spmd(nc, in_maps, core_ids=list(range(
        cfg["n_cores"])))
    return _host_post(cfg, res.results)


# version 16
# speedup vs baseline: 1.0254x; 1.0254x over previous
import contextlib
import numpy as np
import concourse.bass as bass
import concourse.mybir as mybir
from concourse.bass_utils import run_bass_kernel_spmd

F32 = mybir.dt.float32
BF16 = mybir.dt.bfloat16
AF = mybir.ActivationFunctionType
ALU = mybir.AluOpType
AX = mybir.AxisListType


class _FakeInstr:
    __slots__ = ()

    def then_inc(self, sem, k=1):
        return self


class FakeEng:
    def __getattr__(self, name):
        def _f(*a, **k):
            return _FakeInstr()
        return _f


class Plan:
    def __init__(self):
        self.counts = {}
        self.marks = {}
        self.real = False
        self.sems = {}

    def inc(self, instr, semname, key=None, k=1):
        c = self.counts.get(semname, 0) + k
        self.counts[semname] = c
        if key is not None:
            if not self.real:
                assert key not in self.marks, f"dup milestone {key}"
                self.marks[key] = (semname, c)
            else:
                assert self.marks[key] == (semname, c), \
                    f"pass mismatch {key}: {self.marks[key]} vs {(semname, c)}"
        if self.real:
            instr.then_inc(self.sems[semname], k)
        return c

    def wait(self, eng, key):
        if self.real:
            semname, c = self.marks[key]
            eng.wait_ge(self.sems[semname], c)

    def wait_cnt(self, eng, semname, c):
        if self.real:
            eng.wait_ge(self.sems[semname], c)

    def total(self, semname):
        return self.counts.get(semname, 0)


SEMS = ["dx", "st", "dw0", "dw1", "dwa", "dwb", "dwc", "cg",
        "ct", "cs", "cv", "by", "dws", "dwy", "sz0", "sz1", "ds0", "ds1",
        "dg0", "dg1", "dy0", "dy1", "dy2", "dy3", "dze0", "dze1"]
ENGS = ["sync", "gpsimd", "tensor", "scalar", "vector"]
OWN = {"sync": "by", "gpsimd": "cg", "tensor": "ct", "scalar": "cs",
       "vector": "cv"}


def make_cfg(B=2, T=2048, C=1024, NL=12, H=16, E=8, FE=4096, TT=512, FV=512,
             n_cores=8):
    DH = 64
    assert C == H * DH and C % 128 == 0 and H % 2 == 0
    assert T % TT == 0 and TT % 128 == 0 and FE % FV == 0 and FV % 128 == 0
    assert (2 * C) % 512 == 0
    cfg = dict(B=B, T=T, C=C, NL=NL, H=H, DH=DH, E=E, FE=FE, TT=TT, FV=FV,
               n_cores=n_cores,
               NCC=C // 128, NTT=T // TT, NSUB=TT // 128, NCFE=FE // 128,
               NVQ=FE // FV, NFV=FV // 128, NKVG=(2 * C) // 512)
    cfg["NVE"] = E * cfg["NVQ"]
    cfg["sparse"] = True
    cfg["CAP"] = 1024
    assert cfg["CAP"] % TT == 0
    cfg["NST"] = cfg["CAP"] // TT
    ppe, nvq = (cfg["CAP"] // TT) * cfg["NSUB"], cfg["NVQ"]
    assert ppe % nvq == 0 and ppe // nvq <= 2
    cfg["NI"] = E * cfg["CAP"]
    cfg["NCHK"] = T // 128
    assert cfg["NCHK"] * E <= 128 and (cfg["NI"] * 2) % 128 == 0
    return cfg


def build_nc(cfg, es):
    B, T, C, NL, H, DH, E = (cfg[k] for k in
                             ("B", "T", "C", "NL", "H", "DH", "E"))
    FE, TT, FV = cfg["FE"], cfg["TT"], cfg["FV"]
    NCC, NTT, NSUB, NCFE = cfg["NCC"], cfg["NTT"], cfg["NSUB"], cfg["NCFE"]
    NVQ, NVE, NFV, NKVG = cfg["NVQ"], cfg["NVE"], cfg["NFV"], cfg["NKVG"]
    SP = min(8, H)
    SPARSE = cfg.get("sparse", False)
    CAP, NST, NI, NCHK = cfg["CAP"], cfg["NST"], cfg["NI"], cfg["NCHK"]
    ZROW = T
    BIGP = 1.0e5
    EPS_LN, EPS_GATE = 1e-5, 1e-9

    nc = bass.Bass("TRN2", target_bir_lowering=False, debug=False)
    dt = nc.dram_tensor
    x_in = dt("x_in", [B, NTT, 128, NCC * TT], F32, kind="ExternalInput").ap()
    wqh_d = dt("wqh", [NL, 128, NCC * C], F32, kind="ExternalInput").ap()
    wkvh_d = dt("wkvh", [NL, 128, NCC * 2 * C], F32,
                kind="ExternalInput").ap()
    bqqh = dt("bqqh", [NL, 128, NCC], F32, kind="ExternalInput").ap()
    bkvh = dt("bkvh", [NL, 1, 2 * C], F32, kind="ExternalInput").ap()
    wprojh = dt("wprojh", [NL, 128, NCC * C], F32, kind="ExternalInput").ap()
    bpjh = dt("bpjh", [NL, 128, NCC], F32, kind="ExternalInput").ap()
    wgh = dt("wgh", [128, NCC * E], F32, kind="ExternalInput").ap()
    w1h = dt("w1h", [E, NVQ, 128, NCC * FV], F32, kind="ExternalInput").ap()
    b1h = dt("b1h", [E, 128, NCFE], F32, kind="ExternalInput").ap()
    w2h = dt("w2h", [E, NVQ, 128, NFV * C], F32, kind="ExternalInput").ap()
    b2h = dt("b2h", [E, C], F32, kind="ExternalInput").ap()
    eyeh = dt("eyeh", [128, 128], F32, kind="ExternalInput").ap()
    ohbh = dt("ohbh", [E, E * 128], F32, kind="ExternalInput").ap()
    out_d = dt("out", [B, 128, NCC * T], F32, kind="ExternalOutput").ap()
    x1s = dt("x1s", [B, 128, NCC * T], F32, kind="Internal").ap()
    if SPARSE:
        ltc_h = dt("ltc_h", [128, 128], F32, kind="ExternalInput").ap()
        selm_h = dt("selm_h", [128, 128], F32, kind="ExternalInput").ap()
        segc_h = dt("segc_h", [128, 1], F32, kind="ExternalInput").ap()
        tidh_h = dt("tidh_h", [128, NCHK], mybir.dt.int32,
                    kind="ExternalInput").ap()
        pfh_h = dt("pfh_h", [128, NI * 2 // 128], mybir.dt.int32,
                   kind="ExternalInput").ap()
        zt_s = dt("zt_s", [T + 1, C], BF16, kind="Internal").ap()
        yinst_s = dt("yinst_s", [NI, C], F32, kind="Internal").ap()
        idxpw_s = dt("idxpw_s", [NI + 128, 2], mybir.dt.int32,
                     kind="Internal").ap()
        zc_s = dt("zc_s", [E, NST, 128, NSUB * NCC * 128], BF16,
                  kind="Internal").ap()
    dbg = (dt("dbg", [B, 128, NCC * T], F32, kind="ExternalOutput").ap()
           if cfg.get("debug") else None)

    yx = es.enter_context(nc.sbuf_tensor("yx", [128, NCC * T], F32))
    eyes = es.enter_context(nc.sbuf_tensor("eyes", [128, 128], F32))
    eyeb = es.enter_context(nc.sbuf_tensor("eyeb", [128, 128], BF16))
    ohbs = es.enter_context(nc.sbuf_tensor("ohbs", [E, E * 128], BF16))
    b2s = es.enter_context(nc.sbuf_tensor("b2s", [E, C], BF16))
    wgs = es.enter_context(nc.sbuf_tensor("wgs", [128, NCC * E], BF16))
    onec = es.enter_context(nc.sbuf_tensor("onec", [128, 1], BF16))
    oner = es.enter_context(nc.sbuf_tensor("oner", [1, 128], BF16))
    ssb = es.enter_context(nc.sbuf_tensor("ssb", [64, H * DH], F32))
    sbf2 = es.enter_context(nc.sbuf_tensor("sbf2", [128, NCC * 128], BF16))
    pwtb = es.enter_context(nc.sbuf_tensor("pwtb", [E, T], BF16))
    bkvs = es.enter_context(nc.sbuf_tensor("bkvs", [1, 2 * 2 * C], BF16))
    bqs = es.enter_context(nc.sbuf_tensor("bqs", [128, 2 * NCC], F32))
    bps = es.enter_context(nc.sbuf_tensor("bps", [128, 2 * NCC], F32))
    b1s = es.enter_context(nc.sbuf_tensor("b1s", [128, 2 * NCFE], F32))
    pfull = es.enter_context(nc.sbuf_tensor("pfull", [128, E], F32))
    ppm = es.enter_context(nc.sbuf_tensor("ppm", [128, 1], F32))
    ppex = es.enter_context(nc.sbuf_tensor("ppex", [128, E], F32))
    ppa = es.enter_context(nc.sbuf_tensor("ppa", [128, E], F32))
    ppb = es.enter_context(nc.sbuf_tensor("ppb", [128, E], F32))
    ppr = es.enter_context(nc.sbuf_tensor("ppr", [128, 1], F32))
    I32 = mybir.dt.int32
    if SPARSE:
        ltc = es.enter_context(nc.sbuf_tensor("ltc", [128, 128], BF16))
        selm = es.enter_context(nc.sbuf_tensor("selm", [128, 128], BF16))
        segc = es.enter_context(nc.sbuf_tensor("segc", [128, 1], F32))
        tidh = es.enter_context(nc.sbuf_tensor("tidh", [128, NCHK], I32))
        pfh = es.enter_context(nc.sbuf_tensor("pfh", [128, NI * 2 // 128],
                                              I32))
        c64r = es.enter_context(nc.sbuf_tensor("c64r", [1, 128], BF16))
        zrow0 = es.enter_context(nc.sbuf_tensor("zrow0", [1, C], BF16))
        maskb = es.enter_context(nc.sbuf_tensor("maskb", [128, NCHK * E],
                                                BF16))
        maskf = es.enter_context(nc.sbuf_tensor("maskf", [128, NCHK * E],
                                                F32))
        eqf = es.enter_context(nc.sbuf_tensor("eqf", [128, NCHK * E], F32))
        pwm = es.enter_context(nc.sbuf_tensor("pwm", [128, NCHK * E], F32))
        crow = es.enter_context(nc.sbuf_tensor("crow", [1, 128], F32))
        ccb = es.enter_context(nc.sbuf_tensor("ccb", [128, 1], BF16))
        cbt = es.enter_context(nc.sbuf_tensor("cbt", [128, 3], F32))
        qsr = es.enter_context(nc.sbuf_tensor("qsr", [1, 2 * 128], BF16))
        posb = es.enter_context(nc.sbuf_tensor("posb", [128, 2 * NCHK], I32))
        sctg = es.enter_context(nc.sbuf_tensor("sctg", [128, 2 * 2], I32))
        pp1 = es.enter_context(nc.sbuf_tensor("pp1", [128, E], F32))
        pp2 = es.enter_context(nc.sbuf_tensor("pp2", [128, E], F32))
        pp3 = es.enter_context(nc.sbuf_tensor("pp3", [128, 1], F32))
        idxt = es.enter_context(nc.sbuf_tensor("idxt", [128, 2 * 2], I32))
        pwsb = es.enter_context(nc.sbuf_tensor("pwsb", [128, E * NST * NSUB],
                                               F32))
        dmpc = es.enter_context(nc.sbuf_tensor("dmpc", [128, 1], F32))
        ppA = es.enter_context(nc.sbuf_tensor("ppA", [128, NCHK * E], F32))
        ppB = es.enter_context(nc.sbuf_tensor("ppB", [128, NCHK * E], F32))
        pw12 = es.enter_context(nc.sbuf_tensor("pw12", [128, 2 * NCHK], F32))
        sctgA = es.enter_context(nc.sbuf_tensor("sctgA", [128, 4 * NCHK],
                                                I32))

    W1C, W2C = NCC * FV, NFV * C
    ZHC, HSC = NCC * T, NFV * TT
    XSC = 2 * NCC * TT
    KVC, QTC, YTC = 2 * 2 * C, NCC * TT, NCC * TT
    ROWC, RBFC = 2 * 4 * TT, 2 * TT
    WKVC, WQC, WPC, XBC = NCC * 2 * C, NCC * C, NCC * C, NCC * TT
    YAC = 2 * NCC * CAP if SPARSE else 0
    ffn_cols = ZHC + 2 * (W1C + W2C) + YAC + HSC + 2 * TT + 2 * 2 * TT
    attn_cols = WKVC + WQC + WPC + 2 * XBC + KVC + QTC + YTC + 2 * 2 * TT
    ffn_ovl = XSC + ROWC + RBFC + XBC + XBC
    if SPARSE:
        ffn_ovl += 2 * C + 2 * (2 * C)
    attn_ovl = ROWC + RBFC
    ACOLS = max(ffn_cols, attn_cols)
    ffn_ovl_fits = ffn_ovl <= 2 * (W1C + W2C) + YAC
    attn_ovl_fits = attn_ovl <= WKVC
    extra = (0 if ffn_ovl_fits else ffn_ovl) + (0 if attn_ovl_fits else
                                                attn_ovl)
    ACOLS += extra
    arena = es.enter_context(nc.sbuf_tensor("arena", [128, ACOLS], BF16))

    def carve(off, cols):
        assert off + cols <= ACOLS
        return arena[:, off:off + cols], off + cols

    from concourse.ap import AP as _AP

    def sub_ap(slice_ap, extra_off, dims):
        return _AP(tensor=slice_ap.tensor,
                   offset=slice_ap.offset + extra_off,
                   ap=[[int(slice_ap.ap[0][0]), int(slice_ap.ap[0][1])]] +
                      [[int(s), int(n)] for s, n in dims])

    o = 0
    zh, o = carve(o, ZHC)
    wdb_off = o
    w1b, w2b = [], []
    for i in range(2):
        a, o = carve(o, W1C); w1b.append(a)
        a, o = carve(o, W2C); w2b.append(a)
    if SPARSE:
        a, o = carve(o, YAC); yacc_f = a.bitcast(F32)
    hsb, o = carve(o, HSC)
    a, o = carve(o, 2 * TT); tmpv_f = a.bitcast(F32)
    a, o = carve(o, 4 * TT); pbs_f = a.bitcast(F32)
    oo = wdb_off if ffn_ovl_fits else max(ffn_cols, attn_cols)
    a, oo = carve(oo, XSC); xstr_f = a.bitcast(F32)
    a, oo = carve(oo, ROWC); rows_ff = a[0:1, :].bitcast(F32)
    rbf_ff, oo = carve(oo, RBFC); rbf_ff = rbf_ff[0:1, :]
    xbb_ff, oo = carve(oo, XBC)
    xqq_ff, oo = carve(oo, XBC)

    if SPARSE:
        NSTl = cfg["NST"]
        zneed = NSTl * NCC * TT + 2 * (2 * C) + 2 * C
        if zneed <= ZHC:
            zo = 0
        else:
            zo = ACOLS
            arena2 = es.enter_context(
                nc.sbuf_tensor("arena2pad", [128, zneed + 2 * (2 * C)], BF16))
            def carve2(off, cols, _base=zo):
                return arena2[:, off - _base:off - _base + cols], off + cols
        _carve = carve if zneed <= ZHC else carve2
        zeS = []
        for i in range(NSTl):
            a, zo = _carve(zo, NCC * TT); zeS.append(a)
        ytr = []
        for i in range(2):
            a, zo = _carve(zo, 2 * C); ytr.append(a.bitcast(F32))
        zgk = []
        for i in range(2):
            a, zo = _carve(zo, C); zgk.append(a)
        if zneed <= ZHC and ZHC - zneed >= 2 * C and C <= 2 * TT:
            a, zo = _carve(zo, 2 * C); ytr.append(a.bitcast(F32))
            ytr.append(pbs_f[:, 0:C])
        else:
            a, zo = _carve(zo, 2 * C); ytr.append(a.bitcast(F32))
            a, zo = _carve(zo, 2 * C); ytr.append(a.bitcast(F32))
        ztma = []
        for ii in range(2):
            a2, oo = carve(oo, C); ztma.append(a2)
        ybk = []
        for i in range(2):
            a, oo = carve(oo, 2 * C); ybk.append(a.bitcast(F32))
        assert oo <= wdb_off + 2 * (W1C + W2C) + YAC or not ffn_ovl_fits

    o = 0
    wkv, o = carve(o, WKVC)
    wqv, o = carve(o, WQC)
    wpv, o = carve(o, WPC)
    xbA = []
    for i in range(2):
        a, o = carve(o, XBC); xbA.append(a)
    kvt, o = carve(o, KVC)
    qTt, o = carve(o, QTC)
    yTt, o = carve(o, YTC)
    a, o = carve(o, 2 * 2 * TT); tmpa_f = a.bitcast(F32)
    oo = 0 if attn_ovl_fits else \
        (max(ffn_cols, attn_cols) + (0 if ffn_ovl_fits else ffn_ovl))
    a, oo = carve(oo, ROWC); rows_at = a[0:1, :].bitcast(F32)
    rbf_at, oo = carve(oo, RBFC); rbf_at = rbf_at[0:1, :]
    xa_f = xbA[0].bitcast(F32)
    assert xa_f.shape[1] >= T // 2 or True

    ps = [es.enter_context(nc.psum_tensor(f"ps{i}", [128, 512], F32))
          for i in range(8)]
    HB = [6, 7, 2]
    HBS = [6, 7, 3]
    YB = [0, 1, 3]
    YBS = [0, 1]
    KB = [0, 1, 4, 5]
    QB = [4, 5, 0]
    TB = [6, 7, 1]
    PB = [4, 5, 0]

    R_NEGMU, R_SCR, R_RSTD, R_BROW = 0, 1, 2, 3

    def row(view, i, n=1):
        return view[0:1, i * TT:(i + n) * TT]

    NST_ = cfg.get("NST", 1)

    def gidx(k):
        e2, r = divmod(k, NST_ * NSUB)
        t2, s2 = divmod(r, NSUB)
        return (e2, t2, s2)

    def psl(base, qi):
        PPE_ = NST_ * NSUB
        return range(base + qi * PPE_ // NVQ, base + (qi + 1) * PPE_ // NVQ)

    phases = [("setup",)]
    for b in range(B):
        phases += [("ffn", b, 0), ("attn", b), ("ffn", b, 1)]
    if cfg.get("only_ffn1"):
        phases = [("setup",)] + [("ffn", b, 0) for b in range(B)]

    def ln_tile(ename, e, M, p, tag, t, xsrc, zdst, rowsv, rbfv, castb, sqb,
                streamed, tmpb, z_gate=None, s1_gate=None, bc_gate=None):
        K = lambda name: (p, tag + name, t)
        if ename == "tensor":
            M.wait(e, K("xcast"))
            if s1_gate is not None:
                M.wait(e, s1_gate)
            if t >= 1:
                M.wait(e, (p, tag + "statsread", t - 1))
            for cc in range(NCC):
                mm = e.matmul(ps[0][0:1, 0:TT], onec[:],
                              castb[:, cc * TT:(cc + 1) * TT],
                              start=(cc == 0), stop=(cc == NCC - 1))
            M.inc(mm, "ct", K("s1"))
            M.wait(e, K("xsq"))
            for cc in range(NCC):
                mm = e.matmul(ps[1][0:1, 0:TT], onec[:],
                              sqb[:, cc * TT:(cc + 1) * TT],
                              start=(cc == 0), stop=(cc == NCC - 1))
            M.inc(mm, "ct", K("s2"))
            M.wait(e, K("rbf"))
            if t >= 1:
                M.wait(e, (p, tag + "ztile", t - 1))
            if bc_gate is not None:
                M.wait(e, bc_gate)
            e.matmul(ps[2][:, 0:TT], oner[:], rbfv[0:1, 0:TT],
                     start=True, stop=True)
            M.inc(e.matmul(ps[3][:, 0:TT], oner[:], rbfv[0:1, TT:2 * TT],
                           start=True, stop=True), "ct", K("bc"))
        elif ename == "scalar":
            if streamed:
                M.wait(e, K("xld"))
            if t >= 1:
                M.wait(e, (p, tag + "s1", t - 1))
            for cc in range(NCC):
                M.inc(e.copy(out=castb[:, cc * TT:(cc + 1) * TT],
                             in_=xsrc(cc)), "cs",
                      K("xcast") if cc == NCC - 1 else None)
            M.wait(e, K("var"))
            c = M.inc(e.activation(out=row(rowsv, R_RSTD),
                                   in_=row(rowsv, R_SCR), func=AF.Ln), "cs")
            M.wait_cnt(e, "cs", c)
            M.inc(e.activation(out=row(rowsv, R_RSTD), in_=row(rowsv, R_RSTD),
                               func=AF.Exp, scale=-0.5), "cs", K("rstd"))
            M.wait(e, K("brow"))
            M.inc(e.copy(out=rbfv[0:1, 0:2 * TT], in_=row(rowsv, R_RSTD, 2)),
                  "cs", K("rbf"))
        elif ename == "gpsimd":
            if streamed:
                M.wait(e, K("xld"))
            if t >= 1:
                M.wait(e, (p, tag + "s2", t - 1))
            for cc in range(NCC):
                M.inc(e.tensor_tensor(out=sqb[:, cc * TT:(cc + 1) * TT],
                                      in0=xsrc(cc), in1=xsrc(cc),
                                      op=ALU.mult), "cg",
                      K("xsq") if cc == NCC - 1 else None)
        elif ename == "vector":
            def ch(op, key=None):
                c = M.inc(op, "cv", key)
                M.wait_cnt(e, "cv", c)
            M.wait(e, K("s1"))
            M.wait(e, K("s2"))
            ch(e.tensor_scalar(out=row(rowsv, R_NEGMU), in0=ps[0][0:1, 0:TT],
                               scalar1=-1.0 / C, scalar2=None, op0=ALU.mult))
            ch(e.tensor_scalar(out=row(rowsv, R_SCR), in0=ps[1][0:1, 0:TT],
                               scalar1=1.0 / C, scalar2=EPS_LN,
                               op0=ALU.mult, op1=ALU.add),
               (p, tag + "statsread", t))
            ch(e.tensor_tensor(out=row(rowsv, R_RSTD),
                               in0=row(rowsv, R_NEGMU),
                               in1=row(rowsv, R_NEGMU), op=ALU.mult))
            ch(e.tensor_tensor(out=row(rowsv, R_SCR), in0=row(rowsv, R_SCR),
                               in1=row(rowsv, R_RSTD), op=ALU.subtract),
               K("var"))
            M.wait(e, K("rstd"))
            ch(e.tensor_tensor(out=row(rowsv, R_BROW),
                               in0=row(rowsv, R_NEGMU),
                               in1=row(rowsv, R_RSTD), op=ALU.mult),
               K("brow"))
            M.wait(e, K("bc"))
            if z_gate is not None:
                M.wait(e, z_gate)
            for cc in range(NCC):
                ch(e.tensor_tensor(out=tmpb, in0=xsrc(cc),
                                   in1=ps[2][:, 0:TT], op=ALU.mult),
                   K("xfree") if (cc == NCC - 1 and streamed) else None)
                ch(e.tensor_tensor(out=zdst(cc), in0=tmpb,
                                   in1=ps[3][:, 0:TT], op=ALU.add),
                   K("ztile") if cc == NCC - 1 else None)

    def ffn_phase(ename, e, M, p, b, call):
        from_yx = (call == 1)
        NG = NTT * NSUB

        def xsrc(t):
            def f(cc):
                if from_yx:
                    return yx[:, cc * T + t * TT:cc * T + (t + 1) * TT]
                return xstr_f[:, cc * TT:(cc + 1) * TT]
            return f

        def zdst(t):
            def f(cc):
                return zh[:, cc * T + t * TT:cc * T + (t + 1) * TT]
            return f

        def lng(t):
            kw = {}
            if SPARSE:
                if t >= 1:
                    kw["bc_gate"] = (p, "ztmev", t * NSUB - 1)
            return kw

        NTI = NST if SPARSE else NTT

        if ename == "sync":
            if from_yx and dbg is not None:
                M.inc(e.dma_start(out=dbg[b], in_=yx[:]), "st",
                      (p, "dbgst"), 16)
            for t in range(NTT):
                if not from_yx:
                    if t >= 1:
                        M.wait(e, (p, "Lxfree", t - 1))
                    M.inc(e.dma_start(out=xstr_f[:, :], in_=x_in[b, t]),
                          "dx", (p, "Lxld", t), 16)
                if SPARSE:
                    for s_ in range(NSUB):
                        zi = t * NSUB + s_
                        M.wait(e, (p, "ztmev", zi))
                        M.inc(e.dma_start(
                            out=zt_s[zi * 128:(zi + 1) * 128, :],
                            in_=ztma[zi % 2][:, :]),
                            f"sz{zi % 2}", (p, "ztst", zi), 16)
            if not from_yx:
                for t in range(NTT):
                    if t == 0:
                        M.wait(e, (p, "Lxfree", NTT - 1))
                    else:
                        M.wait(e, (p, "yi", (t - 1) * NCC + NCC - 1))
                    M.inc(e.dma_start(out=xstr_f[:, :], in_=x_in[b, t]),
                          "dx", (p, "xld2", t), 16)
            if SPARSE:
                PPE = NST * NSUB
                def emit_zbst(kblock):
                    for k in kblock:
                        ei2, t2, sub2 = gidx(k)
                        M.wait(e, (p, "zbev_v", k))
                        M.inc(e.dma_start(
                            out=zc_s[ei2, t2][:, sub2 * NCC * 128:
                                              (sub2 + 1) * NCC * 128],
                            in_=zgk[k % 2][:, 0:NCC * 128]),
                            f"sz{k % 2}", (p, "zbst", ei2, t2, sub2), 16)
                emit_zbst(range(0, 2 * PPE))
                for i in range(NVE):
                    ei, qi = i // NVQ, i % NVQ
                    if ei + 2 < E:
                        emit_zbst(psl((ei + 2) * PPE, qi))
            if call == 1:
                M.wait(e, (p, "ffn_done"))
                M.inc(e.dma_start(out=out_d[b], in_=yx[:]), "st",
                      (p, "outst"), 16)
            return

        if ename == "gpsimd":
            for t in range(NTT):
                ln_tile("gpsimd", e, M, p, "L", t, xsrc(t), zdst(t),
                        rows_ff, rbf_ff, xbb_ff, xqq_ff, not from_yx,
                        tmpv_f[:, 0:TT], **lng(t))
            if SPARSE:
                M.inc(e.dma_start(out=idxpw_s[0:NI, :].rearrange(
                    "a b -> (a b)").rearrange("(x y) -> x y", x=128),
                    in_=pfh[:]), "dws", (p, "pfil"), 16)
                M.wait(e, (p, "pfil"))
                nds = [0, 0]
                for half2 in range(2):
                    M.wait(e, (p, "sct2", half2))
                    for c in range(NCHK):
                        s2 = half2 * NCHK + c
                        if s2 >= 1:
                            M.wait_cnt(e, "ds0", 16 * nds[0])
                            M.wait_cnt(e, "ds1", 16 * nds[1])
                        cds = M.inc(e.indirect_dma_start(
                            out=idxpw_s[:, :],
                            out_offset=bass.IndirectOffsetOnAxis(
                                posb[:, s2:s2 + 1], 0),
                            in_=sctgA[:, s2 * 2:s2 * 2 + 2],
                            in_offset=None),
                            f"ds{s2 % 2}", (p, "sctd", c, half2), 16)
                        nds[s2 % 2] = cds // 16
            def emit_zgather(kblock):
                for k in kblock:
                    ei2, t2, sub2 = gidx(k)
                    gsl = k % 2
                    if k < 2:
                        M.wait(e, (p, "ztst", NG - 1))
                        M.wait(e, (p, "ztst", NG - 2))
                        M.wait(e, (p, "sctd", NCHK - 1, 1))
                        M.wait(e, (p, "sctd", NCHK - 2, 1))
                    else:
                        M.wait(e, (p, "zbst", *gidx(k - 2)))
                        M.wait(e, (p, "pwx", *gidx(k - 2)))
                    r0 = ei2 * CAP + t2 * TT + sub2 * 128
                    sm = f"dg{k % 2}"
                    cidx = M.inc(e.dma_start(
                        out=idxt[:, gsl * 2:gsl * 2 + 2],
                        in_=idxpw_s[r0:r0 + 128, :]), sm, None, 16)
                    M.wait_cnt(e, sm, cidx)
                    M.inc(e.indirect_dma_start(
                        out=zgk[gsl][:, :], out_offset=None,
                        in_=zt_s[:, :],
                        in_offset=bass.IndirectOffsetOnAxis(
                            idxt[:, gsl * 2:gsl * 2 + 1], 0)),
                        sm, (p, "zg", ei2, t2, sub2), 16)
                    if k >= 1:
                        kp = k - 1
                        ep, tp, sp = gidx(kp)
                        gslp = kp % 2
                        M.wait(e, (p, "zg", ep, tp, sp))
                        M.inc(e.tensor_copy(
                            out=pwsb[:, kp:kp + 1],
                            in_=idxt[:, gslp * 2 + 1:gslp * 2 + 2]
                            .bitcast(F32)), "cg", (p, "pwx", ep, tp, sp))
                    if k == E * NST * NSUB - 1:
                        M.wait(e, (p, "zg", ei2, t2, sub2))
                        M.inc(e.tensor_copy(
                            out=pwsb[:, k:k + 1],
                            in_=idxt[:, gsl * 2 + 1:gsl * 2 + 2]
                            .bitcast(F32)), "cg", (p, "pwx", ei2, t2, sub2))
            PPE = NST * NSUB
            if SPARSE:
                emit_zgather(range(0, 2 * PPE))
            def emit_yst1(ei2, j):
                M.wait(e, (p, "ysclh", ei2, j, 1))
                r0 = ei2 * CAP + j * 128
                M.inc(e.dma_start(
                    out=yinst_s[r0:r0 + 128, :],
                    in_=ytr[j % 4][:, 0:C]),
                    f"dy{j % 4}", (p, "yst", ei2, j), 16)
            for i in range(NVE):
                ei, qi = i // NVQ, i % NVQ
                par = i % 2
                if SPARSE and qi == 0 and ei >= 1:
                    for j in range(PPE):
                        emit_yst1(ei - 1, j)
                if SPARSE and ei + 2 < E:
                    emit_zgather(psl((ei + 2) * PPE, qi))
                if i < 2:
                    M.wait(e, (p, "yi", NTT * NCC - 1))
                else:
                    M.wait(e, (p, "ymm_last", i - 2))
                M.inc(e.dma_start(out=w1b[par][:, :], in_=w1h[ei, qi]),
                      f"dw{par}", None, 16)
                if qi == 0:
                    M.inc(e.dma_start(
                        out=b1s[:, (ei % 2) * NCFE:(ei % 2 + 1) * NCFE],
                        in_=b1h[ei]), f"dw{par}", None, 16)
                M.inc(e.dma_start(out=w2b[par][:, :], in_=w2h[ei, qi]),
                      f"dw{par}", (p, "wv", i), 16)
                if SPARSE and i == NVE - 1:
                    for j in range(PPE):
                        emit_yst1(E - 1, j)
            if SPARSE:
                for s_ in ("dy0", "dy1", "dy2", "dy3"):
                    M.wait_cnt(e, s_, M.total(s_))
                for c in range(NCHK):
                    if c >= 1:
                        M.wait(e, (p, "ybtp", c - 1, NCC - 1))
                    for half2 in range(2):
                        M.inc(e.indirect_dma_start(
                            out=ybk[half2][:, 0:C], out_offset=None,
                            in_=yinst_s[:, :],
                            in_offset=bass.IndirectOffsetOnAxis(
                                posb[:, half2 * NCHK + c:
                                     half2 * NCHK + c + 1], 0)),
                            f"dg{c % 2}",
                            (p, "ybg", c) if half2 == 1 else None, 16)
            return

        if ename == "tensor":
            for t in range(NTT):
                ln_tile("tensor", e, M, p, "L", t, xsrc(t), zdst(t),
                        rows_ff, rbf_ff, xbb_ff, xqq_ff, not from_yx,
                        tmpv_f[:, 0:TT], **lng(t))
                M.wait(e, (p, "Lztile", t))
                for s in range(NSUB):
                    g = t * NSUB + s
                    if g >= 4:
                        M.wait(e, (p, "pexp", g - 4))
                    for cc in range(NCC):
                        mm = e.matmul(
                            ps[4 + g % 4][:, 0:E],
                            zh[:, cc * T + t * TT + s * 128:
                               cc * T + t * TT + (s + 1) * 128],
                            wgs[:, cc * E:(cc + 1) * E],
                            start=(cc == 0), stop=(cc == NCC - 1))
                    M.inc(mm, "ct", (p, "lg", g))
                    M.wait(e, (p, "pw", g))
                    if g >= 2:
                        M.wait(e, (p, "pwtev", g - 2))
                    M.inc(e.transpose(ps[6 + g % 2][0:E, 128:256],
                                      pfull[:, 0:E], eyes[:]),
                          "ct", (p, "pwt_ps", g))
                    if SPARSE:
                        if g >= 2:
                            M.wait(e, (p, "crow", g - 2))
                        M.wait(e, (p, "maskw", g))
                        M.inc(e.matmul(ps[6 + g % 2][0:1, 384:384 + E],
                                       onec[:],
                                       maskb[:, g * E:(g + 1) * E],
                                       start=True, stop=True),
                              "ct", (p, "cnt", g))
                if SPARSE:
                    for s in range(NSUB):
                        zi = t * NSUB + s
                        if zi >= 2:
                            M.wait(e, (p, "ztmev", zi - 2))
                        for cc in range(NCC):
                            mm = e.transpose(
                                ps[2 + zi % 2].bitcast(BF16)[
                                    :, cc * 128:(cc + 1) * 128],
                                zh[:, cc * T + t * TT + s * 128:
                                   cc * T + t * TT + (s + 1) * 128],
                                eyeb[:])
                        M.inc(mm, "ct", (p, "ztp", zi))
            if SPARSE:
                M.wait(e, (p, "crow", NG - 1))
                M.inc(e.transpose(ps[4][0:128, 0:1], crow[0:1, 0:128],
                                  eyes[0:1, 0:1]), "ct", (p, "ccolp"))
                M.wait(e, (p, "ccbev"))
                M.inc(e.matmul(ps[4][:, 0:1], selm[:, :], ccb[:, 0:1],
                               start=True, stop=True), "ct", (p, "cbp"))
                M.wait(e, (p, "qscols"))
                M.inc(e.transpose(ps[4][0:1, 0:128], cbt[:, 1:2], eyes[:]),
                      "ct", (p, "qtp"))
                M.wait(e, (p, "qev2"))
                M.inc(e.transpose(ps[4][0:1, 0:128], cbt[:, 2:3], eyes[:]),
                      "ct", (p, "stp"))
                M.wait(e, (p, "qsrev"))
                M.wait(e, (p, "pexp", NG - 2))
                e.matmul(ps[6][:, 0:NCHK * E], ltc[:, :],
                         maskb[:, 0:NCHK * E], start=True, stop=False)
                e.matmul(ps[6][:, 0:NCHK * E], c64r[:, :],
                         qsr[0:1, 0:NCHK * E], start=False, stop=False)
                M.inc(e.matmul(ps[6][:, 0:NCHK * E], oner[:, :],
                               qsr[0:1, 128:128 + NCHK * E],
                               start=False, stop=True),
                      "ct", (p, "pospAll"))
            M.wait(e, (p, "pwtev", NG - 1))
            M.wait(e, (p, "Lstatsread", NTT - 1))
            if SPARSE:
                M.wait(e, (p, "crow", NG - 1))
            for t in range(NTT):
                for cc in range(NCC):
                    g = t * NCC + cc
                    if g >= 2:
                        M.wait(e, (p, "yi", g - 2))
                    M.inc(e.matmul(ps[g % 2][:, 0:TT],
                                   b2s[:, cc * 128:(cc + 1) * 128],
                                   pwtb[:, t * TT:(t + 1) * TT],
                                   start=True, stop=True),
                          "ct", (p, "bm", g))
            def emit_ztr2(e, kblock):
                for k in kblock:
                    ei2, t2, sub2 = gidx(k)
                    gsl = k % 2
                    pro = k < 2 * PPE
                    bank = 2 + (k % 2 if pro else 0)
                    M.wait(e, (p, "zg", ei2, t2, sub2))
                    if k < 2:
                        M.wait(e, (p, "ztmev", NG - 2 + k))
                    elif pro:
                        M.wait(e, (p, "zbev_v", k - 2))
                    else:
                        M.wait(e, (p, "zbev_v", k - 1))
                    for cc in range(NCC):
                        mm = e.transpose(
                            ps[bank].bitcast(BF16)[
                                :, cc * 128:(cc + 1) * 128],
                            zgk[gsl][:, cc * 128:(cc + 1) * 128],
                            eyeb[:])
                    M.inc(mm, "ct", (p, "ztr2", ei2, t2, sub2))
            PPE = NST * NSUB
            if SPARSE:
                emit_ztr2(e, range(0, 2 * PPE))
            for i in range(NVE):
                par = i % 2
                ei, qi = i // NVQ, i % NVQ
                M.wait(e, (p, "wv", i))
                for t in range(NTI):
                    g = i * NTI + t
                    if not SPARSE:
                        if g >= 2:
                            M.wait(e, (p, "pbev", g - 2))
                        M.inc(e.matmul(ps[4 + g % 2][:, 0:TT],
                                       ohbs[:, ei * 128:(ei + 1) * 128],
                                       pwtb[:, t * TT:(t + 1) * TT],
                                       start=True, stop=True),
                              "ct", (p, "pb", g))
                    if SPARSE:
                        M.wait(e, (p, "zeld", ei, t))
                    HBm = HBS if SPARSE else HB
                    for fc in range(NFV):
                        hg = g * NFV + fc
                        if SPARSE and hg == 0:
                            M.wait(e, (p, "posv2"))
                        if SPARSE and hg == 1:
                            M.wait(e, (p, "pwtev", NG - 1))
                        if SPARSE and hg == 2:
                            M.wait(e, (p, "zbev_v", 2 * PPE - 1))
                        if hg == 2 and not SPARSE:
                            M.wait(e, (p, "Lztile", NTT - 1))
                        if hg >= len(HBm):
                            M.wait(e, (p, "hev", hg - len(HBm)))
                        for cc in range(NCC):
                            if SPARSE:
                                mslice = sub_ap(zeS[t][:, :], cc * 128,
                                                [(NCC * 128, NSUB),
                                                 (1, 128)])
                            else:
                                mslice = zh[:, cc * T + t * TT:
                                            cc * T + (t + 1) * TT]
                            mm = e.matmul(
                                ps[HBm[hg % len(HBm)]][:, 0:TT],
                                w1b[par][:, cc * FV + fc * 128:
                                         cc * FV + (fc + 1) * 128],
                                mslice,
                                start=(cc == 0), stop=(cc == NCC - 1))
                        M.inc(mm, "ct", (p, "h", hg))
                    M.wait(e, (p, "hev", g * NFV + NFV - 1))
                    if i == 0 and t == 0:
                        M.wait(e, (p, "yi", NTT * NCC - 1))
                    YBm = YBS if SPARSE else YB
                    for dc in range(NCC):
                        yg = g * NCC + dc
                        if yg == 2 and not SPARSE:
                            M.wait(e, (p, "Lztile", NTT - 1))
                        if yg >= len(YBm):
                            M.wait(e, (p, "yev" if not SPARSE else "yadd",
                                       yg - len(YBm)))
                        for fc in range(NFV):
                            mm = e.matmul(
                                ps[YBm[yg % len(YBm)]][:, 0:TT],
                                w2b[par][:, fc * C + dc * 128:
                                         fc * C + (dc + 1) * 128],
                                hsb[:, fc * TT:(fc + 1) * TT],
                                start=(fc == 0), stop=(fc == NFV - 1))
                        key = (p, "ymm", yg)
                        if t == NTI - 1 and dc == NCC - 1:
                            key = (p, "ymm_last", i)
                        M.inc(mm, "ct", key)
                if SPARSE and qi == NVQ - 1 and t == NTI - 1:
                    for j in range(PPE):
                        tl = j // NSUB
                        M.wait(e, (p, "yadd",
                                   (i * NTI + tl) * NCC + NCC - 1))
                        jj = ei * PPE + j
                        for hh in range(2):
                            if jj >= 1:
                                M.wait(e, (p, "ysclh",
                                           *(divmod(jj - 1, PPE) + (hh,))))
                            elif hh == 0:
                                M.wait(e, (p, "qsrev"))
                                M.wait(e, (p, "pexp", NG - 1))
                                M.wait(e, (p, "pexp", NG - 2))
                            for dcl in range(NCC // 2):
                                dc = hh * (NCC // 2) + dcl
                                mm = e.transpose(
                                    ps[4 + hh][:, dcl * 128:
                                               (dcl + 1) * 128],
                                    yacc_f[:, dc * CAP + j * 128:
                                           dc * CAP + (j + 1) * 128],
                                    eyes[:])
                            M.inc(mm, "ct", (p, "ytp2h", ei, j, hh))
                if SPARSE and ei + 2 < E:
                    emit_ztr2(e, psl((ei + 2) * PPE, qi))
            if SPARSE:
                for c in range(NCHK):
                    M.wait(e, (p, "ybsum", c))
                    if c == 0:
                        M.wait(e, (p, "ysclh", E - 1, PPE - 1, 1))
                    for cc in range(NCC):
                        if cc >= 2:
                            M.wait(e, (p, "ybadd", c, cc - 2))
                        elif c >= 1:
                            M.wait(e, (p, "ybadd", c - 1, NCC - 1))
                        mm = e.transpose(
                            ps[4 + cc % 2][:, 0:128],
                            ybk[0][:, cc * 128:(cc + 1) * 128], eyes[:])
                        M.inc(mm, "ct", (p, "ybtp", c, cc))
            return

        if ename == "scalar":
            for t in range(NTT):
                ln_tile("scalar", e, M, p, "L", t, xsrc(t), zdst(t),
                        rows_ff, rbf_ff, xbb_ff, xqq_ff, not from_yx,
                        tmpv_f[:, 0:TT], **lng(t))
                for s in range(NSUB):
                    g = t * NSUB + s
                    M.wait(e, (p, "negmax", g))
                    M.inc(e.activation(out=ppex[:, 0:E],
                                       in_=ps[4 + g % 4][:, 0:E],
                                       func=AF.Exp, bias=ppm[:, 0:1],
                                       scale=1.0), "cs", (p, "pexp", g))
                    M.wait(e, (p, "pwt_ps", g))
                    M.inc(e.copy(out=pwtb[0:E, t * TT + s * 128:
                                          t * TT + (s + 1) * 128],
                                 in_=ps[6 + g % 2][0:E, 128:256]),
                          "cs", (p, "pwtev", g))
                    if SPARSE:
                        M.wait(e, (p, "cnt", g))
                        M.inc(e.copy(out=crow[0:1, g * E:(g + 1) * E],
                                     in_=ps[6 + g % 2][0:1, 384:384 + E]),
                              "cs", (p, "crow", g))
                if SPARSE:
                    for s in range(NSUB):
                        zi = t * NSUB + s
                        M.wait(e, (p, "ztp", zi))
                        if zi >= 2:
                            M.wait(e, (p, "ztst", zi - 2))
                        for cc in range(NCC):
                            op = e.copy(
                                out=ztma[zi % 2][:, cc * 128:(cc + 1) * 128],
                                in_=ps[2 + zi % 2].bitcast(BF16)[
                                    :, cc * 128:(cc + 1) * 128])
                            M.inc(op, "cs",
                                  (p, "ztmev", zi) if cc == NCC - 1 else None)
            if SPARSE:
                M.wait(e, (p, "ccolp"))
                M.inc(e.copy(out=ccb[:, 0:1], in_=ps[4][0:128, 0:1]),
                      "cs", (p, "ccbev"))
                M.wait(e, (p, "qtp"))
                M.inc(e.copy(out=qsr[0:1, 0:128], in_=ps[4][0:1, 0:128]),
                      "cs", (p, "qev2"))
                M.wait(e, (p, "stp"))
                M.inc(e.copy(out=qsr[0:1, 128:256], in_=ps[4][0:1, 0:128]),
                      "cs", (p, "qsrev"))
            for i in range(NVE):
                ei, qi = i // NVQ, i % NVQ
                if SPARSE and qi == 0:
                    for t in range(NST):
                        if ei >= 1:
                            hlast = (((ei - 1) * NVQ + NVQ - 1) * NTI
                                     + t) * NFV + NFV - 1
                            M.wait(e, (p, "h", hlast))
                        M.wait(e, (p, "zbst", ei, t, NSUB - 1))
                        M.wait(e, (p, "zbst", ei, t, NSUB - 2))
                        M.inc(e.dma_start(out=zeS[t][:, :],
                                          in_=zc_s[ei, t]),
                              f"dze{t % 2}", (p, "zeld", ei, t), 16)
                for t in range(NTI):
                    g = i * NTI + t
                    if not SPARSE:
                        M.wait(e, (p, "pb", g))
                        if g >= 2:
                            M.wait(e, (p, "yev", (g - 2) * NCC + NCC - 1))
                        M.inc(e.copy(
                            out=pbs_f[:, (g % 2) * TT:(g % 2 + 1) * TT],
                            in_=ps[4 + g % 2][:, 0:TT]),
                            "cs", (p, "pbev", g))
                    for fc in range(NFV):
                        hg = g * NFV + fc
                        M.wait(e, (p, "h", hg))
                        fcg = (ei % 2) * NCFE + qi * NFV + fc
                        HBm = HBS if SPARSE else HB
                        M.inc(e.activation(
                            out=hsb[:, fc * TT:(fc + 1) * TT],
                            in_=ps[HBm[hg % len(HBm)]][:, 0:TT],
                            func=AF.Gelu,
                            bias=b1s[:, fcg:fcg + 1], scale=1.0),
                            "cs", (p, "hev", hg))
            return

        if ename == "vector":
            def ch(op, key=None):
                c = M.inc(op, "cv", key)
                M.wait_cnt(e, "cv", c)
            for t in range(NTT):
                ln_tile("vector", e, M, p, "L", t, xsrc(t), zdst(t),
                        rows_ff, rbf_ff, xbb_ff, xqq_ff, not from_yx,
                        tmpv_f[:, 0:TT], **lng(t))
                for s in range(NSUB):
                    g = t * NSUB + s
                    M.wait(e, (p, "lg", g))
                    if g >= 1:
                        M.wait(e, (p, "pwt_ps", g - 1))
                    ch(e.tensor_reduce(out=ppm[:, 0:1],
                                       in_=ps[4 + g % 4][:, 0:E],
                                       axis=AX.X, op=ALU.max, negate=True),
                       (p, "negmax", g))
                    M.wait(e, (p, "pexp", g))
                    ch(e.tensor_reduce(out=ppm[:, 0:1], in_=ppex[:, 0:E],
                                       axis=AX.X, op=ALU.max))
                    ch(e.tensor_scalar(out=ppb[:, 0:E], in0=ppex[:, 0:E],
                                       scalar1=ppm[:, 0:1], scalar2=None,
                                       op0=ALU.is_ge))
                    if SPARSE:
                        ch(e.tensor_copy(out=eqf[:, g * E:(g + 1) * E],
                                         in_=ppb[:, 0:E]))
                    ch(e.scalar_tensor_tensor(out=ppb[:, 0:E],
                                              in0=ppb[:, 0:E], scalar=-2.0,
                                              in1=ppex[:, 0:E], op0=ALU.mult,
                                              op1=ALU.add))
                    ch(e.tensor_reduce(out=ppm[:, 0:1], in_=ppb[:, 0:E],
                                       axis=AX.X, op=ALU.max))
                    ch(e.tensor_scalar(out=ppb[:, 0:E], in0=ppex[:, 0:E],
                                       scalar1=ppm[:, 0:1], scalar2=None,
                                       op0=ALU.is_ge))
                    if SPARSE:
                        ch(e.tensor_copy(out=maskf[:, g * E:(g + 1) * E],
                                         in_=ppb[:, 0:E]))
                        ch(e.tensor_copy(out=maskb[:, g * E:(g + 1) * E],
                                         in_=ppb[:, 0:E]), (p, "maskw", g))
                    ch(e.tensor_tensor(out=ppa[:, 0:E], in0=ppex[:, 0:E],
                                       in1=ppb[:, 0:E], op=ALU.mult))
                    ch(e.tensor_reduce(out=ppr[:, 0:1], in_=ppa[:, 0:E],
                                       axis=AX.X, op=ALU.add))
                    ch(e.tensor_scalar(out=ppr[:, 0:1], in0=ppr[:, 0:1],
                                       scalar1=2.0, scalar2=2 * EPS_GATE,
                                       op0=ALU.mult, op1=ALU.add))
                    ch(e.reciprocal(out=ppr[:, 0:1], in_=ppr[:, 0:1]))
                    ch(e.tensor_scalar(out=pfull[:, 0:E], in0=ppa[:, 0:E],
                                       scalar1=ppr[:, 0:1], scalar2=None,
                                       op0=ALU.mult), (p, "pw", g))
                    if SPARSE:
                        ch(e.tensor_copy(out=pwm[:, g * E:(g + 1) * E],
                                         in_=pfull[:, 0:E]), (p, "pwmw", g))
            if SPARSE:
                M.wait(e, (p, "cbp"))
                ch(e.tensor_tensor(out=cbt[:, 0:1], in0=ps[4][:, 0:1],
                                   in1=segc[:, 0:1], op=ALU.add))
                st_i = idxt[:, 0:1]
                ch(e.tensor_copy(out=st_i, in_=cbt[:, 0:1]))
                ch(e.tensor_scalar(out=cbt[:, 2:3].bitcast(mybir.dt.int32),
                                   in0=st_i, scalar1=63, scalar2=None,
                                   op0=ALU.bitwise_and))
                ch(e.tensor_copy(out=cbt[:, 2:3],
                                 in_=cbt[:, 2:3].bitcast(mybir.dt.int32)))
                ch(e.tensor_scalar(out=cbt[:, 1:2].bitcast(mybir.dt.int32),
                                   in0=st_i, scalar1=6, scalar2=None,
                                   op0=ALU.arith_shift_right))
                ch(e.tensor_copy(out=cbt[:, 1:2],
                                 in_=cbt[:, 1:2].bitcast(mybir.dt.int32)),
                   (p, "qscols"))
                NE = NCHK * E
                M.wait(e, (p, "pospAll"))
                M.wait(e, (p, "pwmw", NG - 1))
                M.wait(e, "dmpc")
                ch(e.tensor_scalar(out=ppA[:, 0:NE],
                                   in0=maskf[:, 0:NE],
                                   scalar1=-1.0, scalar2=1.0,
                                   op0=ALU.mult, op1=ALU.add))
                ch(e.tensor_scalar(out=ppA[:, 0:NE], in0=ppA[:, 0:NE],
                                   scalar1=dmpc[:, 0:1], scalar2=None,
                                   op0=ALU.mult))
                ch(e.tensor_tensor(out=ppB[:, 0:NE], in0=ps[6][:, 0:NE],
                                   in1=maskf[:, 0:NE], op=ALU.mult),
                   (p, "posv2"))
                ch(e.tensor_tensor(out=ppA[:, 0:NE], in0=ppB[:, 0:NE],
                                   in1=ppA[:, 0:NE], op=ALU.add))

                def segsum(dst, srcsel):
                    assert E & (E - 1) == 0
                    w = E
                    while w > 1:
                        h2 = w // 2
                        va = sub_ap(ppB[:, 0:NE], 0, [(E, NCHK), (1, h2)])
                        vb = sub_ap(ppB[:, 0:NE], h2, [(E, NCHK), (1, h2)])
                        ch(e.tensor_tensor(out=va, in0=va, in1=vb,
                                           op=ALU.add))
                        w = h2
                    ch(e.tensor_copy(
                        out=dst, in_=sub_ap(ppB[:, 0:NE], 0,
                                            [(E, NCHK), (1, 1)])))
                ch(e.tensor_tensor(out=ppB[:, 0:NE], in0=ppA[:, 0:NE],
                                   in1=eqf[:, 0:NE], op=ALU.mult))
                segsum(posb[:, 0:NCHK], None)
                ch(e.tensor_tensor(out=ppB[:, 0:NE], in0=pwm[:, 0:NE],
                                   in1=eqf[:, 0:NE], op=ALU.mult))
                segsum(pw12[:, 0:NCHK], None)
                ch(e.tensor_tensor(out=ppB[:, 0:NE], in0=maskf[:, 0:NE],
                                   in1=eqf[:, 0:NE], op=ALU.subtract))
                ch(e.tensor_tensor(out=ppA[:, 0:NE], in0=ppA[:, 0:NE],
                                   in1=ppB[:, 0:NE], op=ALU.mult))
                ch(e.tensor_tensor(out=ppB[:, 0:NE], in0=pwm[:, 0:NE],
                                   in1=ppB[:, 0:NE], op=ALU.mult))
                segsum(pw12[:, NCHK:2 * NCHK], None)
                ch(e.tensor_copy(out=ppB[:, 0:NE], in_=ppA[:, 0:NE]))
                segsum(posb[:, NCHK:2 * NCHK], None)
                sctf = sctgA[:, :].bitcast(F32)
                for half in range(2):
                    ch(e.tensor_copy(
                        out=sub_ap(sctgA[:, :], 2 * NCHK * half,
                                   [(2, NCHK)]),
                        in_=tidh[:, 0:NCHK]))
                    ch(e.tensor_copy(
                        out=sub_ap(sctf, 2 * NCHK * half + 1, [(2, NCHK)]),
                        in_=pw12[:, half * NCHK:(half + 1) * NCHK]),
                       (p, "sct2", half))
            if from_yx and dbg is not None:
                M.wait(e, (p, "dbgst"))
            for t in range(NTT):
                if not from_yx:
                    M.wait(e, (p, "xld2", t))
                for cc in range(NCC):
                    g = t * NCC + cc
                    M.wait(e, (p, "bm", g))
                    dst = yx[:, cc * T + t * TT:cc * T + (t + 1) * TT]
                    src0 = dst if from_yx else \
                        xstr_f[:, cc * TT:(cc + 1) * TT]
                    ch(e.tensor_tensor(out=dst, in0=src0,
                                       in1=ps[g % 2][:, 0:TT], op=ALU.add),
                       (p, "yi", g))
            if SPARSE:
                def emit_zbev(kblock):
                    for k in kblock:
                        ei2, t2, sub2 = gidx(k)
                        gsl = k % 2
                        bank = 2 + (k % 2 if k < 2 * PPE else 0)
                        M.wait(e, (p, "ztr2", ei2, t2, sub2))
                        if k >= 2:
                            M.wait(e, (p, "zbst", *gidx(k - 2)))
                        for cc in range(NCC):
                            op = e.tensor_copy(
                                out=zgk[gsl][:, cc * 128:(cc + 1) * 128],
                                in_=ps[bank].bitcast(BF16)[
                                    :, cc * 128:(cc + 1) * 128])
                            M.inc(op, "cv",
                                  (p, "zbev_v", k) if cc == NCC - 1
                                  else None)
                        M.wait_cnt(e, "cv", M.total("cv"))
                PPE = NST * NSUB
                emit_zbev(range(0, 2 * PPE))
                for i in range(NVE):
                    ei, qi = i // NVQ, i % NVQ
                    for t in range(NST):
                        g = i * NST + t
                        for dc in range(NCC):
                            yg = g * NCC + dc
                            if t == NST - 1 and dc == NCC - 1:
                                M.wait(e, (p, "ymm_last", i))
                            else:
                                M.wait(e, (p, "ymm", yg))
                            dst = yacc_f[:, dc * CAP + t * TT:
                                         dc * CAP + (t + 1) * TT]
                            if qi == 0:
                                if ei >= 1 and dc == 0:
                                    M.wait(e, (p, "ytp2h", ei - 1,
                                               t * NSUB + NSUB - 1, 1))
                                op = e.tensor_copy(
                                    out=dst, in_=ps[YBS[yg % 2]][:, 0:TT])
                            else:
                                op = e.tensor_tensor(
                                    out=dst, in0=dst,
                                    in1=ps[YBS[yg % 2]][:, 0:TT],
                                    op=ALU.add)
                            ch(op, (p, "yadd", yg))
                    if qi == NVQ - 1:
                        for j in range(PPE):
                            jj = ei * PPE + j
                            M.wait(e, (p, "ytp2h", ei, j, 0))
                            if jj >= 4:
                                M.wait(e, (p, "yst", *divmod(jj - 4, PPE)))
                            M.wait(e, (p, "pwx", ei, j // NSUB, j % NSUB))
                            ch(e.tensor_scalar(
                                out=ytr[j % 4][:, 0:C // 2],
                                in0=ps[4][:, 0:C // 2],
                                scalar1=pwsb[:, jj:jj + 1],
                                scalar2=None, op0=ALU.mult),
                               (p, "ysclh", ei, j, 0))
                            M.wait(e, (p, "ytp2h", ei, j, 1))
                            ch(e.tensor_scalar(
                                out=ytr[j % 4][:, C // 2:C],
                                in0=ps[5][:, 0:C // 2],
                                scalar1=pwsb[:, jj:jj + 1],
                                scalar2=None, op0=ALU.mult),
                               (p, "ysclh", ei, j, 1))
                    if ei + 2 < E:
                        emit_zbev(psl((ei + 2) * PPE, qi))
                for c in range(NCHK):
                    M.wait(e, (p, "ybg", c))
                    ch(e.tensor_tensor(out=ybk[0][:, 0:C],
                                       in0=ybk[0][:, 0:C],
                                       in1=ybk[1][:, 0:C], op=ALU.add),
                       (p, "ybsum", c))
                    for cc in range(NCC):
                        M.wait(e, (p, "ybtp", c, cc))
                        dst = yx[:, cc * T + c * 128:cc * T + (c + 1) * 128]
                        key = (p, "ybadd", c, cc)
                        if c == NCHK - 1 and cc == NCC - 1:
                            key = (p, "ffn_done")
                        ch(e.tensor_tensor(
                            out=dst, in0=dst,
                            in1=ps[4 + cc % 2][:, 0:128],
                            op=ALU.add), key)
            else:
                for i in range(NVE):
                    for t in range(NTT):
                        g = i * NTT + t
                        M.wait(e, (p, "pbev", g))
                        for dc in range(NCC):
                            yg = g * NCC + dc
                            if t == NTT - 1 and dc == NCC - 1:
                                M.wait(e, (p, "ymm_last", i))
                            else:
                                M.wait(e, (p, "ymm", yg))
                            ch(e.tensor_tensor(
                                out=tmpv_f[:, 0:TT],
                                in0=ps[YB[yg % 3]][:, 0:TT],
                                in1=pbs_f[:, (g % 2) * TT:(g % 2 + 1) * TT],
                                op=ALU.mult))
                            dst = yx[:, dc * T + t * TT:dc * T + (t + 1) * TT]
                            key = (p, "yev", yg)
                            if i == NVE - 1 and t == NTT - 1 and dc == NCC - 1:
                                key = (p, "ffn_done")
                            ch(e.tensor_tensor(out=dst, in0=dst,
                                               in1=tmpv_f[:, 0:TT],
                                               op=ALU.add), key)
            return

    def attn_phase(ename, e, M, p, b):
        NGK = NTT * NSUB * NKVG
        NSG = NTT * NSUB

        def xsrc(t):
            def f(cc):
                return yx[:, cc * T + t * TT:cc * T + (t + 1) * TT]
            return f

        zdst = xsrc

        if ename == "sync":
            M.inc(e.dma_start(out=x1s[b], in_=yx[:]), "st", (p, "x1st"), 16)
            if NL >= 1:
                M.wait(e, (p, "projmm", NL - 1, NTT - 1, NCC - 1))
            else:
                M.wait(e, (p, "Aztile", NTT - 1))
            CH = min(T, (NCC * TT) // 2)
            for cc in range(NCC):
                for j in range(T // CH):
                    if cc + j > 0:
                        M.wait(e, (p, "xaadd", cc, j - 1) if j >= 1 else
                               (p, "xaadd", cc - 1, T // CH - 1))
                    M.inc(e.dma_start(
                        out=xa_f[:, 0:CH],
                        in_=x1s[b][:, cc * T + j * CH:cc * T + (j + 1) * CH]),
                        "dx", (p, "x1ld", cc, j), 16)
            return

        if ename == "gpsimd":
            for t in range(NTT):
                ln_tile("gpsimd", e, M, p, "A", t, xsrc(t), zdst(t),
                        rows_at, rbf_at, xbA[0], xbA[1], False,
                        tmpa_f[:, 0:TT], z_gate=(p, "x1st"))
            for L in range(NL):
                if L == 0:
                    M.wait(e, (p, "Aztile", NTT - 1))
                else:
                    M.wait(e, (p, "kvmm", L - 1, NGK - 1))
                M.inc(e.dma_start(out=wkv[:, :], in_=wkvh_d[L]),
                      "dwa", None, 16)
                M.inc(e.dma_start(
                    out=bkvs[0:1, (L % 2) * 2 * C:(L % 2 + 1) * 2 * C],
                    in_=bkvh[L]), "dwa", (p, "wkv", L), 16)
                if L >= 1:
                    M.wait(e, (p, "projmm", L - 1, NTT - 1, NCC - 1))
                M.inc(e.dma_start(out=wqv[:, :], in_=wqh_d[L]),
                      "dwb", None, 16)
                M.inc(e.dma_start(out=wpv[:, :], in_=wprojh[L]),
                      "dwb", None, 16)
                M.inc(e.dma_start(out=bqs[:, (L % 2) * NCC:(L % 2 + 1) * NCC],
                                  in_=bqqh[L]), "dwb", None, 16)
                M.inc(e.dma_start(out=bps[:, (L % 2) * NCC:(L % 2 + 1) * NCC],
                                  in_=bpjh[L]), "dwb", (p, "wqp", L), 16)
            return

        if ename == "tensor":
            for t in range(NTT):
                ln_tile("tensor", e, M, p, "A", t, xsrc(t), zdst(t),
                        rows_at, rbf_at, xbA[0], xbA[1], False,
                        tmpa_f[:, 0:TT])
            for L in range(NL):
                M.wait(e, (p, "wkv", L))
                def emit_S(sg):
                    slot = (sg % 2) * 2 * C
                    if sg == 0 and L == 0:
                        M.wait(e, (p, "Aztile", NTT - 1))
                    if sg >= 2:
                        M.wait(e, (p, "sacc", L, sg - 2))
                    elif sg == 1:
                        if L >= 1:
                            M.wait(e, (p, "yTev", L - 1, NTT - 1, NCC - 1))
                    elif L >= 1:
                        M.wait(e, (p, "sacc", L - 1, NSG - 2))
                    sb0 = 2 + 4 * (sg % 2)
                    for h in range(H):
                        mm = e.matmul(
                            ps[sb0 + h // SP][0:64,
                                              (h % SP) * 64:
                                              (h % SP) * 64 + 64],
                            kvt[:, slot + h * DH:slot + (h + 1) * DH],
                            kvt[:, slot + C + h * DH:
                                slot + C + (h + 1) * DH],
                            start=True, stop=True)
                    M.inc(mm, "ct", (p, "smm", L, sg))
                for t in range(NTT):
                    M.wait(e, (p, "xbc1", L, t))
                    for sub in range(NSUB):
                        sg = t * NSUB + sub
                        for kg in range(NKVG):
                            gk = sg * NKVG + kg
                            if L == 0 and gk < 2:
                                M.wait(e, (p, "Astatsread", NTT - 1))
                            if L >= 1 and gk in (2, 3):
                                M.wait(e, (p, "projev", L - 1, NTT - 1,
                                           NCC - 1))
                            if gk >= 4:
                                M.wait(e, (p, "kvev", L, gk - 4))
                            for cc in range(NCC):
                                e.matmul(
                                    ps[KB[gk % 4]][:, 0:512],
                                    xbA[t % 2][:, cc * TT + sub * 128:
                                               cc * TT + (sub + 1) * 128],
                                    wkv[:, cc * 2 * C + kg * 512:
                                        cc * 2 * C + (kg + 1) * 512],
                                    start=(cc == 0), stop=False)
                            M.inc(e.matmul(
                                ps[KB[gk % 4]][:, 0:512], oner[:],
                                bkvs[0:1, (L % 2) * 2 * C + kg * 512:
                                     (L % 2) * 2 * C + (kg + 1) * 512],
                                start=False, stop=True),
                                "ct", (p, "kvmm", L, gk))
                        if sg >= 1:
                            M.wait(e, (p, "kvev", L,
                                       (sg - 1) * NKVG + NKVG - 1))
                            emit_S(sg - 1)
                        if sg == NSUB * NTT - 1:
                            M.wait(e, (p, "kvev", L, sg * NKVG + NKVG - 1))
                            emit_S(sg)
                M.wait(e, (p, "wqp", L))
                M.wait(e, (p, "sbf2", L))
                def qwait(t, fc):
                    if fc >= 3:
                        M.wait(e, (p, "qev", L, t, fc - 3))
                    elif t >= 1:
                        M.wait(e, (p, "projev", L, t - 1, NCC - 1))
                    elif L >= 1:
                        M.wait(e, (p, "projev", L - 1, NTT - 1, NCC - 1))
                for t in range(NTT):
                    M.wait(e, (p, "xbc2", L, t))
                    if t == 0:
                        M.wait(e, (p, "kvev", L, NGK - 1))
                    for fc in range(NCC):
                        qwait(t, fc)
                        for cc in range(NCC):
                            mm = e.matmul(
                                ps[QB[fc % 3]][:, 0:TT],
                                wqv[:, cc * C + fc * 128:
                                    cc * C + (fc + 1) * 128],
                                xbA[t % 2][:, cc * TT:(cc + 1) * TT],
                                start=(cc == 0), stop=(cc == NCC - 1))
                        M.inc(mm, "ct", (p, "qmm", L, t, fc))
                    for fc in range(NCC):
                        M.wait(e, (p, "qev", L, t, fc))
                        if fc >= 3:
                            M.wait(e, (p, "yTev", L, t, fc - 3))
                        elif t >= 1:
                            M.wait(e, (p, "yTev", L, t - 1, NCC - 1))
                        elif L >= 1:
                            M.wait(e, (p, "yTev", L - 1, NTT - 1, NCC - 1))
                        M.inc(e.matmul(ps[TB[fc % 3]][:, 0:TT],
                                       sbf2[:, fc * 128:(fc + 1) * 128],
                                       qTt[:, fc * TT:(fc + 1) * TT],
                                       start=True, stop=True),
                              "ct", (p, "yTmm", L, t, fc))
                    M.wait(e, (p, "yTev", L, t, NCC - 1))
                    for cc in range(NCC):
                        if cc >= 3:
                            M.wait(e, (p, "projev", L, t, cc - 3))
                        for fc in range(NCC):
                            mm = e.matmul(
                                ps[PB[cc % 3]][:, 0:TT],
                                wpv[:, fc * C + cc * 128:
                                    fc * C + (cc + 1) * 128],
                                yTt[:, fc * TT:(fc + 1) * TT],
                                start=(fc == 0), stop=(fc == NCC - 1))
                        M.inc(mm, "ct", (p, "projmm", L, t, cc))
            return

        if ename == "scalar":
            for t in range(NTT):
                ln_tile("scalar", e, M, p, "A", t, xsrc(t), zdst(t),
                        rows_at, rbf_at, xbA[0], xbA[1], False,
                        tmpa_f[:, 0:TT])
            for L in range(NL):
                for t in range(NTT):
                    for sub in range(NSUB):
                        for kg in range(NKVG):
                            gk = (t * NSUB + sub) * NKVG + kg
                            sg = t * NSUB + sub
                            M.wait(e, (p, "kvmm", L, gk))
                            slot = (sg % 2) * 2 * C
                            M.inc(e.copy(
                                out=kvt[:, slot + kg * 512:
                                        slot + (kg + 1) * 512],
                                in_=ps[KB[gk % 4]][:, 0:512]),
                                "cs", (p, "kvev", L, gk))
                M.wait(e, (p, "sacc", L, NSG - 1))
                for h in range(H):
                    off = (h % 2) * 64
                    fc = h // 2
                    M.inc(e.activation(
                        out=sbf2[off:off + 64,
                                 fc * 128 + off:fc * 128 + off + 64],
                        in_=ssb[0:64, h * DH:(h + 1) * DH],
                        func=AF.Identity, bias=0.0, scale=0.125),
                        "cs", (p, "sbf2", L) if h == H - 1 else None)
                for t in range(NTT):
                    if t < 2:
                        M.wait(e, (p, "smm", L, NSG - 1))
                    else:
                        M.wait(e, (p, "qmm", L, t - 2, NCC - 1))
                    for cc in range(NCC):
                        M.inc(e.copy(out=xbA[t % 2][:, cc * TT:(cc + 1) * TT],
                                     in_=xsrc(t)(cc)), "cs",
                              (p, "xbc2", L, t) if cc == NCC - 1 else None)
                    M.wait_cnt(e, "cs", M.total("cs"))
                    for fc in range(NCC):
                        M.wait(e, (p, "qmm", L, t, fc))
                        M.inc(e.activation(
                            out=qTt[:, fc * TT:(fc + 1) * TT],
                            in_=ps[QB[fc % 3]][:, 0:TT], func=AF.Identity,
                            bias=bqs[:, (L % 2) * NCC + fc:
                                     (L % 2) * NCC + fc + 1], scale=1.0),
                            "cs", (p, "qev", L, t, fc))
                    for fc in range(NCC):
                        M.wait(e, (p, "yTmm", L, t, fc))
                        M.inc(e.copy(out=yTt[:, fc * TT:(fc + 1) * TT],
                                     in_=ps[TB[fc % 3]][:, 0:TT]),
                              "cs", (p, "yTev", L, t, fc))
                    for cc in range(NCC):
                        M.wait(e, (p, "projmm", L, t, cc))
                        if cc >= 2:
                            M.wait(e, (p, "xadd", L, t, cc - 2))
                        elif t >= 1:
                            M.wait(e, (p, "xadd", L, t - 1, NCC - 2 + cc))
                        elif L >= 1:
                            M.wait(e, (p, "xadd", L - 1, NTT - 1,
                                       NCC - 2 + cc))
                        M.inc(e.activation(
                            out=tmpa_f[:, (cc % 2) * TT:(cc % 2 + 1) * TT],
                            in_=ps[PB[cc % 3]][:, 0:TT], func=AF.Identity,
                            bias=bps[:, (L % 2) * NCC + cc:
                                     (L % 2) * NCC + cc + 1], scale=1.0),
                            "cs", (p, "projev", L, t, cc))
            return

        if ename == "vector":
            def ch(op, key=None):
                c = M.inc(op, "cv", key)
                M.wait_cnt(e, "cv", c)
            for t in range(NTT):
                ln_tile("vector", e, M, p, "A", t, xsrc(t), zdst(t),
                        rows_at, rbf_at, xbA[0], xbA[1], False,
                        tmpa_f[:, 0:TT], z_gate=(p, "x1st"))
            for L in range(NL):
                for t in range(NTT):
                    if t >= 2:
                        M.wait(e, (p, "kvmm", L,
                                   ((t - 2) * NSUB + NSUB - 1) * NKVG +
                                   NKVG - 1))
                    elif L == 0:
                        M.wait(e, (p, "As1", NTT - 1))
                        M.wait(e, (p, "As2", NTT - 1))
                    else:
                        M.wait(e, (p, "projmm", L - 1, NTT - 1, NCC - 1))
                    for cc in range(NCC):
                        M.inc(e.tensor_copy(
                            out=xbA[t % 2][:, cc * TT:(cc + 1) * TT],
                            in_=xsrc(t)(cc)), "cv",
                            (p, "xbc1", L, t) if cc == NCC - 1 else None)
                    M.wait_cnt(e, "cv", M.total("cv"))
                    for sub in range(NSUB):
                        sg = t * NSUB + sub
                        for s_ in ([sg - 1] if sg >= 1 else []) + \
                                ([sg] if sg == NSUB * NTT - 1 else []):
                            M.wait(e, (p, "smm", L, s_))
                            first = (s_ == 0)
                            if first and L >= 1:
                                M.wait(e, (p, "sbf2", L - 1))
                            sb0 = 2 + 4 * (s_ % 2)
                            if first:
                                ch(e.tensor_copy(
                                    out=ssb[:, 0:SP * 64],
                                    in_=ps[sb0][0:64, 0:SP * 64]))
                            else:
                                ch(e.tensor_tensor(
                                    out=ssb[:, 0:SP * 64],
                                    in0=ssb[:, 0:SP * 64],
                                    in1=ps[sb0][0:64, 0:SP * 64],
                                    op=ALU.add))
                            if H > SP:
                                op2 = (e.tensor_copy(
                                    out=ssb[:, SP * 64:H * 64],
                                    in_=ps[sb0 + 1][0:64, 0:(H - SP) * 64])
                                    if first else
                                    e.tensor_tensor(
                                        out=ssb[:, SP * 64:H * 64],
                                        in0=ssb[:, SP * 64:H * 64],
                                        in1=ps[sb0 + 1][0:64,
                                                        0:(H - SP) * 64],
                                        op=ALU.add))
                                ch(op2, (p, "sacc", L, s_))
                            else:
                                M.inc(e.nop(), "cv", (p, "sacc", L, s_))
                for t in range(NTT):
                    for cc in range(NCC):
                        M.wait(e, (p, "projev", L, t, cc))
                        dst = yx[:, cc * T + t * TT:cc * T + (t + 1) * TT]
                        ch(e.tensor_tensor(
                            out=dst, in0=dst,
                            in1=tmpa_f[:, (cc % 2) * TT:(cc % 2 + 1) * TT],
                            op=ALU.add), (p, "xadd", L, t, cc))
            CH = min(T, (NCC * TT) // 2)
            for cc in range(NCC):
                for j in range(T // CH):
                    M.wait(e, (p, "x1ld", cc, j))
                    dst = yx[:, cc * T + j * CH:cc * T + (j + 1) * CH]
                    ch(e.tensor_tensor(out=dst, in0=dst, in1=xa_f[:, 0:CH],
                                       op=ALU.add), (p, "xaadd", cc, j))
            return

    def setup_phase(ename, e, M, p):
        if ename == "sync":
            M.inc(e.dma_start(out=eyes[:], in_=eyeh[:]), "dx", "eyes", 16)
        elif ename == "gpsimd":
            M.inc(e.dma_start(out=eyeb[:], in_=eyeh[:]), "dwc", None, 16)
            M.inc(e.dma_start(out=wgs[:], in_=wgh[:]), "dwc", None, 16)
            M.inc(e.dma_start(out=b2s[:], in_=b2h[:]), "dwc", None, 16)
            M.inc(e.dma_start(out=ohbs[:], in_=ohbh[:]), "dwc", "const_w", 16)
            M.inc(e.memset(onec[:], 1.0), "cg", None)
            M.inc(e.memset(oner[:], 1.0), "cg", None)
            M.inc(e.memset(sbf2[:], 0.0), "cg", "memsets")
            if SPARSE:
                M.inc(e.dma_start(out=ltc[:], in_=ltc_h[:]), "dwc", None, 16)
                M.inc(e.dma_start(out=selm[:], in_=selm_h[:]), "dwc", None,
                      16)
                M.inc(e.dma_start(out=segc[:], in_=segc_h[:]), "dwc", None,
                      16)
                M.inc(e.dma_start(out=tidh[:], in_=tidh_h[:]), "dwc", None,
                      16)
                M.inc(e.dma_start(out=pfh[:], in_=pfh_h[:]), "dwc",
                      "sparse_consts", 16)
                M.inc(e.memset(c64r[:], 64.0), "cg", None)
                M.inc(e.memset(crow[:], 0.0), "cg", None)
                M.inc(e.memset(zrow0[:], 0.0), "cg", "memsets2")
                M.wait(e, "memsets2")
                M.inc(e.dma_start(out=zt_s[T:T + 1, :], in_=zrow0[:]),
                      "dws", "ztzero", 16)
        elif ename == "vector" and SPARSE:
            M.wait(e, "sparse_consts")
            c_ = M.inc(e.tensor_copy(out=dmpc[:, 0:1], in_=tidh[:, 0:1]),
                       "cv")
            M.wait_cnt(e, "cv", c_)
            M.inc(e.tensor_scalar(out=dmpc[:, 0:1], in0=dmpc[:, 0:1],
                                  scalar1=float(NI), scalar2=None,
                                  op0=ALU.add), "cv", "dmpc")


    def make_emitter(ename):
        def f(e, M):
            for pid, ph in enumerate(phases):
                if pid >= 1:
                    for o_ in ENGS:
                        if o_ != ename:
                            M.wait(e, ("pd", pid - 1, o_))
                if ph[0] == "setup":
                    setup_phase(ename, e, M, pid)
                elif ph[0] == "ffn":
                    ffn_phase(ename, e, M, pid, ph[1], ph[2])
                else:
                    attn_phase(ename, e, M, pid, ph[1])
                if ename == "sync":
                    M.wait_cnt(e, "dx", M.total("dx"))
                    M.wait_cnt(e, "st", M.total("st"))
                    M.wait_cnt(e, "sz0", M.total("sz0"))
                    M.wait_cnt(e, "sz1", M.total("sz1"))
                elif ename == "gpsimd":
                    for s_ in ("dw0", "dw1", "dwa", "dwb", "dwc", "dws",
                               "dwy", "ds0", "ds1", "dg0", "dg1",
                               "dy0", "dy1", "dy2", "dy3"):
                        M.wait_cnt(e, s_, M.total(s_))
                else:
                    if ename == "scalar":
                        M.wait_cnt(e, "dze0", M.total("dze0"))
                        M.wait_cnt(e, "dze1", M.total("dze1"))
                    M.wait_cnt(e, OWN[ename], M.total(OWN[ename]))
                M.inc(e.nop(), OWN[ename], ("pd", pid, ename))
            if ename == "sync":
                M.wait_cnt(e, "st", M.total("st"))
        return f

    emitters = [(n, make_emitter(n)) for n in ENGS]
    M = Plan()
    M.real = False
    M.counts = {}
    fake = FakeEng()
    for n, f in emitters:
        f(fake, M)
    M.real = True
    M.counts = {}
    for s in SEMS:
        M.sems[s] = es.enter_context(nc.semaphore(s))
    block = es.enter_context(nc.Block())
    for n, f in emitters:
        getattr(block, n)(lambda eng, f=f: f(eng, M))
    return nc


def _host_prep(cfg, L0_attn_inputs_embeds, L0_attn_Wqkv, L0_attn_bqkv,
               L0_attn_Wproj, L0_attn_bproj, L0_ffn_Wg, L0_ffn_W1, L0_ffn_b1,
               L0_ffn_W2, L0_ffn_b2):
    B, T, C, NL, E = cfg["B"], cfg["T"], cfg["C"], cfg["NL"], cfg["E"]
    FE, TT, FV = cfg["FE"], cfg["TT"], cfg["FV"]
    NCC, NTT, NCFE, NVQ, NFV = (cfg["NCC"], cfg["NTT"], cfg["NCFE"],
                                cfg["NVQ"], cfg["NFV"])
    f32 = np.float32
    ac = np.ascontiguousarray

    x = np.asarray(L0_attn_inputs_embeds, dtype=f32)
    Bfull = x.shape[0]
    xt = x.transpose(0, 2, 1)
    x_in = ac(xt.reshape(Bfull, NCC, 128, NTT, TT)
              .transpose(0, 3, 2, 1, 4)
              .reshape(Bfull, NTT, 128, NCC * TT))

    Wqkv = np.asarray(L0_attn_Wqkv, dtype=f32)
    WqkvT = Wqkv.transpose(0, 2, 1)
    wqh = ac(WqkvT[:, :, :C].reshape(NL, NCC, 128, C)
             .transpose(0, 2, 1, 3).reshape(NL, 128, NCC * C))
    wkvh = ac(WqkvT[:, :, C:].reshape(NL, NCC, 128, 2 * C)
              .transpose(0, 2, 1, 3).reshape(NL, 128, NCC * 2 * C))
    bqkv = np.asarray(L0_attn_bqkv, dtype=f32)
    bqqh = ac(bqkv[:, :C].reshape(NL, NCC, 128).transpose(0, 2, 1))
    bkvh = ac(bqkv[:, C:].reshape(NL, 1, 2 * C))
    Wproj = np.asarray(L0_attn_Wproj, dtype=f32)
    wprojh = ac(Wproj.transpose(0, 2, 1).reshape(NL, NCC, 128, C)
                .transpose(0, 2, 1, 3).reshape(NL, 128, NCC * C))
    bpjh = ac(np.asarray(L0_attn_bproj, dtype=f32)
              .reshape(NL, NCC, 128).transpose(0, 2, 1))

    Wg = np.asarray(L0_ffn_Wg, dtype=f32)
    wgh = ac(Wg.reshape(NCC, 128, E).transpose(1, 0, 2).reshape(128, NCC * E))
    W1 = np.asarray(L0_ffn_W1, dtype=f32)
    w1h = ac(W1.reshape(E, NCC, 128, NVQ, FV)
             .transpose(0, 3, 2, 1, 4).reshape(E, NVQ, 128, NCC * FV))
    b1h = ac(np.asarray(L0_ffn_b1, dtype=f32)
             .reshape(E, NCFE, 128).transpose(0, 2, 1))
    W2 = np.asarray(L0_ffn_W2, dtype=f32)
    w2h = ac(W2.reshape(E, NVQ, NFV, 128, C)
             .transpose(0, 1, 3, 2, 4).reshape(E, NVQ, 128, NFV * C))
    b2h = ac(np.asarray(L0_ffn_b2, dtype=f32))
    eyeh = np.eye(128, dtype=f32)
    ohbh = ac(np.kron(np.eye(E, dtype=f32), np.ones((1, 128), f32))
              .reshape(E, E * 128))
    sparse_inputs = {}
    if cfg.get("sparse"):
        CAP, NI, NCHK = cfg["CAP"], cfg["NI"], cfg["NCHK"]
        ltc = np.tril(np.ones((128, 128), f32), -1)
        ltc = ac(ltc.T * 0 + np.triu(np.ones((128, 128), f32), 1))
        selm = np.zeros((128, 128), f32)
        for pp in range(NCHK * E):
            for p2 in range(NCHK * E):
                if pp % E == p2 % E and pp // E < p2 // E:
                    selm[pp, p2] = 1.0
        segc = np.zeros((128, 1), f32)
        segc[:NCHK * E, 0] = (np.arange(NCHK * E) % E) * CAP
        tidh = ac((np.arange(NCHK)[None, :] * 128 +
                   np.arange(128)[:, None]).astype(np.int32))
        pf = np.zeros((NI, 2), np.int32)
        pf[:, 0] = cfg["T"]
        pf[:, 1] = 0
        pfh = ac(pf.reshape(128, NI * 2 // 128))
        sparse_inputs = dict(ltc_h=ltc, selm_h=selm, segc_h=segc,
                             tidh_h=tidh, pfh_h=pfh)

    shared = dict(wqh=wqh, wkvh=wkvh, bqqh=bqqh, bkvh=bkvh, wprojh=wprojh,
                  bpjh=bpjh, wgh=wgh, w1h=w1h, b1h=b1h, w2h=w2h, b2h=b2h,
                  eyeh=eyeh, ohbh=ohbh, **sparse_inputs)
    in_maps = []
    for c in range(cfg["n_cores"]):
        m = dict(shared)
        m["x_in"] = ac(x_in[c * B:(c + 1) * B])
        in_maps.append(m)
    return in_maps


def _host_post(cfg, results):
    B, T, C, NCC = cfg["B"], cfg["T"], cfg["C"], cfg["NCC"]
    outs = []
    for r in results:
        o = r["out"]
        o = o.reshape(B, 128, NCC, T).transpose(0, 2, 1, 3) \
             .reshape(B, C, T).transpose(0, 2, 1)
        outs.append(o)
    return np.ascontiguousarray(np.concatenate(outs, axis=0)).astype(
        np.float32)


def kernel(L0_attn_inputs_embeds, L0_attn_Wqkv, L0_attn_bqkv, L0_attn_Wproj,
           L0_attn_bproj, L0_ffn_Wg, L0_ffn_W1, L0_ffn_b1, L0_ffn_W2,
           L0_ffn_b2):
    cfg = make_cfg()
    in_maps = _host_prep(cfg, L0_attn_inputs_embeds, L0_attn_Wqkv,
                         L0_attn_bqkv, L0_attn_Wproj, L0_attn_bproj,
                         L0_ffn_Wg, L0_ffn_W1, L0_ffn_b1, L0_ffn_W2,
                         L0_ffn_b2)
    with contextlib.ExitStack() as es:
        nc = build_nc(cfg, es)
    res = run_bass_kernel_spmd(nc, in_maps, core_ids=list(range(
        cfg["n_cores"])))
    return _host_post(cfg, res.results)


# version 17 (final)
# speedup vs baseline: 1.0470x; 1.0211x over previous
import contextlib
import numpy as np
import concourse.bass as bass
import concourse.mybir as mybir
from concourse.bass_utils import run_bass_kernel_spmd

F32 = mybir.dt.float32
BF16 = mybir.dt.bfloat16
AF = mybir.ActivationFunctionType
ALU = mybir.AluOpType
AX = mybir.AxisListType


class _FakeInstr:
    __slots__ = ()

    def then_inc(self, sem, k=1):
        return self


class FakeEng:
    def __getattr__(self, name):
        def _f(*a, **k):
            return _FakeInstr()
        return _f


class Plan:
    def __init__(self):
        self.counts = {}
        self.marks = {}
        self.real = False
        self.sems = {}

    def inc(self, instr, semname, key=None, k=1):
        c = self.counts.get(semname, 0) + k
        self.counts[semname] = c
        if key is not None:
            if not self.real:
                assert key not in self.marks, f"dup milestone {key}"
                self.marks[key] = (semname, c)
            else:
                assert self.marks[key] == (semname, c), \
                    f"pass mismatch {key}: {self.marks[key]} vs {(semname, c)}"
        if self.real:
            instr.then_inc(self.sems[semname], k)
        return c

    def wait(self, eng, key):
        if self.real:
            semname, c = self.marks[key]
            eng.wait_ge(self.sems[semname], c)

    def wait_cnt(self, eng, semname, c):
        if self.real:
            eng.wait_ge(self.sems[semname], c)

    def total(self, semname):
        return self.counts.get(semname, 0)


SEMS = ["dx", "st", "dw0", "dw1", "dwa", "dwb", "dwc", "cg",
        "ct", "cs", "cv", "by", "dws", "dwy", "sz0", "sz1", "ds0", "ds1",
        "dg0", "dg1", "dy0", "dy1", "dy2", "dy3", "dze0", "dze1"]
ENGS = ["sync", "gpsimd", "tensor", "scalar", "vector"]
OWN = {"sync": "by", "gpsimd": "cg", "tensor": "ct", "scalar": "cs",
       "vector": "cv"}


def make_cfg(B=2, T=2048, C=1024, NL=12, H=16, E=8, FE=4096, TT=512, FV=512,
             n_cores=8):
    DH = 64
    assert C == H * DH and C % 128 == 0 and H % 2 == 0
    assert T % TT == 0 and TT % 128 == 0 and FE % FV == 0 and FV % 128 == 0
    assert (2 * C) % 512 == 0
    cfg = dict(B=B, T=T, C=C, NL=NL, H=H, DH=DH, E=E, FE=FE, TT=TT, FV=FV,
               n_cores=n_cores,
               NCC=C // 128, NTT=T // TT, NSUB=TT // 128, NCFE=FE // 128,
               NVQ=FE // FV, NFV=FV // 128, NKVG=(2 * C) // 512)
    cfg["NVE"] = E * cfg["NVQ"]
    cfg["sparse"] = True
    cfg["CAP"] = 1024
    assert cfg["CAP"] % TT == 0
    cfg["NST"] = cfg["CAP"] // TT
    ppe, nvq = (cfg["CAP"] // TT) * cfg["NSUB"], cfg["NVQ"]
    assert ppe % nvq == 0 and ppe // nvq <= 2
    cfg["NI"] = E * cfg["CAP"]
    cfg["NCHK"] = T // 128
    assert cfg["NCHK"] * E <= 128 and (cfg["NI"] * 2) % 128 == 0
    return cfg


def build_nc(cfg, es):
    B, T, C, NL, H, DH, E = (cfg[k] for k in
                             ("B", "T", "C", "NL", "H", "DH", "E"))
    FE, TT, FV = cfg["FE"], cfg["TT"], cfg["FV"]
    NCC, NTT, NSUB, NCFE = cfg["NCC"], cfg["NTT"], cfg["NSUB"], cfg["NCFE"]
    NVQ, NVE, NFV, NKVG = cfg["NVQ"], cfg["NVE"], cfg["NFV"], cfg["NKVG"]
    SP = min(8, H)
    SPARSE = cfg.get("sparse", False)
    CAP, NST, NI, NCHK = cfg["CAP"], cfg["NST"], cfg["NI"], cfg["NCHK"]
    ZROW = T
    BIGP = 1.0e5
    EPS_LN, EPS_GATE = 1e-5, 1e-9

    nc = bass.Bass("TRN2", target_bir_lowering=False, debug=False)
    dt = nc.dram_tensor
    x_in = dt("x_in", [B, NTT, 128, NCC * TT], F32, kind="ExternalInput").ap()
    wqh_d = dt("wqh", [NL, 128, NCC * C], F32, kind="ExternalInput").ap()
    wkvh_d = dt("wkvh", [NL, 128, NCC * 2 * C], F32,
                kind="ExternalInput").ap()
    bqqh = dt("bqqh", [NL, 128, NCC], F32, kind="ExternalInput").ap()
    bkvh = dt("bkvh", [NL, 1, 2 * C], F32, kind="ExternalInput").ap()
    wprojh = dt("wprojh", [NL, 128, NCC * C], F32, kind="ExternalInput").ap()
    bpjh = dt("bpjh", [NL, 128, NCC], F32, kind="ExternalInput").ap()
    wgh = dt("wgh", [128, NCC * E], F32, kind="ExternalInput").ap()
    w1h = dt("w1h", [E, NVQ, 128, NCC * FV], F32, kind="ExternalInput").ap()
    b1h = dt("b1h", [E, 128, NCFE], F32, kind="ExternalInput").ap()
    w2h = dt("w2h", [E, NVQ, 128, NFV * C], F32, kind="ExternalInput").ap()
    b2h = dt("b2h", [E, C], F32, kind="ExternalInput").ap()
    eyeh = dt("eyeh", [128, 128], F32, kind="ExternalInput").ap()
    ohbh = dt("ohbh", [E, E * 128], F32, kind="ExternalInput").ap()
    out_d = dt("out", [B, 128, NCC * T], F32, kind="ExternalOutput").ap()
    x1s = dt("x1s", [B, 128, NCC * T], F32, kind="Internal").ap()
    if SPARSE:
        ltc_h = dt("ltc_h", [128, 128], F32, kind="ExternalInput").ap()
        selm_h = dt("selm_h", [128, 128], F32, kind="ExternalInput").ap()
        segc_h = dt("segc_h", [128, 1], F32, kind="ExternalInput").ap()
        tidh_h = dt("tidh_h", [128, NCHK], mybir.dt.int32,
                    kind="ExternalInput").ap()
        pfh_h = dt("pfh_h", [128, NI * 2 // 128], mybir.dt.int32,
                   kind="ExternalInput").ap()
        zt_s = dt("zt_s", [T + 1, C], BF16, kind="Internal").ap()
        yinst_s = dt("yinst_s", [NI, C], F32, kind="Internal").ap()
        idxpw_s = dt("idxpw_s", [NI + 128, 2], mybir.dt.int32,
                     kind="Internal").ap()
        zc_s = dt("zc_s", [E, NST, 128, NSUB * NCC * 128], BF16,
                  kind="Internal").ap()
    dbg = (dt("dbg", [B, 128, NCC * T], F32, kind="ExternalOutput").ap()
           if cfg.get("debug") else None)

    yx = es.enter_context(nc.sbuf_tensor("yx", [128, NCC * T], F32))
    eyes = es.enter_context(nc.sbuf_tensor("eyes", [128, 128], F32))
    eyeb = es.enter_context(nc.sbuf_tensor("eyeb", [128, 128], BF16))
    ohbs = es.enter_context(nc.sbuf_tensor("ohbs", [E, E * 128], BF16))
    b2s = es.enter_context(nc.sbuf_tensor("b2s", [E, C], BF16))
    wgs = es.enter_context(nc.sbuf_tensor("wgs", [128, NCC * E], BF16))
    onec = es.enter_context(nc.sbuf_tensor("onec", [128, 1], BF16))
    oner = es.enter_context(nc.sbuf_tensor("oner", [1, 128], BF16))
    ssb = es.enter_context(nc.sbuf_tensor("ssb", [64, H * DH], F32))
    sbf2 = es.enter_context(nc.sbuf_tensor("sbf2", [128, NCC * 128], BF16))
    pwtb = es.enter_context(nc.sbuf_tensor("pwtb", [E, T], BF16))
    bkvs = es.enter_context(nc.sbuf_tensor("bkvs", [1, 2 * 2 * C], BF16))
    bqs = es.enter_context(nc.sbuf_tensor("bqs", [128, 2 * NCC], F32))
    bps = es.enter_context(nc.sbuf_tensor("bps", [128, 2 * NCC], F32))
    b1s = es.enter_context(nc.sbuf_tensor("b1s", [128, 2 * NCFE], F32))
    pfull = es.enter_context(nc.sbuf_tensor("pfull", [128, E], F32))
    ppm = es.enter_context(nc.sbuf_tensor("ppm", [128, 1], F32))
    ppex = es.enter_context(nc.sbuf_tensor("ppex", [128, E], F32))
    ppa = es.enter_context(nc.sbuf_tensor("ppa", [128, E], F32))
    ppb = es.enter_context(nc.sbuf_tensor("ppb", [128, E], F32))
    ppr = es.enter_context(nc.sbuf_tensor("ppr", [128, 1], F32))
    I32 = mybir.dt.int32
    if SPARSE:
        ltc = es.enter_context(nc.sbuf_tensor("ltc", [128, 128], BF16))
        selm = es.enter_context(nc.sbuf_tensor("selm", [128, 128], BF16))
        segc = es.enter_context(nc.sbuf_tensor("segc", [128, 1], F32))
        tidh = es.enter_context(nc.sbuf_tensor("tidh", [128, NCHK], I32))
        pfh = es.enter_context(nc.sbuf_tensor("pfh", [128, NI * 2 // 128],
                                              I32))
        c64r = es.enter_context(nc.sbuf_tensor("c64r", [1, 128], BF16))
        zrow0 = es.enter_context(nc.sbuf_tensor("zrow0", [1, C], BF16))
        maskb = es.enter_context(nc.sbuf_tensor("maskb", [128, NCHK * E],
                                                BF16))
        maskf = es.enter_context(nc.sbuf_tensor("maskf", [128, NCHK * E],
                                                F32))
        eqf = es.enter_context(nc.sbuf_tensor("eqf", [128, NCHK * E], F32))
        pwm = es.enter_context(nc.sbuf_tensor("pwm", [128, NCHK * E], F32))
        crow = es.enter_context(nc.sbuf_tensor("crow", [1, 128], F32))
        ccb = es.enter_context(nc.sbuf_tensor("ccb", [128, 1], BF16))
        cbt = es.enter_context(nc.sbuf_tensor("cbt", [128, 3], F32))
        qsr = es.enter_context(nc.sbuf_tensor("qsr", [1, 2 * 128], BF16))
        posb = es.enter_context(nc.sbuf_tensor("posb", [128, 2 * NCHK], I32))
        sctg = es.enter_context(nc.sbuf_tensor("sctg", [128, 2 * 2], I32))
        pp1 = es.enter_context(nc.sbuf_tensor("pp1", [128, E], F32))
        pp2 = es.enter_context(nc.sbuf_tensor("pp2", [128, E], F32))
        pp3 = es.enter_context(nc.sbuf_tensor("pp3", [128, 1], F32))
        idxt = es.enter_context(nc.sbuf_tensor("idxt", [128, 2 * 2], I32))
        pwsb = es.enter_context(nc.sbuf_tensor("pwsb", [128, E * NST * NSUB],
                                               F32))
        dmpc = es.enter_context(nc.sbuf_tensor("dmpc", [128, 1], F32))
        ppA = es.enter_context(nc.sbuf_tensor("ppA", [128, NCHK * E], F32))
        ppB = es.enter_context(nc.sbuf_tensor("ppB", [128, NCHK * E], F32))
        pw12 = es.enter_context(nc.sbuf_tensor("pw12", [128, 2 * NCHK], F32))
        sctgA = es.enter_context(nc.sbuf_tensor("sctgA", [128, 4 * NCHK],
                                                I32))

    W1C, W2C = NCC * FV, NFV * C
    ZHC, HSC = NCC * T, NFV * TT
    XSC = 2 * NCC * TT
    KVC, QTC, YTC = 2 * 2 * C, NCC * TT, NCC * TT
    ROWC, RBFC = 2 * 4 * TT, 2 * TT
    WKVC, WQC, WPC, XBC = NCC * 2 * C, NCC * C, NCC * C, NCC * TT
    YAC = 2 * NCC * CAP if SPARSE else 0
    ffn_cols = ZHC + 2 * (W1C + W2C) + YAC + HSC + 2 * TT + 2 * 2 * TT
    attn_cols = WKVC + WQC + WPC + 2 * XBC + KVC + QTC + YTC + 2 * 2 * TT
    ffn_ovl = XSC + ROWC + RBFC + XBC + XBC
    if SPARSE:
        ffn_ovl += 2 * C + 2 * (2 * C)
    attn_ovl = ROWC + RBFC
    ACOLS = max(ffn_cols, attn_cols)
    ffn_ovl_fits = ffn_ovl <= 2 * (W1C + W2C) + YAC
    attn_ovl_fits = attn_ovl <= WKVC
    extra = (0 if ffn_ovl_fits else ffn_ovl) + (0 if attn_ovl_fits else
                                                attn_ovl)
    ACOLS += extra
    arena = es.enter_context(nc.sbuf_tensor("arena", [128, ACOLS], BF16))

    def carve(off, cols):
        assert off + cols <= ACOLS
        return arena[:, off:off + cols], off + cols

    from concourse.ap import AP as _AP

    def sub_ap(slice_ap, extra_off, dims):
        return _AP(tensor=slice_ap.tensor,
                   offset=slice_ap.offset + extra_off,
                   ap=[[int(slice_ap.ap[0][0]), int(slice_ap.ap[0][1])]] +
                      [[int(s), int(n)] for s, n in dims])

    o = 0
    zh, o = carve(o, ZHC)
    wdb_off = o
    w1b, w2b = [], []
    for i in range(2):
        a, o = carve(o, W1C); w1b.append(a)
        a, o = carve(o, W2C); w2b.append(a)
    if SPARSE:
        a, o = carve(o, YAC); yacc_f = a.bitcast(F32)
    hsb, o = carve(o, HSC)
    a, o = carve(o, 2 * TT); tmpv_f = a.bitcast(F32)
    a, o = carve(o, 4 * TT); pbs_f = a.bitcast(F32)
    oo = wdb_off if ffn_ovl_fits else max(ffn_cols, attn_cols)
    a, oo = carve(oo, XSC); xstr_f = a.bitcast(F32)
    a, oo = carve(oo, ROWC); rows_ff = a[0:1, :].bitcast(F32)
    rbf_ff, oo = carve(oo, RBFC); rbf_ff = rbf_ff[0:1, :]
    xbb_ff, oo = carve(oo, XBC)
    xqq_ff, oo = carve(oo, XBC)

    if SPARSE:
        NSTl = cfg["NST"]
        zneed = NSTl * NCC * TT + 2 * (2 * C) + 2 * C
        if zneed <= ZHC:
            zo = 0
        else:
            zo = ACOLS
            arena2 = es.enter_context(
                nc.sbuf_tensor("arena2pad", [128, zneed + 2 * (2 * C)], BF16))
            def carve2(off, cols, _base=zo):
                return arena2[:, off - _base:off - _base + cols], off + cols
        _carve = carve if zneed <= ZHC else carve2
        zeS = []
        for i in range(NSTl):
            a, zo = _carve(zo, NCC * TT); zeS.append(a)
        ytr = []
        for i in range(2):
            a, zo = _carve(zo, 2 * C); ytr.append(a.bitcast(F32))
        zgk = []
        for i in range(2):
            a, zo = _carve(zo, C); zgk.append(a)
        if zneed <= ZHC and ZHC - zneed >= 2 * C and C <= 2 * TT:
            a, zo = _carve(zo, 2 * C); ytr.append(a.bitcast(F32))
            ytr.append(pbs_f[:, 0:C])
        else:
            a, zo = _carve(zo, 2 * C); ytr.append(a.bitcast(F32))
            a, zo = _carve(zo, 2 * C); ytr.append(a.bitcast(F32))
        ztma = []
        for ii in range(2):
            a2, oo = carve(oo, C); ztma.append(a2)
        ybk = []
        for i in range(2):
            a, oo = carve(oo, 2 * C); ybk.append(a.bitcast(F32))
        assert oo <= wdb_off + 2 * (W1C + W2C) + YAC or not ffn_ovl_fits

    o = 0
    wkv, o = carve(o, WKVC)
    wqv, o = carve(o, WQC)
    wpv, o = carve(o, WPC)
    xbA = []
    for i in range(2):
        a, o = carve(o, XBC); xbA.append(a)
    kvt, o = carve(o, KVC)
    qTt, o = carve(o, QTC)
    yTt, o = carve(o, YTC)
    a, o = carve(o, 2 * 2 * TT); tmpa_f = a.bitcast(F32)
    oo = 0 if attn_ovl_fits else \
        (max(ffn_cols, attn_cols) + (0 if ffn_ovl_fits else ffn_ovl))
    a, oo = carve(oo, ROWC); rows_at = a[0:1, :].bitcast(F32)
    rbf_at, oo = carve(oo, RBFC); rbf_at = rbf_at[0:1, :]
    xa_f = xbA[0].bitcast(F32)
    assert xa_f.shape[1] >= T // 2 or True

    ps = [es.enter_context(nc.psum_tensor(f"ps{i}", [128, 512], F32))
          for i in range(8)]
    HB = [6, 7, 2]
    HBS = [6, 7, 3]
    YB = [0, 1, 3]
    YBS = [0, 1]
    KB = [0, 1, 4, 5]
    QB = [4, 5, 0]
    TB = [6, 7, 1]
    PB = [4, 5, 0]

    R_NEGMU, R_SCR, R_RSTD, R_BROW = 0, 1, 2, 3

    def row(view, i, n=1):
        return view[0:1, i * TT:(i + n) * TT]

    NST_ = cfg.get("NST", 1)

    def gidx(k):
        e2, r = divmod(k, NST_ * NSUB)
        t2, s2 = divmod(r, NSUB)
        return (e2, t2, s2)

    def psl(base, qi):
        PPE_ = NST_ * NSUB
        return range(base + qi * PPE_ // NVQ, base + (qi + 1) * PPE_ // NVQ)

    phases = [("setup",)]
    for b in range(B):
        phases += [("ffn", b, 0), ("attn", b), ("ffn", b, 1)]
    if cfg.get("only_ffn1"):
        phases = [("setup",)] + [("ffn", b, 0) for b in range(B)]

    def ln_tile(ename, e, M, p, tag, t, xsrc, zdst, rowsv, rbfv, castb, sqb,
                streamed, tmpb, z_gate=None, s1_gate=None, bc_gate=None):
        K = lambda name: (p, tag + name, t)
        if ename == "tensor":
            M.wait(e, K("xcast"))
            if s1_gate is not None:
                M.wait(e, s1_gate)
            if t >= 1:
                M.wait(e, (p, tag + "statsread", t - 1))
            for cc in range(NCC):
                mm = e.matmul(ps[0][0:1, 0:TT], onec[:],
                              castb[:, cc * TT:(cc + 1) * TT],
                              start=(cc == 0), stop=(cc == NCC - 1))
            M.inc(mm, "ct", K("s1"))
            M.wait(e, K("xsq"))
            for cc in range(NCC):
                mm = e.matmul(ps[1][0:1, 0:TT], onec[:],
                              sqb[:, cc * TT:(cc + 1) * TT],
                              start=(cc == 0), stop=(cc == NCC - 1))
            M.inc(mm, "ct", K("s2"))
            M.wait(e, K("rbf"))
            if t >= 1:
                M.wait(e, (p, tag + "ztile", t - 1))
            if bc_gate is not None:
                M.wait(e, bc_gate)
            e.matmul(ps[2][:, 0:TT], oner[:], rbfv[0:1, 0:TT],
                     start=True, stop=True)
            M.inc(e.matmul(ps[3][:, 0:TT], oner[:], rbfv[0:1, TT:2 * TT],
                           start=True, stop=True), "ct", K("bc"))
        elif ename == "scalar":
            if streamed:
                M.wait(e, K("xld"))
            if t >= 1:
                M.wait(e, (p, tag + "s1", t - 1))
            for cc in range(NCC):
                M.inc(e.copy(out=castb[:, cc * TT:(cc + 1) * TT],
                             in_=xsrc(cc)), "cs",
                      K("xcast") if cc == NCC - 1 else None)
            M.wait(e, K("var"))
            c = M.inc(e.activation(out=row(rowsv, R_RSTD),
                                   in_=row(rowsv, R_SCR), func=AF.Ln), "cs")
            M.wait_cnt(e, "cs", c)
            M.inc(e.activation(out=row(rowsv, R_RSTD), in_=row(rowsv, R_RSTD),
                               func=AF.Exp, scale=-0.5), "cs", K("rstd"))
            M.wait(e, K("brow"))
            M.inc(e.copy(out=rbfv[0:1, 0:2 * TT], in_=row(rowsv, R_RSTD, 2)),
                  "cs", K("rbf"))
        elif ename == "gpsimd":
            if streamed:
                M.wait(e, K("xld"))
            if t >= 1:
                M.wait(e, (p, tag + "s2", t - 1))
            for cc in range(NCC):
                M.inc(e.tensor_tensor(out=sqb[:, cc * TT:(cc + 1) * TT],
                                      in0=xsrc(cc), in1=xsrc(cc),
                                      op=ALU.mult), "cg",
                      K("xsq") if cc == NCC - 1 else None)
        elif ename == "vector":
            def ch(op, key=None):
                c = M.inc(op, "cv", key)
                M.wait_cnt(e, "cv", c)
            M.wait(e, K("s1"))
            M.wait(e, K("s2"))
            ch(e.tensor_scalar(out=row(rowsv, R_NEGMU), in0=ps[0][0:1, 0:TT],
                               scalar1=-1.0 / C, scalar2=None, op0=ALU.mult))
            ch(e.tensor_scalar(out=row(rowsv, R_SCR), in0=ps[1][0:1, 0:TT],
                               scalar1=1.0 / C, scalar2=EPS_LN,
                               op0=ALU.mult, op1=ALU.add),
               (p, tag + "statsread", t))
            ch(e.tensor_tensor(out=row(rowsv, R_RSTD),
                               in0=row(rowsv, R_NEGMU),
                               in1=row(rowsv, R_NEGMU), op=ALU.mult))
            ch(e.tensor_tensor(out=row(rowsv, R_SCR), in0=row(rowsv, R_SCR),
                               in1=row(rowsv, R_RSTD), op=ALU.subtract),
               K("var"))
            M.wait(e, K("rstd"))
            ch(e.tensor_tensor(out=row(rowsv, R_BROW),
                               in0=row(rowsv, R_NEGMU),
                               in1=row(rowsv, R_RSTD), op=ALU.mult),
               K("brow"))
            M.wait(e, K("bc"))
            if z_gate is not None:
                M.wait(e, z_gate)
            for cc in range(NCC):
                ch(e.tensor_tensor(out=tmpb, in0=xsrc(cc),
                                   in1=ps[2][:, 0:TT], op=ALU.mult),
                   K("xfree") if (cc == NCC - 1 and streamed) else None)
                ch(e.tensor_tensor(out=zdst(cc), in0=tmpb,
                                   in1=ps[3][:, 0:TT], op=ALU.add),
                   K("ztile") if cc == NCC - 1 else None)

    def ffn_phase(ename, e, M, p, b, call):
        from_yx = (call == 1)
        NG = NTT * NSUB

        def xsrc(t):
            def f(cc):
                if from_yx:
                    return yx[:, cc * T + t * TT:cc * T + (t + 1) * TT]
                return xstr_f[:, cc * TT:(cc + 1) * TT]
            return f

        def zdst(t):
            def f(cc):
                return zh[:, cc * T + t * TT:cc * T + (t + 1) * TT]
            return f

        def lng(t):
            kw = {}
            if SPARSE:
                if t >= 1:
                    kw["bc_gate"] = (p, "ztmev", t * NSUB - 1)
            return kw

        NTI = NST if SPARSE else NTT

        if ename == "sync":
            if from_yx and dbg is not None:
                M.inc(e.dma_start(out=dbg[b], in_=yx[:]), "st",
                      (p, "dbgst"), 16)
            for t in range(NTT):
                if not from_yx:
                    if t >= 1:
                        M.wait(e, (p, "Lxfree", t - 1))
                    M.inc(e.dma_start(out=xstr_f[:, :], in_=x_in[b, t]),
                          "dx", (p, "Lxld", t), 16)
                if SPARSE:
                    for s_ in range(NSUB):
                        zi = t * NSUB + s_
                        M.wait(e, (p, "ztmev", zi))
                        M.inc(e.dma_start(
                            out=zt_s[zi * 128:(zi + 1) * 128, :],
                            in_=ztma[zi % 2][:, :]),
                            f"sz{zi % 2}", (p, "ztst", zi), 16)
            if not from_yx:
                for t in range(NTT):
                    if t == 0:
                        M.wait(e, (p, "Lxfree", NTT - 1))
                    else:
                        M.wait(e, (p, "yi", (t - 1) * NCC + NCC - 1))
                    M.inc(e.dma_start(out=xstr_f[:, :], in_=x_in[b, t]),
                          "dx", (p, "xld2", t), 16)
            if SPARSE:
                PPE = NST * NSUB
                def emit_zbst(kblock):
                    for k in kblock:
                        ei2, t2, sub2 = gidx(k)
                        M.wait(e, (p, "zbev_v", k))
                        M.inc(e.dma_start(
                            out=zc_s[ei2, t2][:, sub2 * NCC * 128:
                                              (sub2 + 1) * NCC * 128],
                            in_=zgk[k % 2][:, 0:NCC * 128]),
                            f"sz{k % 2}", (p, "zbst", ei2, t2, sub2), 16)
                emit_zbst(range(0, 2 * PPE))
                for i in range(NVE):
                    ei, qi = i // NVQ, i % NVQ
                    if ei + 2 < E:
                        emit_zbst(psl((ei + 2) * PPE, qi))
            if call == 1:
                M.wait(e, (p, "ffn_done"))
                M.inc(e.dma_start(out=out_d[b], in_=yx[:]), "st",
                      (p, "outst"), 16)
            return

        if ename == "gpsimd":
            for t in range(NTT):
                ln_tile("gpsimd", e, M, p, "L", t, xsrc(t), zdst(t),
                        rows_ff, rbf_ff, xbb_ff, xqq_ff, not from_yx,
                        tmpv_f[:, 0:TT], **lng(t))
            if SPARSE:
                M.inc(e.dma_start(out=idxpw_s[0:NI, :].rearrange(
                    "a b -> (a b)").rearrange("(x y) -> x y", x=128),
                    in_=pfh[:]), "dws", (p, "pfil"), 16)
                M.wait(e, (p, "pfil"))
                nds = [0, 0]
                for half2 in range(2):
                    M.wait(e, (p, "sct2", half2))
                    for c in range(NCHK):
                        s2 = half2 * NCHK + c
                        if s2 >= 1:
                            M.wait_cnt(e, "ds0", 16 * nds[0])
                            M.wait_cnt(e, "ds1", 16 * nds[1])
                        cds = M.inc(e.indirect_dma_start(
                            out=idxpw_s[:, :],
                            out_offset=bass.IndirectOffsetOnAxis(
                                posb[:, s2:s2 + 1], 0),
                            in_=sctgA[:, s2 * 2:s2 * 2 + 2],
                            in_offset=None),
                            f"ds{s2 % 2}", (p, "sctd", c, half2), 16)
                        nds[s2 % 2] = cds // 16
            def emit_zgather(kblock):
                for k in kblock:
                    ei2, t2, sub2 = gidx(k)
                    gsl = k % 2
                    if k < 2:
                        M.wait(e, (p, "ztst", NG - 1))
                        M.wait(e, (p, "ztst", NG - 2))
                        M.wait(e, (p, "sctd", NCHK - 1, 1))
                        M.wait(e, (p, "sctd", NCHK - 2, 1))
                    else:
                        M.wait(e, (p, "zbst", *gidx(k - 2)))
                        M.wait(e, (p, "pwx", *gidx(k - 2)))
                    r0 = ei2 * CAP + t2 * TT + sub2 * 128
                    sm = f"dg{k % 2}"
                    cidx = M.inc(e.dma_start(
                        out=idxt[:, gsl * 2:gsl * 2 + 2],
                        in_=idxpw_s[r0:r0 + 128, :]), sm, None, 16)
                    M.wait_cnt(e, sm, cidx)
                    M.inc(e.indirect_dma_start(
                        out=zgk[gsl][:, :], out_offset=None,
                        in_=zt_s[:, :],
                        in_offset=bass.IndirectOffsetOnAxis(
                            idxt[:, gsl * 2:gsl * 2 + 1], 0)),
                        sm, (p, "zg", ei2, t2, sub2), 16)
                    if k >= 1:
                        kp = k - 1
                        ep, tp, sp = gidx(kp)
                        gslp = kp % 2
                        M.wait(e, (p, "zg", ep, tp, sp))
                        M.inc(e.tensor_copy(
                            out=pwsb[:, kp:kp + 1],
                            in_=idxt[:, gslp * 2 + 1:gslp * 2 + 2]
                            .bitcast(F32)), "cg", (p, "pwx", ep, tp, sp))
                    if k == E * NST * NSUB - 1:
                        M.wait(e, (p, "zg", ei2, t2, sub2))
                        M.inc(e.tensor_copy(
                            out=pwsb[:, k:k + 1],
                            in_=idxt[:, gsl * 2 + 1:gsl * 2 + 2]
                            .bitcast(F32)), "cg", (p, "pwx", ei2, t2, sub2))
            PPE = NST * NSUB
            if SPARSE:
                emit_zgather(range(0, 2 * PPE))
            def emit_yst1(ei2, j):
                M.wait(e, (p, "ysclh", ei2, j, 1))
                r0 = ei2 * CAP + j * 128
                M.inc(e.dma_start(
                    out=yinst_s[r0:r0 + 128, :],
                    in_=ytr[j % 4][:, 0:C]),
                    f"dy{j % 4}", (p, "yst", ei2, j), 16)
            for i in range(NVE):
                ei, qi = i // NVQ, i % NVQ
                par = i % 2
                if i < 2:
                    M.wait(e, (p, "yi", NTT * NCC - 1))
                else:
                    M.wait(e, (p, "ymm_last", i - 2))
                M.inc(e.dma_start(out=w1b[par][:, :], in_=w1h[ei, qi]),
                      f"dw{par}", None, 16)
                if qi == 0:
                    M.inc(e.dma_start(
                        out=b1s[:, (ei % 2) * NCFE:(ei % 2 + 1) * NCFE],
                        in_=b1h[ei]), f"dw{par}", None, 16)
                M.inc(e.dma_start(out=w2b[par][:, :], in_=w2h[ei, qi]),
                      f"dw{par}", (p, "wv", i), 16)
                if SPARSE and qi == 0 and ei >= 1:
                    for j in range(PPE):
                        emit_yst1(ei - 1, j)
                if SPARSE and ei + 2 < E:
                    emit_zgather(psl((ei + 2) * PPE, qi))
                if SPARSE and i == NVE - 1:
                    for j in range(PPE):
                        emit_yst1(E - 1, j)
            if SPARSE:
                for s_ in ("dy0", "dy1", "dy2", "dy3"):
                    M.wait_cnt(e, s_, M.total(s_))
                for c in range(NCHK):
                    if c >= 1:
                        M.wait(e, (p, "ybtp", c - 1, NCC - 1))
                    for half2 in range(2):
                        M.inc(e.indirect_dma_start(
                            out=ybk[half2][:, 0:C], out_offset=None,
                            in_=yinst_s[:, :],
                            in_offset=bass.IndirectOffsetOnAxis(
                                posb[:, half2 * NCHK + c:
                                     half2 * NCHK + c + 1], 0)),
                            f"dg{c % 2}",
                            (p, "ybg", c) if half2 == 1 else None, 16)
            return

        if ename == "tensor":
            for t in range(NTT):
                ln_tile("tensor", e, M, p, "L", t, xsrc(t), zdst(t),
                        rows_ff, rbf_ff, xbb_ff, xqq_ff, not from_yx,
                        tmpv_f[:, 0:TT], **lng(t))
                M.wait(e, (p, "Lztile", t))
                for s in range(NSUB):
                    g = t * NSUB + s
                    if g >= 4:
                        M.wait(e, (p, "pexp", g - 4))
                    for cc in range(NCC):
                        mm = e.matmul(
                            ps[4 + g % 4][:, 0:E],
                            zh[:, cc * T + t * TT + s * 128:
                               cc * T + t * TT + (s + 1) * 128],
                            wgs[:, cc * E:(cc + 1) * E],
                            start=(cc == 0), stop=(cc == NCC - 1))
                    M.inc(mm, "ct", (p, "lg", g))
                    M.wait(e, (p, "pw", g))
                    if g >= 2:
                        M.wait(e, (p, "pwtev", g - 2))
                    M.inc(e.transpose(ps[6 + g % 2][0:E, 128:256],
                                      pfull[:, 0:E], eyes[:]),
                          "ct", (p, "pwt_ps", g))
                    if SPARSE:
                        if g >= 2:
                            M.wait(e, (p, "crow", g - 2))
                        M.wait(e, (p, "maskw", g))
                        M.inc(e.matmul(ps[6 + g % 2][0:1, 384:384 + E],
                                       onec[:],
                                       maskb[:, g * E:(g + 1) * E],
                                       start=True, stop=True),
                              "ct", (p, "cnt", g))
                if SPARSE:
                    for s in range(NSUB):
                        zi = t * NSUB + s
                        if zi >= 2:
                            M.wait(e, (p, "ztmev", zi - 2))
                        for cc in range(NCC):
                            mm = e.transpose(
                                ps[2 + zi % 2].bitcast(BF16)[
                                    :, cc * 128:(cc + 1) * 128],
                                zh[:, cc * T + t * TT + s * 128:
                                   cc * T + t * TT + (s + 1) * 128],
                                eyeb[:])
                        M.inc(mm, "ct", (p, "ztp", zi))
            if SPARSE:
                M.wait(e, (p, "crow", NG - 1))
                M.inc(e.transpose(ps[4][0:128, 0:1], crow[0:1, 0:128],
                                  eyes[0:1, 0:1]), "ct", (p, "ccolp"))
                M.wait(e, (p, "ccbev"))
                M.inc(e.matmul(ps[4][:, 0:1], selm[:, :], ccb[:, 0:1],
                               start=True, stop=True), "ct", (p, "cbp"))
                M.wait(e, (p, "qscols"))
                M.inc(e.transpose(ps[4][0:1, 0:128], cbt[:, 1:2], eyes[:]),
                      "ct", (p, "qtp"))
                M.wait(e, (p, "qev2"))
                M.inc(e.transpose(ps[4][0:1, 0:128], cbt[:, 2:3], eyes[:]),
                      "ct", (p, "stp"))
                M.wait(e, (p, "qsrev"))
                M.wait(e, (p, "pexp", NG - 2))
                e.matmul(ps[6][:, 0:NCHK * E], ltc[:, :],
                         maskb[:, 0:NCHK * E], start=True, stop=False)
                e.matmul(ps[6][:, 0:NCHK * E], c64r[:, :],
                         qsr[0:1, 0:NCHK * E], start=False, stop=False)
                M.inc(e.matmul(ps[6][:, 0:NCHK * E], oner[:, :],
                               qsr[0:1, 128:128 + NCHK * E],
                               start=False, stop=True),
                      "ct", (p, "pospAll"))
            M.wait(e, (p, "pwtev", NG - 1))
            M.wait(e, (p, "Lstatsread", NTT - 1))
            if SPARSE:
                M.wait(e, (p, "crow", NG - 1))
            for t in range(NTT):
                for cc in range(NCC):
                    g = t * NCC + cc
                    if g >= 2:
                        M.wait(e, (p, "yi", g - 2))
                    M.inc(e.matmul(ps[g % 2][:, 0:TT],
                                   b2s[:, cc * 128:(cc + 1) * 128],
                                   pwtb[:, t * TT:(t + 1) * TT],
                                   start=True, stop=True),
                          "ct", (p, "bm", g))
            def emit_ztr2(e, kblock):
                for k in kblock:
                    ei2, t2, sub2 = gidx(k)
                    gsl = k % 2
                    pro = k < 2 * PPE
                    bank = 2 + (k % 2 if pro else 0)
                    M.wait(e, (p, "zg", ei2, t2, sub2))
                    if k < 2:
                        M.wait(e, (p, "ztmev", NG - 2 + k))
                    elif pro:
                        M.wait(e, (p, "zbev_v", k - 2))
                    else:
                        M.wait(e, (p, "zbev_v", k - 1))
                    for cc in range(NCC):
                        mm = e.transpose(
                            ps[bank].bitcast(BF16)[
                                :, cc * 128:(cc + 1) * 128],
                            zgk[gsl][:, cc * 128:(cc + 1) * 128],
                            eyeb[:])
                    M.inc(mm, "ct", (p, "ztr2", ei2, t2, sub2))
            PPE = NST * NSUB
            if SPARSE:
                emit_ztr2(e, range(0, 2 * PPE))
            for i in range(NVE):
                par = i % 2
                ei, qi = i // NVQ, i % NVQ
                M.wait(e, (p, "wv", i))
                for t in range(NTI):
                    g = i * NTI + t
                    if not SPARSE:
                        if g >= 2:
                            M.wait(e, (p, "pbev", g - 2))
                        M.inc(e.matmul(ps[4 + g % 2][:, 0:TT],
                                       ohbs[:, ei * 128:(ei + 1) * 128],
                                       pwtb[:, t * TT:(t + 1) * TT],
                                       start=True, stop=True),
                              "ct", (p, "pb", g))
                    if SPARSE:
                        M.wait(e, (p, "zeld", ei, t))
                    HBm = HBS if SPARSE else HB
                    for fc in range(NFV):
                        hg = g * NFV + fc
                        if SPARSE and hg == 0:
                            M.wait(e, (p, "posv2"))
                        if SPARSE and hg == 1:
                            M.wait(e, (p, "pwtev", NG - 1))
                        if SPARSE and hg == 2:
                            M.wait(e, (p, "zbev_v", 2 * PPE - 1))
                        if hg == 2 and not SPARSE:
                            M.wait(e, (p, "Lztile", NTT - 1))
                        if hg >= len(HBm):
                            M.wait(e, (p, "hev", hg - len(HBm)))
                        for cc in range(NCC):
                            if SPARSE:
                                mslice = sub_ap(zeS[t][:, :], cc * 128,
                                                [(NCC * 128, NSUB),
                                                 (1, 128)])
                            else:
                                mslice = zh[:, cc * T + t * TT:
                                            cc * T + (t + 1) * TT]
                            mm = e.matmul(
                                ps[HBm[hg % len(HBm)]][:, 0:TT],
                                w1b[par][:, cc * FV + fc * 128:
                                         cc * FV + (fc + 1) * 128],
                                mslice,
                                start=(cc == 0), stop=(cc == NCC - 1))
                        M.inc(mm, "ct", (p, "h", hg))
                    M.wait(e, (p, "hev", g * NFV + NFV - 1))
                    if i == 0 and t == 0:
                        M.wait(e, (p, "yi", NTT * NCC - 1))
                    YBm = YBS if SPARSE else YB
                    for dc in range(NCC):
                        yg = g * NCC + dc
                        if yg == 2 and not SPARSE:
                            M.wait(e, (p, "Lztile", NTT - 1))
                        if yg >= len(YBm):
                            M.wait(e, (p, "yev" if not SPARSE else "yadd",
                                       yg - len(YBm)))
                        for fc in range(NFV):
                            mm = e.matmul(
                                ps[YBm[yg % len(YBm)]][:, 0:TT],
                                w2b[par][:, fc * C + dc * 128:
                                         fc * C + (dc + 1) * 128],
                                hsb[:, fc * TT:(fc + 1) * TT],
                                start=(fc == 0), stop=(fc == NFV - 1))
                        key = (p, "ymm", yg)
                        if t == NTI - 1 and dc == NCC - 1:
                            key = (p, "ymm_last", i)
                        M.inc(mm, "ct", key)
                if SPARSE and qi == NVQ - 1 and t == NTI - 1:
                    for j in range(PPE):
                        tl = j // NSUB
                        M.wait(e, (p, "yadd",
                                   (i * NTI + tl) * NCC + NCC - 1))
                        jj = ei * PPE + j
                        for hh in range(2):
                            if jj >= 1:
                                M.wait(e, (p, "ysclh",
                                           *(divmod(jj - 1, PPE) + (hh,))))
                            elif hh == 0:
                                M.wait(e, (p, "qsrev"))
                                M.wait(e, (p, "pexp", NG - 1))
                                M.wait(e, (p, "pexp", NG - 2))
                            for dcl in range(NCC // 2):
                                dc = hh * (NCC // 2) + dcl
                                mm = e.transpose(
                                    ps[4 + hh][:, dcl * 128:
                                               (dcl + 1) * 128],
                                    yacc_f[:, dc * CAP + j * 128:
                                           dc * CAP + (j + 1) * 128],
                                    eyes[:])
                            M.inc(mm, "ct", (p, "ytp2h", ei, j, hh))
                if SPARSE and ei + 2 < E:
                    emit_ztr2(e, psl((ei + 2) * PPE, qi))
            if SPARSE:
                for c in range(NCHK):
                    M.wait(e, (p, "ybsum", c))
                    if c == 0:
                        M.wait(e, (p, "ysclh", E - 1, PPE - 1, 1))
                    for cc in range(NCC):
                        if cc >= 2:
                            M.wait(e, (p, "ybadd", c, cc - 2))
                        elif c >= 1:
                            M.wait(e, (p, "ybadd", c - 1, NCC - 1))
                        mm = e.transpose(
                            ps[4 + cc % 2][:, 0:128],
                            ybk[0][:, cc * 128:(cc + 1) * 128], eyes[:])
                        M.inc(mm, "ct", (p, "ybtp", c, cc))
            return

        if ename == "scalar":
            for t in range(NTT):
                ln_tile("scalar", e, M, p, "L", t, xsrc(t), zdst(t),
                        rows_ff, rbf_ff, xbb_ff, xqq_ff, not from_yx,
                        tmpv_f[:, 0:TT], **lng(t))
                for s in range(NSUB):
                    g = t * NSUB + s
                    M.wait(e, (p, "negmax", g))
                    M.inc(e.activation(out=ppex[:, 0:E],
                                       in_=ps[4 + g % 4][:, 0:E],
                                       func=AF.Exp, bias=ppm[:, 0:1],
                                       scale=1.0), "cs", (p, "pexp", g))
                    M.wait(e, (p, "pwt_ps", g))
                    M.inc(e.copy(out=pwtb[0:E, t * TT + s * 128:
                                          t * TT + (s + 1) * 128],
                                 in_=ps[6 + g % 2][0:E, 128:256]),
                          "cs", (p, "pwtev", g))
                    if SPARSE:
                        M.wait(e, (p, "cnt", g))
                        M.inc(e.copy(out=crow[0:1, g * E:(g + 1) * E],
                                     in_=ps[6 + g % 2][0:1, 384:384 + E]),
                              "cs", (p, "crow", g))
                if SPARSE:
                    for s in range(NSUB):
                        zi = t * NSUB + s
                        M.wait(e, (p, "ztp", zi))
                        if zi >= 2:
                            M.wait(e, (p, "ztst", zi - 2))
                        for cc in range(NCC):
                            op = e.copy(
                                out=ztma[zi % 2][:, cc * 128:(cc + 1) * 128],
                                in_=ps[2 + zi % 2].bitcast(BF16)[
                                    :, cc * 128:(cc + 1) * 128])
                            M.inc(op, "cs",
                                  (p, "ztmev", zi) if cc == NCC - 1 else None)
            if SPARSE:
                M.wait(e, (p, "ccolp"))
                M.inc(e.copy(out=ccb[:, 0:1], in_=ps[4][0:128, 0:1]),
                      "cs", (p, "ccbev"))
                M.wait(e, (p, "qtp"))
                M.inc(e.copy(out=qsr[0:1, 0:128], in_=ps[4][0:1, 0:128]),
                      "cs", (p, "qev2"))
                M.wait(e, (p, "stp"))
                M.inc(e.copy(out=qsr[0:1, 128:256], in_=ps[4][0:1, 0:128]),
                      "cs", (p, "qsrev"))
            for i in range(NVE):
                ei, qi = i // NVQ, i % NVQ
                if SPARSE and qi == 0:
                    for t in range(NST):
                        if ei >= 1:
                            hlast = (((ei - 1) * NVQ + NVQ - 1) * NTI
                                     + t) * NFV + NFV - 1
                            M.wait(e, (p, "h", hlast))
                        M.wait(e, (p, "zbst", ei, t, NSUB - 1))
                        M.wait(e, (p, "zbst", ei, t, NSUB - 2))
                        M.inc(e.dma_start(out=zeS[t][:, :],
                                          in_=zc_s[ei, t]),
                              f"dze{t % 2}", (p, "zeld", ei, t), 16)
                for t in range(NTI):
                    g = i * NTI + t
                    if not SPARSE:
                        M.wait(e, (p, "pb", g))
                        if g >= 2:
                            M.wait(e, (p, "yev", (g - 2) * NCC + NCC - 1))
                        M.inc(e.copy(
                            out=pbs_f[:, (g % 2) * TT:(g % 2 + 1) * TT],
                            in_=ps[4 + g % 2][:, 0:TT]),
                            "cs", (p, "pbev", g))
                    for fc in range(NFV):
                        hg = g * NFV + fc
                        M.wait(e, (p, "h", hg))
                        fcg = (ei % 2) * NCFE + qi * NFV + fc
                        HBm = HBS if SPARSE else HB
                        M.inc(e.activation(
                            out=hsb[:, fc * TT:(fc + 1) * TT],
                            in_=ps[HBm[hg % len(HBm)]][:, 0:TT],
                            func=AF.Gelu,
                            bias=b1s[:, fcg:fcg + 1], scale=1.0),
                            "cs", (p, "hev", hg))
            return

        if ename == "vector":
            def ch(op, key=None):
                c = M.inc(op, "cv", key)
                M.wait_cnt(e, "cv", c)
            for t in range(NTT):
                ln_tile("vector", e, M, p, "L", t, xsrc(t), zdst(t),
                        rows_ff, rbf_ff, xbb_ff, xqq_ff, not from_yx,
                        tmpv_f[:, 0:TT], **lng(t))
                for s in range(NSUB):
                    g = t * NSUB + s
                    M.wait(e, (p, "lg", g))
                    if g >= 1:
                        M.wait(e, (p, "pwt_ps", g - 1))
                    ch(e.tensor_reduce(out=ppm[:, 0:1],
                                       in_=ps[4 + g % 4][:, 0:E],
                                       axis=AX.X, op=ALU.max, negate=True),
                       (p, "negmax", g))
                    M.wait(e, (p, "pexp", g))
                    ch(e.tensor_reduce(out=ppm[:, 0:1], in_=ppex[:, 0:E],
                                       axis=AX.X, op=ALU.max))
                    ch(e.tensor_scalar(out=ppb[:, 0:E], in0=ppex[:, 0:E],
                                       scalar1=ppm[:, 0:1], scalar2=None,
                                       op0=ALU.is_ge))
                    if SPARSE:
                        ch(e.tensor_copy(out=eqf[:, g * E:(g + 1) * E],
                                         in_=ppb[:, 0:E]))
                    ch(e.scalar_tensor_tensor(out=ppb[:, 0:E],
                                              in0=ppb[:, 0:E], scalar=-2.0,
                                              in1=ppex[:, 0:E], op0=ALU.mult,
                                              op1=ALU.add))
                    ch(e.tensor_reduce(out=ppm[:, 0:1], in_=ppb[:, 0:E],
                                       axis=AX.X, op=ALU.max))
                    ch(e.tensor_scalar(out=ppb[:, 0:E], in0=ppex[:, 0:E],
                                       scalar1=ppm[:, 0:1], scalar2=None,
                                       op0=ALU.is_ge))
                    if SPARSE:
                        ch(e.tensor_copy(out=maskf[:, g * E:(g + 1) * E],
                                         in_=ppb[:, 0:E]))
                        ch(e.tensor_copy(out=maskb[:, g * E:(g + 1) * E],
                                         in_=ppb[:, 0:E]), (p, "maskw", g))
                    ch(e.tensor_tensor(out=ppa[:, 0:E], in0=ppex[:, 0:E],
                                       in1=ppb[:, 0:E], op=ALU.mult))
                    ch(e.tensor_reduce(out=ppr[:, 0:1], in_=ppa[:, 0:E],
                                       axis=AX.X, op=ALU.add))
                    ch(e.tensor_scalar(out=ppr[:, 0:1], in0=ppr[:, 0:1],
                                       scalar1=2.0, scalar2=2 * EPS_GATE,
                                       op0=ALU.mult, op1=ALU.add))
                    ch(e.reciprocal(out=ppr[:, 0:1], in_=ppr[:, 0:1]))
                    ch(e.tensor_scalar(out=pfull[:, 0:E], in0=ppa[:, 0:E],
                                       scalar1=ppr[:, 0:1], scalar2=None,
                                       op0=ALU.mult), (p, "pw", g))
                    if SPARSE:
                        ch(e.tensor_copy(out=pwm[:, g * E:(g + 1) * E],
                                         in_=pfull[:, 0:E]), (p, "pwmw", g))
            if SPARSE:
                M.wait(e, (p, "cbp"))
                ch(e.tensor_tensor(out=cbt[:, 0:1], in0=ps[4][:, 0:1],
                                   in1=segc[:, 0:1], op=ALU.add))
                st_i = idxt[:, 0:1]
                ch(e.tensor_copy(out=st_i, in_=cbt[:, 0:1]))
                ch(e.tensor_scalar(out=cbt[:, 2:3].bitcast(mybir.dt.int32),
                                   in0=st_i, scalar1=63, scalar2=None,
                                   op0=ALU.bitwise_and))
                ch(e.tensor_copy(out=cbt[:, 2:3],
                                 in_=cbt[:, 2:3].bitcast(mybir.dt.int32)))
                ch(e.tensor_scalar(out=cbt[:, 1:2].bitcast(mybir.dt.int32),
                                   in0=st_i, scalar1=6, scalar2=None,
                                   op0=ALU.arith_shift_right))
                ch(e.tensor_copy(out=cbt[:, 1:2],
                                 in_=cbt[:, 1:2].bitcast(mybir.dt.int32)),
                   (p, "qscols"))
                NE = NCHK * E
                M.wait(e, (p, "pospAll"))
                M.wait(e, (p, "pwmw", NG - 1))
                M.wait(e, "dmpc")
                ch(e.tensor_scalar(out=ppA[:, 0:NE],
                                   in0=maskf[:, 0:NE],
                                   scalar1=-1.0, scalar2=1.0,
                                   op0=ALU.mult, op1=ALU.add))
                ch(e.tensor_scalar(out=ppA[:, 0:NE], in0=ppA[:, 0:NE],
                                   scalar1=dmpc[:, 0:1], scalar2=None,
                                   op0=ALU.mult))
                ch(e.tensor_tensor(out=ppB[:, 0:NE], in0=ps[6][:, 0:NE],
                                   in1=maskf[:, 0:NE], op=ALU.mult),
                   (p, "posv2"))
                ch(e.tensor_tensor(out=ppA[:, 0:NE], in0=ppB[:, 0:NE],
                                   in1=ppA[:, 0:NE], op=ALU.add))

                def segsum(dst, srcsel):
                    assert E & (E - 1) == 0
                    w = E
                    while w > 1:
                        h2 = w // 2
                        va = sub_ap(ppB[:, 0:NE], 0, [(E, NCHK), (1, h2)])
                        vb = sub_ap(ppB[:, 0:NE], h2, [(E, NCHK), (1, h2)])
                        ch(e.tensor_tensor(out=va, in0=va, in1=vb,
                                           op=ALU.add))
                        w = h2
                    ch(e.tensor_copy(
                        out=dst, in_=sub_ap(ppB[:, 0:NE], 0,
                                            [(E, NCHK), (1, 1)])))
                ch(e.tensor_tensor(out=ppB[:, 0:NE], in0=ppA[:, 0:NE],
                                   in1=eqf[:, 0:NE], op=ALU.mult))
                segsum(posb[:, 0:NCHK], None)
                ch(e.tensor_tensor(out=ppB[:, 0:NE], in0=pwm[:, 0:NE],
                                   in1=eqf[:, 0:NE], op=ALU.mult))
                segsum(pw12[:, 0:NCHK], None)
                ch(e.tensor_tensor(out=ppB[:, 0:NE], in0=maskf[:, 0:NE],
                                   in1=eqf[:, 0:NE], op=ALU.subtract))
                ch(e.tensor_tensor(out=ppA[:, 0:NE], in0=ppA[:, 0:NE],
                                   in1=ppB[:, 0:NE], op=ALU.mult))
                ch(e.tensor_tensor(out=ppB[:, 0:NE], in0=pwm[:, 0:NE],
                                   in1=ppB[:, 0:NE], op=ALU.mult))
                segsum(pw12[:, NCHK:2 * NCHK], None)
                ch(e.tensor_copy(out=ppB[:, 0:NE], in_=ppA[:, 0:NE]))
                segsum(posb[:, NCHK:2 * NCHK], None)
                sctf = sctgA[:, :].bitcast(F32)
                for half in range(2):
                    ch(e.tensor_copy(
                        out=sub_ap(sctgA[:, :], 2 * NCHK * half,
                                   [(2, NCHK)]),
                        in_=tidh[:, 0:NCHK]))
                    ch(e.tensor_copy(
                        out=sub_ap(sctf, 2 * NCHK * half + 1, [(2, NCHK)]),
                        in_=pw12[:, half * NCHK:(half + 1) * NCHK]),
                       (p, "sct2", half))
            if from_yx and dbg is not None:
                M.wait(e, (p, "dbgst"))
            for t in range(NTT):
                if not from_yx:
                    M.wait(e, (p, "xld2", t))
                for cc in range(NCC):
                    g = t * NCC + cc
                    M.wait(e, (p, "bm", g))
                    dst = yx[:, cc * T + t * TT:cc * T + (t + 1) * TT]
                    src0 = dst if from_yx else \
                        xstr_f[:, cc * TT:(cc + 1) * TT]
                    ch(e.tensor_tensor(out=dst, in0=src0,
                                       in1=ps[g % 2][:, 0:TT], op=ALU.add),
                       (p, "yi", g))
            if SPARSE:
                def emit_zbev(kblock):
                    for k in kblock:
                        ei2, t2, sub2 = gidx(k)
                        gsl = k % 2
                        bank = 2 + (k % 2 if k < 2 * PPE else 0)
                        M.wait(e, (p, "ztr2", ei2, t2, sub2))
                        if k >= 2:
                            M.wait(e, (p, "zbst", *gidx(k - 2)))
                        for cc in range(NCC):
                            op = e.tensor_copy(
                                out=zgk[gsl][:, cc * 128:(cc + 1) * 128],
                                in_=ps[bank].bitcast(BF16)[
                                    :, cc * 128:(cc + 1) * 128])
                            M.inc(op, "cv",
                                  (p, "zbev_v", k) if cc == NCC - 1
                                  else None)
                        M.wait_cnt(e, "cv", M.total("cv"))
                PPE = NST * NSUB
                emit_zbev(range(0, 2 * PPE))
                for i in range(NVE):
                    ei, qi = i // NVQ, i % NVQ
                    for t in range(NST):
                        g = i * NST + t
                        for dc in range(NCC):
                            yg = g * NCC + dc
                            if t == NST - 1 and dc == NCC - 1:
                                M.wait(e, (p, "ymm_last", i))
                            else:
                                M.wait(e, (p, "ymm", yg))
                            dst = yacc_f[:, dc * CAP + t * TT:
                                         dc * CAP + (t + 1) * TT]
                            if qi == 0:
                                if ei >= 1 and dc == 0:
                                    M.wait(e, (p, "ytp2h", ei - 1,
                                               t * NSUB + NSUB - 1, 1))
                                op = e.tensor_copy(
                                    out=dst, in_=ps[YBS[yg % 2]][:, 0:TT])
                            else:
                                op = e.tensor_tensor(
                                    out=dst, in0=dst,
                                    in1=ps[YBS[yg % 2]][:, 0:TT],
                                    op=ALU.add)
                            ch(op, (p, "yadd", yg))
                    if qi == NVQ - 1:
                        for j in range(PPE):
                            jj = ei * PPE + j
                            M.wait(e, (p, "ytp2h", ei, j, 0))
                            if jj >= 4:
                                M.wait(e, (p, "yst", *divmod(jj - 4, PPE)))
                            M.wait(e, (p, "pwx", ei, j // NSUB, j % NSUB))
                            ch(e.tensor_scalar(
                                out=ytr[j % 4][:, 0:C // 2],
                                in0=ps[4][:, 0:C // 2],
                                scalar1=pwsb[:, jj:jj + 1],
                                scalar2=None, op0=ALU.mult),
                               (p, "ysclh", ei, j, 0))
                            M.wait(e, (p, "ytp2h", ei, j, 1))
                            ch(e.tensor_scalar(
                                out=ytr[j % 4][:, C // 2:C],
                                in0=ps[5][:, 0:C // 2],
                                scalar1=pwsb[:, jj:jj + 1],
                                scalar2=None, op0=ALU.mult),
                               (p, "ysclh", ei, j, 1))
                    if ei + 2 < E:
                        emit_zbev(psl((ei + 2) * PPE, qi))
                for c in range(NCHK):
                    M.wait(e, (p, "ybg", c))
                    ch(e.tensor_tensor(out=ybk[0][:, 0:C],
                                       in0=ybk[0][:, 0:C],
                                       in1=ybk[1][:, 0:C], op=ALU.add),
                       (p, "ybsum", c))
                    for cc in range(NCC):
                        M.wait(e, (p, "ybtp", c, cc))
                        dst = yx[:, cc * T + c * 128:cc * T + (c + 1) * 128]
                        key = (p, "ybadd", c, cc)
                        if c == NCHK - 1 and cc == NCC - 1:
                            key = (p, "ffn_done")
                        ch(e.tensor_tensor(
                            out=dst, in0=dst,
                            in1=ps[4 + cc % 2][:, 0:128],
                            op=ALU.add), key)
            else:
                for i in range(NVE):
                    for t in range(NTT):
                        g = i * NTT + t
                        M.wait(e, (p, "pbev", g))
                        for dc in range(NCC):
                            yg = g * NCC + dc
                            if t == NTT - 1 and dc == NCC - 1:
                                M.wait(e, (p, "ymm_last", i))
                            else:
                                M.wait(e, (p, "ymm", yg))
                            ch(e.tensor_tensor(
                                out=tmpv_f[:, 0:TT],
                                in0=ps[YB[yg % 3]][:, 0:TT],
                                in1=pbs_f[:, (g % 2) * TT:(g % 2 + 1) * TT],
                                op=ALU.mult))
                            dst = yx[:, dc * T + t * TT:dc * T + (t + 1) * TT]
                            key = (p, "yev", yg)
                            if i == NVE - 1 and t == NTT - 1 and dc == NCC - 1:
                                key = (p, "ffn_done")
                            ch(e.tensor_tensor(out=dst, in0=dst,
                                               in1=tmpv_f[:, 0:TT],
                                               op=ALU.add), key)
            return

    def attn_phase(ename, e, M, p, b):
        NGK = NTT * NSUB * NKVG
        NSG = NTT * NSUB

        def xsrc(t):
            def f(cc):
                return yx[:, cc * T + t * TT:cc * T + (t + 1) * TT]
            return f

        zdst = xsrc

        if ename == "sync":
            M.inc(e.dma_start(out=x1s[b], in_=yx[:]), "st", (p, "x1st"), 16)
            if NL >= 1:
                M.wait(e, (p, "projmm", NL - 1, NTT - 1, NCC - 1))
            else:
                M.wait(e, (p, "Aztile", NTT - 1))
            CH = min(T, (NCC * TT) // 2)
            for cc in range(NCC):
                for j in range(T // CH):
                    if cc + j > 0:
                        M.wait(e, (p, "xaadd", cc, j - 1) if j >= 1 else
                               (p, "xaadd", cc - 1, T // CH - 1))
                    M.inc(e.dma_start(
                        out=xa_f[:, 0:CH],
                        in_=x1s[b][:, cc * T + j * CH:cc * T + (j + 1) * CH]),
                        "dx", (p, "x1ld", cc, j), 16)
            return

        if ename == "gpsimd":
            for t in range(NTT):
                ln_tile("gpsimd", e, M, p, "A", t, xsrc(t), zdst(t),
                        rows_at, rbf_at, xbA[0], xbA[1], False,
                        tmpa_f[:, 0:TT], z_gate=(p, "x1st"))
            for L in range(NL):
                if L == 0:
                    M.wait(e, (p, "Aztile", NTT - 1))
                else:
                    M.wait(e, (p, "kvmm", L - 1, NGK - 1))
                M.inc(e.dma_start(out=wkv[:, :], in_=wkvh_d[L]),
                      "dwa", None, 16)
                M.inc(e.dma_start(
                    out=bkvs[0:1, (L % 2) * 2 * C:(L % 2 + 1) * 2 * C],
                    in_=bkvh[L]), "dwa", (p, "wkv", L), 16)
                if L >= 1:
                    M.wait(e, (p, "projmm", L - 1, NTT - 1, NCC - 1))
                M.inc(e.dma_start(out=wqv[:, :], in_=wqh_d[L]),
                      "dwb", None, 16)
                M.inc(e.dma_start(out=wpv[:, :], in_=wprojh[L]),
                      "dwb", None, 16)
                M.inc(e.dma_start(out=bqs[:, (L % 2) * NCC:(L % 2 + 1) * NCC],
                                  in_=bqqh[L]), "dwb", None, 16)
                M.inc(e.dma_start(out=bps[:, (L % 2) * NCC:(L % 2 + 1) * NCC],
                                  in_=bpjh[L]), "dwb", (p, "wqp", L), 16)
            return

        if ename == "tensor":
            for t in range(NTT):
                ln_tile("tensor", e, M, p, "A", t, xsrc(t), zdst(t),
                        rows_at, rbf_at, xbA[0], xbA[1], False,
                        tmpa_f[:, 0:TT])
            for L in range(NL):
                M.wait(e, (p, "wkv", L))
                def emit_S(sg):
                    slot = (sg % 2) * 2 * C
                    if sg == 0 and L == 0:
                        M.wait(e, (p, "Aztile", NTT - 1))
                    if sg >= 2:
                        M.wait(e, (p, "sacc", L, sg - 2))
                    elif sg == 1:
                        if L >= 1:
                            M.wait(e, (p, "yTev", L - 1, NTT - 1, NCC - 1))
                    elif L >= 1:
                        M.wait(e, (p, "sacc", L - 1, NSG - 2))
                    sb0 = 2 + 4 * (sg % 2)
                    for h in range(H):
                        mm = e.matmul(
                            ps[sb0 + h // SP][0:64,
                                              (h % SP) * 64:
                                              (h % SP) * 64 + 64],
                            kvt[:, slot + h * DH:slot + (h + 1) * DH],
                            kvt[:, slot + C + h * DH:
                                slot + C + (h + 1) * DH],
                            start=True, stop=True)
                    M.inc(mm, "ct", (p, "smm", L, sg))
                for t in range(NTT):
                    M.wait(e, (p, "xbc1", L, t))
                    for sub in range(NSUB):
                        sg = t * NSUB + sub
                        for kg in range(NKVG):
                            gk = sg * NKVG + kg
                            if L == 0 and gk < 2:
                                M.wait(e, (p, "Astatsread", NTT - 1))
                            if L >= 1 and gk in (2, 3):
                                M.wait(e, (p, "projev", L - 1, NTT - 1,
                                           NCC - 1))
                            if gk >= 4:
                                M.wait(e, (p, "kvev", L, gk - 4))
                            for cc in range(NCC):
                                e.matmul(
                                    ps[KB[gk % 4]][:, 0:512],
                                    xbA[t % 2][:, cc * TT + sub * 128:
                                               cc * TT + (sub + 1) * 128],
                                    wkv[:, cc * 2 * C + kg * 512:
                                        cc * 2 * C + (kg + 1) * 512],
                                    start=(cc == 0), stop=False)
                            M.inc(e.matmul(
                                ps[KB[gk % 4]][:, 0:512], oner[:],
                                bkvs[0:1, (L % 2) * 2 * C + kg * 512:
                                     (L % 2) * 2 * C + (kg + 1) * 512],
                                start=False, stop=True),
                                "ct", (p, "kvmm", L, gk))
                        if sg >= 1:
                            M.wait(e, (p, "kvev", L,
                                       (sg - 1) * NKVG + NKVG - 1))
                            emit_S(sg - 1)
                        if sg == NSUB * NTT - 1:
                            M.wait(e, (p, "kvev", L, sg * NKVG + NKVG - 1))
                            emit_S(sg)
                M.wait(e, (p, "wqp", L))
                M.wait(e, (p, "sbf2", L))
                def qwait(t, fc):
                    if fc >= 3:
                        M.wait(e, (p, "qev", L, t, fc - 3))
                    elif t >= 1:
                        M.wait(e, (p, "projev", L, t - 1, NCC - 1))
                    elif L >= 1:
                        M.wait(e, (p, "projev", L - 1, NTT - 1, NCC - 1))
                for t in range(NTT):
                    M.wait(e, (p, "xbc2", L, t))
                    if t == 0:
                        M.wait(e, (p, "kvev", L, NGK - 1))
                    for fc in range(NCC):
                        qwait(t, fc)
                        for cc in range(NCC):
                            mm = e.matmul(
                                ps[QB[fc % 3]][:, 0:TT],
                                wqv[:, cc * C + fc * 128:
                                    cc * C + (fc + 1) * 128],
                                xbA[t % 2][:, cc * TT:(cc + 1) * TT],
                                start=(cc == 0), stop=(cc == NCC - 1))
                        M.inc(mm, "ct", (p, "qmm", L, t, fc))
                    for fc in range(NCC):
                        M.wait(e, (p, "qev", L, t, fc))
                        if fc >= 3:
                            M.wait(e, (p, "yTev", L, t, fc - 3))
                        elif t >= 1:
                            M.wait(e, (p, "yTev", L, t - 1, NCC - 1))
                        elif L >= 1:
                            M.wait(e, (p, "yTev", L - 1, NTT - 1, NCC - 1))
                        M.inc(e.matmul(ps[TB[fc % 3]][:, 0:TT],
                                       sbf2[:, fc * 128:(fc + 1) * 128],
                                       qTt[:, fc * TT:(fc + 1) * TT],
                                       start=True, stop=True),
                              "ct", (p, "yTmm", L, t, fc))
                    M.wait(e, (p, "yTev", L, t, NCC - 1))
                    for cc in range(NCC):
                        if cc >= 3:
                            M.wait(e, (p, "projev", L, t, cc - 3))
                        for fc in range(NCC):
                            mm = e.matmul(
                                ps[PB[cc % 3]][:, 0:TT],
                                wpv[:, fc * C + cc * 128:
                                    fc * C + (cc + 1) * 128],
                                yTt[:, fc * TT:(fc + 1) * TT],
                                start=(fc == 0), stop=(fc == NCC - 1))
                        M.inc(mm, "ct", (p, "projmm", L, t, cc))
            return

        if ename == "scalar":
            for t in range(NTT):
                ln_tile("scalar", e, M, p, "A", t, xsrc(t), zdst(t),
                        rows_at, rbf_at, xbA[0], xbA[1], False,
                        tmpa_f[:, 0:TT])
            for L in range(NL):
                for t in range(NTT):
                    for sub in range(NSUB):
                        for kg in range(NKVG):
                            gk = (t * NSUB + sub) * NKVG + kg
                            sg = t * NSUB + sub
                            M.wait(e, (p, "kvmm", L, gk))
                            slot = (sg % 2) * 2 * C
                            M.inc(e.copy(
                                out=kvt[:, slot + kg * 512:
                                        slot + (kg + 1) * 512],
                                in_=ps[KB[gk % 4]][:, 0:512]),
                                "cs", (p, "kvev", L, gk))
                M.wait(e, (p, "sacc", L, NSG - 1))
                for h in range(H):
                    off = (h % 2) * 64
                    fc = h // 2
                    M.inc(e.activation(
                        out=sbf2[off:off + 64,
                                 fc * 128 + off:fc * 128 + off + 64],
                        in_=ssb[0:64, h * DH:(h + 1) * DH],
                        func=AF.Identity, bias=0.0, scale=0.125),
                        "cs", (p, "sbf2", L) if h == H - 1 else None)
                for t in range(NTT):
                    if t < 2:
                        M.wait(e, (p, "smm", L, NSG - 1))
                    else:
                        M.wait(e, (p, "qmm", L, t - 2, NCC - 1))
                    for cc in range(NCC):
                        M.inc(e.copy(out=xbA[t % 2][:, cc * TT:(cc + 1) * TT],
                                     in_=xsrc(t)(cc)), "cs",
                              (p, "xbc2", L, t) if cc == NCC - 1 else None)
                    M.wait_cnt(e, "cs", M.total("cs"))
                    for fc in range(NCC):
                        M.wait(e, (p, "qmm", L, t, fc))
                        M.inc(e.activation(
                            out=qTt[:, fc * TT:(fc + 1) * TT],
                            in_=ps[QB[fc % 3]][:, 0:TT], func=AF.Identity,
                            bias=bqs[:, (L % 2) * NCC + fc:
                                     (L % 2) * NCC + fc + 1], scale=1.0),
                            "cs", (p, "qev", L, t, fc))
                    for fc in range(NCC):
                        M.wait(e, (p, "yTmm", L, t, fc))
                        M.inc(e.copy(out=yTt[:, fc * TT:(fc + 1) * TT],
                                     in_=ps[TB[fc % 3]][:, 0:TT]),
                              "cs", (p, "yTev", L, t, fc))
                    for cc in range(NCC):
                        M.wait(e, (p, "projmm", L, t, cc))
                        if cc >= 2:
                            M.wait(e, (p, "xadd", L, t, cc - 2))
                        elif t >= 1:
                            M.wait(e, (p, "xadd", L, t - 1, NCC - 2 + cc))
                        elif L >= 1:
                            M.wait(e, (p, "xadd", L - 1, NTT - 1,
                                       NCC - 2 + cc))
                        M.inc(e.activation(
                            out=tmpa_f[:, (cc % 2) * TT:(cc % 2 + 1) * TT],
                            in_=ps[PB[cc % 3]][:, 0:TT], func=AF.Identity,
                            bias=bps[:, (L % 2) * NCC + cc:
                                     (L % 2) * NCC + cc + 1], scale=1.0),
                            "cs", (p, "projev", L, t, cc))
            return

        if ename == "vector":
            def ch(op, key=None):
                c = M.inc(op, "cv", key)
                M.wait_cnt(e, "cv", c)
            for t in range(NTT):
                ln_tile("vector", e, M, p, "A", t, xsrc(t), zdst(t),
                        rows_at, rbf_at, xbA[0], xbA[1], False,
                        tmpa_f[:, 0:TT], z_gate=(p, "x1st"))
            for L in range(NL):
                for t in range(NTT):
                    if t >= 2:
                        M.wait(e, (p, "kvmm", L,
                                   ((t - 2) * NSUB + NSUB - 1) * NKVG +
                                   NKVG - 1))
                    elif L == 0:
                        M.wait(e, (p, "As1", NTT - 1))
                        M.wait(e, (p, "As2", NTT - 1))
                    else:
                        M.wait(e, (p, "projmm", L - 1, NTT - 1, NCC - 1))
                    for cc in range(NCC):
                        M.inc(e.tensor_copy(
                            out=xbA[t % 2][:, cc * TT:(cc + 1) * TT],
                            in_=xsrc(t)(cc)), "cv",
                            (p, "xbc1", L, t) if cc == NCC - 1 else None)
                    M.wait_cnt(e, "cv", M.total("cv"))
                    for sub in range(NSUB):
                        sg = t * NSUB + sub
                        for s_ in ([sg - 1] if sg >= 1 else []) + \
                                ([sg] if sg == NSUB * NTT - 1 else []):
                            M.wait(e, (p, "smm", L, s_))
                            first = (s_ == 0)
                            if first and L >= 1:
                                M.wait(e, (p, "sbf2", L - 1))
                            sb0 = 2 + 4 * (s_ % 2)
                            if first:
                                ch(e.tensor_copy(
                                    out=ssb[:, 0:SP * 64],
                                    in_=ps[sb0][0:64, 0:SP * 64]))
                            else:
                                ch(e.tensor_tensor(
                                    out=ssb[:, 0:SP * 64],
                                    in0=ssb[:, 0:SP * 64],
                                    in1=ps[sb0][0:64, 0:SP * 64],
                                    op=ALU.add))
                            if H > SP:
                                op2 = (e.tensor_copy(
                                    out=ssb[:, SP * 64:H * 64],
                                    in_=ps[sb0 + 1][0:64, 0:(H - SP) * 64])
                                    if first else
                                    e.tensor_tensor(
                                        out=ssb[:, SP * 64:H * 64],
                                        in0=ssb[:, SP * 64:H * 64],
                                        in1=ps[sb0 + 1][0:64,
                                                        0:(H - SP) * 64],
                                        op=ALU.add))
                                ch(op2, (p, "sacc", L, s_))
                            else:
                                M.inc(e.nop(), "cv", (p, "sacc", L, s_))
                for t in range(NTT):
                    for cc in range(NCC):
                        M.wait(e, (p, "projev", L, t, cc))
                        dst = yx[:, cc * T + t * TT:cc * T + (t + 1) * TT]
                        ch(e.tensor_tensor(
                            out=dst, in0=dst,
                            in1=tmpa_f[:, (cc % 2) * TT:(cc % 2 + 1) * TT],
                            op=ALU.add), (p, "xadd", L, t, cc))
            CH = min(T, (NCC * TT) // 2)
            for cc in range(NCC):
                for j in range(T // CH):
                    M.wait(e, (p, "x1ld", cc, j))
                    dst = yx[:, cc * T + j * CH:cc * T + (j + 1) * CH]
                    ch(e.tensor_tensor(out=dst, in0=dst, in1=xa_f[:, 0:CH],
                                       op=ALU.add), (p, "xaadd", cc, j))
            return

    def setup_phase(ename, e, M, p):
        if ename == "sync":
            M.inc(e.dma_start(out=eyes[:], in_=eyeh[:]), "dx", "eyes", 16)
        elif ename == "gpsimd":
            M.inc(e.dma_start(out=eyeb[:], in_=eyeh[:]), "dwc", None, 16)
            M.inc(e.dma_start(out=wgs[:], in_=wgh[:]), "dwc", None, 16)
            M.inc(e.dma_start(out=b2s[:], in_=b2h[:]), "dwc", None, 16)
            M.inc(e.dma_start(out=ohbs[:], in_=ohbh[:]), "dwc", "const_w", 16)
            M.inc(e.memset(onec[:], 1.0), "cg", None)
            M.inc(e.memset(oner[:], 1.0), "cg", None)
            M.inc(e.memset(sbf2[:], 0.0), "cg", "memsets")
            if SPARSE:
                M.inc(e.dma_start(out=ltc[:], in_=ltc_h[:]), "dwc", None, 16)
                M.inc(e.dma_start(out=selm[:], in_=selm_h[:]), "dwc", None,
                      16)
                M.inc(e.dma_start(out=segc[:], in_=segc_h[:]), "dwc", None,
                      16)
                M.inc(e.dma_start(out=tidh[:], in_=tidh_h[:]), "dwc", None,
                      16)
                M.inc(e.dma_start(out=pfh[:], in_=pfh_h[:]), "dwc",
                      "sparse_consts", 16)
                M.inc(e.memset(c64r[:], 64.0), "cg", None)
                M.inc(e.memset(crow[:], 0.0), "cg", None)
                M.inc(e.memset(zrow0[:], 0.0), "cg", "memsets2")
                M.wait(e, "memsets2")
                M.inc(e.dma_start(out=zt_s[T:T + 1, :], in_=zrow0[:]),
                      "dws", "ztzero", 16)
        elif ename == "vector" and SPARSE:
            M.wait(e, "sparse_consts")
            c_ = M.inc(e.tensor_copy(out=dmpc[:, 0:1], in_=tidh[:, 0:1]),
                       "cv")
            M.wait_cnt(e, "cv", c_)
            M.inc(e.tensor_scalar(out=dmpc[:, 0:1], in0=dmpc[:, 0:1],
                                  scalar1=float(NI), scalar2=None,
                                  op0=ALU.add), "cv", "dmpc")


    def make_emitter(ename):
        def f(e, M):
            for pid, ph in enumerate(phases):
                if pid >= 1:
                    for o_ in ENGS:
                        if o_ != ename:
                            M.wait(e, ("pd", pid - 1, o_))
                if ph[0] == "setup":
                    setup_phase(ename, e, M, pid)
                elif ph[0] == "ffn":
                    ffn_phase(ename, e, M, pid, ph[1], ph[2])
                else:
                    attn_phase(ename, e, M, pid, ph[1])
                if ename == "sync":
                    M.wait_cnt(e, "dx", M.total("dx"))
                    M.wait_cnt(e, "st", M.total("st"))
                    M.wait_cnt(e, "sz0", M.total("sz0"))
                    M.wait_cnt(e, "sz1", M.total("sz1"))
                elif ename == "gpsimd":
                    for s_ in ("dw0", "dw1", "dwa", "dwb", "dwc", "dws",
                               "dwy", "ds0", "ds1", "dg0", "dg1",
                               "dy0", "dy1", "dy2", "dy3"):
                        M.wait_cnt(e, s_, M.total(s_))
                else:
                    if ename == "scalar":
                        M.wait_cnt(e, "dze0", M.total("dze0"))
                        M.wait_cnt(e, "dze1", M.total("dze1"))
                    M.wait_cnt(e, OWN[ename], M.total(OWN[ename]))
                M.inc(e.nop(), OWN[ename], ("pd", pid, ename))
            if ename == "sync":
                M.wait_cnt(e, "st", M.total("st"))
        return f

    emitters = [(n, make_emitter(n)) for n in ENGS]
    M = Plan()
    M.real = False
    M.counts = {}
    fake = FakeEng()
    for n, f in emitters:
        f(fake, M)
    M.real = True
    M.counts = {}
    for s in SEMS:
        M.sems[s] = es.enter_context(nc.semaphore(s))
    block = es.enter_context(nc.Block())
    for n, f in emitters:
        getattr(block, n)(lambda eng, f=f: f(eng, M))
    return nc


def _host_prep(cfg, L0_attn_inputs_embeds, L0_attn_Wqkv, L0_attn_bqkv,
               L0_attn_Wproj, L0_attn_bproj, L0_ffn_Wg, L0_ffn_W1, L0_ffn_b1,
               L0_ffn_W2, L0_ffn_b2):
    B, T, C, NL, E = cfg["B"], cfg["T"], cfg["C"], cfg["NL"], cfg["E"]
    FE, TT, FV = cfg["FE"], cfg["TT"], cfg["FV"]
    NCC, NTT, NCFE, NVQ, NFV = (cfg["NCC"], cfg["NTT"], cfg["NCFE"],
                                cfg["NVQ"], cfg["NFV"])
    f32 = np.float32
    ac = np.ascontiguousarray

    x = np.asarray(L0_attn_inputs_embeds, dtype=f32)
    Bfull = x.shape[0]
    xt = x.transpose(0, 2, 1)
    x_in = ac(xt.reshape(Bfull, NCC, 128, NTT, TT)
              .transpose(0, 3, 2, 1, 4)
              .reshape(Bfull, NTT, 128, NCC * TT))

    Wqkv = np.asarray(L0_attn_Wqkv, dtype=f32)
    WqkvT = Wqkv.transpose(0, 2, 1)
    wqh = ac(WqkvT[:, :, :C].reshape(NL, NCC, 128, C)
             .transpose(0, 2, 1, 3).reshape(NL, 128, NCC * C))
    wkvh = ac(WqkvT[:, :, C:].reshape(NL, NCC, 128, 2 * C)
              .transpose(0, 2, 1, 3).reshape(NL, 128, NCC * 2 * C))
    bqkv = np.asarray(L0_attn_bqkv, dtype=f32)
    bqqh = ac(bqkv[:, :C].reshape(NL, NCC, 128).transpose(0, 2, 1))
    bkvh = ac(bqkv[:, C:].reshape(NL, 1, 2 * C))
    Wproj = np.asarray(L0_attn_Wproj, dtype=f32)
    wprojh = ac(Wproj.transpose(0, 2, 1).reshape(NL, NCC, 128, C)
                .transpose(0, 2, 1, 3).reshape(NL, 128, NCC * C))
    bpjh = ac(np.asarray(L0_attn_bproj, dtype=f32)
              .reshape(NL, NCC, 128).transpose(0, 2, 1))

    Wg = np.asarray(L0_ffn_Wg, dtype=f32)
    wgh = ac(Wg.reshape(NCC, 128, E).transpose(1, 0, 2).reshape(128, NCC * E))
    W1 = np.asarray(L0_ffn_W1, dtype=f32)
    w1h = ac(W1.reshape(E, NCC, 128, NVQ, FV)
             .transpose(0, 3, 2, 1, 4).reshape(E, NVQ, 128, NCC * FV))
    b1h = ac(np.asarray(L0_ffn_b1, dtype=f32)
             .reshape(E, NCFE, 128).transpose(0, 2, 1))
    W2 = np.asarray(L0_ffn_W2, dtype=f32)
    w2h = ac(W2.reshape(E, NVQ, NFV, 128, C)
             .transpose(0, 1, 3, 2, 4).reshape(E, NVQ, 128, NFV * C))
    b2h = ac(np.asarray(L0_ffn_b2, dtype=f32))
    eyeh = np.eye(128, dtype=f32)
    ohbh = ac(np.kron(np.eye(E, dtype=f32), np.ones((1, 128), f32))
              .reshape(E, E * 128))
    sparse_inputs = {}
    if cfg.get("sparse"):
        CAP, NI, NCHK = cfg["CAP"], cfg["NI"], cfg["NCHK"]
        ltc = np.tril(np.ones((128, 128), f32), -1)
        ltc = ac(ltc.T * 0 + np.triu(np.ones((128, 128), f32), 1))
        selm = np.zeros((128, 128), f32)
        for pp in range(NCHK * E):
            for p2 in range(NCHK * E):
                if pp % E == p2 % E and pp // E < p2 // E:
                    selm[pp, p2] = 1.0
        segc = np.zeros((128, 1), f32)
        segc[:NCHK * E, 0] = (np.arange(NCHK * E) % E) * CAP
        tidh = ac((np.arange(NCHK)[None, :] * 128 +
                   np.arange(128)[:, None]).astype(np.int32))
        pf = np.zeros((NI, 2), np.int32)
        pf[:, 0] = cfg["T"]
        pf[:, 1] = 0
        pfh = ac(pf.reshape(128, NI * 2 // 128))
        sparse_inputs = dict(ltc_h=ltc, selm_h=selm, segc_h=segc,
                             tidh_h=tidh, pfh_h=pfh)

    shared = dict(wqh=wqh, wkvh=wkvh, bqqh=bqqh, bkvh=bkvh, wprojh=wprojh,
                  bpjh=bpjh, wgh=wgh, w1h=w1h, b1h=b1h, w2h=w2h, b2h=b2h,
                  eyeh=eyeh, ohbh=ohbh, **sparse_inputs)
    in_maps = []
    for c in range(cfg["n_cores"]):
        m = dict(shared)
        m["x_in"] = ac(x_in[c * B:(c + 1) * B])
        in_maps.append(m)
    return in_maps


def _host_post(cfg, results):
    B, T, C, NCC = cfg["B"], cfg["T"], cfg["C"], cfg["NCC"]
    outs = []
    for r in results:
        o = r["out"]
        o = o.reshape(B, 128, NCC, T).transpose(0, 2, 1, 3) \
             .reshape(B, C, T).transpose(0, 2, 1)
        outs.append(o)
    return np.ascontiguousarray(np.concatenate(outs, axis=0)).astype(
        np.float32)


def kernel(L0_attn_inputs_embeds, L0_attn_Wqkv, L0_attn_bqkv, L0_attn_Wproj,
           L0_attn_bproj, L0_ffn_Wg, L0_ffn_W1, L0_ffn_b1, L0_ffn_W2,
           L0_ffn_b2):
    cfg = make_cfg()
    in_maps = _host_prep(cfg, L0_attn_inputs_embeds, L0_attn_Wqkv,
                         L0_attn_bqkv, L0_attn_Wproj, L0_attn_bproj,
                         L0_ffn_Wg, L0_ffn_W1, L0_ffn_b1, L0_ffn_W2,
                         L0_ffn_b2)
    with contextlib.ExitStack() as es:
        nc = build_nc(cfg, es)
    res = run_bass_kernel_spmd(nc, in_maps, core_ids=list(range(
        cfg["n_cores"])))
    return _host_post(cfg, res.results)
